# version 12
# baseline (speedup 1.0000x reference)
import numpy as np
from contextlib import ExitStack

import concourse.bass as bass
import concourse.mybir as mybir
from concourse.bass_utils import run_bass_kernel_spmd

F32 = mybir.dt.float32
BF16 = mybir.dt.bfloat16
F16 = mybir.dt.float16
AF = mybir.ActivationFunctionType
ALU = mybir.AluOpType
AXX = mybir.AxisListType.X

B, S, E, H, KVH, D = 2, 4096, 4096, 32, 8, 128
GRP = H // KVH
NSLOT = 8
EXT = [512 * (s + 1) for s in range(NSLOT)]
OFF = [0]
for _e in EXT:
    OFF.append(OFF[-1] + _e)
TOT = OFF[-1]
NUM_REMAIN = 820
SQD = float(np.sqrt(D))
STEPS = [128.0, 64.0, 32.0, 16.0, 8.0, 4.0, 2.0]
SPL = [0, 528, 800, 1072, 1328, 1600, 1856, 2128]


class Res:
    __slots__ = ("writers", "readers")

    def __init__(self):
        self.writers = []
        self.readers = []


class Prog:
    ENGS = ("pe", "dve", "act", "pool", "syn")
    PARANOID = False

    def __init__(self, nc, ctx):
        self.nc = nc
        self.sem = {e: ctx.enter_context(nc.semaphore("sem_" + e)) for e in self.ENGS}
        self.cnt = {e: 0 for e in self.ENGS}
        self.sched = {e: [] for e in self.ENGS}
        self.waited = {e: {} for e in self.ENGS}
        self.barrier = []
        self.pad_sb = ctx.enter_context(nc.sbuf_tensor("pe_pad", [128, 2048], BF16))

    def op(self, eng, fn, reads=(), writes=(), writes_acc=(), ndma=0):
        deps = {}

        def add(tok):
            if tok is None:
                return
            e2, v2 = tok
            if e2 == eng and eng == "pe":
                return
            if deps.get(e2, 0) < v2:
                deps[e2] = v2

        for t in self.barrier:
            add(t)
        if Prog.PARANOID:
            for e2 in self.ENGS:
                if self.cnt[e2] > 0 and e2 != eng:
                    add((e2, self.cnt[e2]))
        for r in reads:
            for t in r.writers:
                add(t)
        for w in writes:
            for t in w.writers:
                add(t)
            for t in w.readers:
                add(t)
        for w in writes_acc:
            for t in w.readers:
                add(t)
        waits = []
        wt = self.waited[eng]
        for e2 in self.ENGS:
            v2 = deps.get(e2)
            if v2 is None:
                continue
            if wt.get(e2, -1) < v2:
                waits.append((e2, v2))
                wt[e2] = v2
        if eng in ("dve", "act") and any(e2 == "pe" for e2, _ in waits):
            pad = self.pad_sb
            inner = fn

            def fn(q, _inner=inner, _pad=pad, _eng=eng):
                if _eng == "dve":
                    p = q.tensor_copy(_pad[:, 0:512], _pad[:, 512:1024])
                else:
                    p = q.copy(out=_pad[:, 1024:1536], in_=_pad[:, 1536:2048])
                ins = _inner(q)
                if not isinstance(ins, (list, tuple)):
                    ins = [ins]
                return [p] + list(ins)
        inc = 16 * ndma if ndma else 1
        self.cnt[eng] += inc
        self.sched[eng].append((waits, fn, bool(ndma)))
        tok = (eng, self.cnt[eng])
        for r in reads:
            r.readers.append(tok)
        for w in writes:
            w.writers = [tok]
            w.readers = []
        for w in writes_acc:
            w.writers.append(tok)
        return tok

    def phase_barrier(self):
        self.barrier = [(e, self.cnt[e]) for e in self.ENGS if self.cnt[e] > 0]

    def emit(self, block):
        sections = {"pe": block.tensor, "dve": block.vector, "act": block.scalar,
                    "pool": block.gpsimd, "syn": block.sync}

        def mk(eng):
            sched = self.sched[eng]
            sems = self.sem

            def emit_stream(q):
                for waits, fn, is_dma in sched:
                    for e2, v2 in waits:
                        q.wait_ge(sems[e2], v2)
                    instrs = fn(q)
                    if not isinstance(instrs, (list, tuple)):
                        instrs = [instrs]
                    if is_dma:
                        for ins in instrs:
                            ins.then_inc(sems[eng], 16)
                    else:
                        instrs[-1].then_inc(sems[eng], 1)
            return emit_stream

        for eng, dec in sections.items():
            dec(mk(eng))


def build_program(n_heads=H, debug=False):
    nc = bass.Bass("TRN2", target_bir_lowering=False)

    def din(name, shape, dt=F32):
        return nc.dram_tensor(name, list(shape), dt, kind="ExternalInput").ap()

    def dscr(name, shape, dt):
        return nc.dram_tensor(name, list(shape), dt, kind="Internal").ap()

    xfull = din("xfull", (S, E))
    xown = din("xown", (1024, E))
    wqt = din("wqt", (E, E))
    wkt = din("wkt", (E, KVH * D))
    wvt = din("wvt", (E, KVH * D))
    wot = din("wot", (E, E))
    fker = din("fker", (E, E))
    fbias = din("fbias", (1, E))
    hp1 = din("hp1", (H, D, D))
    hb1 = din("hb1", (H, D, 1))
    hp2 = din("hp2", (H, D, D))
    hb2 = din("hb2", (H, D, 1))
    lat = din("lat", (16, 128, 1024))
    last = din("last", (128, 1024))
    lbt = din("lbt", (16, 32, E))
    lbst = din("lbst", (32, E))
    gateown = din("gateown", (1, 64))
    coskf = din("coskf", (D, S))
    sinkf = din("sinkf", (D, S))
    cosqf = din("cosqf", (D, 1024))
    sinqf = din("sinqf", (D, 1024))
    identb = din("identb", (128, 128), BF16)
    identf = din("identf", (128, 128))
    maskc = din("maskc", (128, 512), BF16)
    cc1640 = din("cc1640", (128, NSLOT))
    cr820 = din("cr820", (128, NSLOT))

    out = nc.dram_tensor("out", [1024, E], F32, kind="ExternalOutput").ap()
    if debug:
        dbg = {
            "d_xn": nc.dram_tensor("d_xn", [32, 128, S], BF16, kind="ExternalOutput").ap(),
            "d_kr": nc.dram_tensor("d_kr", [KVH, 128, S], BF16, kind="ExternalOutput").ap(),
            "d_v": nc.dram_tensor("d_v", [KVH, 128, 32, 128], BF16, kind="ExternalOutput").ap(),
            "d_qr": nc.dram_tensor("d_qr", [32, 128, 1024], BF16, kind="ExternalOutput").ap(),
            "d_qn2": nc.dram_tensor("d_qn2", [32, 1024], F32, kind="ExternalOutput").ap(),
            "d_ctx": nc.dram_tensor("d_ctx", [H, 128, 1024], BF16, kind="ExternalOutput").ap(),
            "d_negb": nc.dram_tensor("d_negb", [128, NSLOT * H], F32, kind="ExternalOutput").ap(),
            "d_sim": nc.dram_tensor("d_sim", [128, TOT], BF16, kind="ExternalOutput").ap(),
            "d_book": nc.dram_tensor("d_book", [128, 2 * NSLOT], F32, kind="ExternalOutput").ap(),
            "d_sel": nc.dram_tensor("d_sel", [128, 4096], BF16, kind="ExternalOutput").ap(),
            "d_p": nc.dram_tensor("d_p", [128, 4096], BF16, kind="ExternalOutput").ap(),
            "d_xo": nc.dram_tensor("d_xo", [32, 128, 1024], BF16, kind="ExternalOutput").ap(),
            "d_cosk": nc.dram_tensor("d_cosk", [128, S], F32, kind="ExternalOutput").ap(),
            "d_sink": nc.dram_tensor("d_sink", [128, S], F32, kind="ExternalOutput").ap(),
            "d_cosq": nc.dram_tensor("d_cosq", [128, 1024], F32, kind="ExternalOutput").ap(),
            "d_qn2r": nc.dram_tensor("d_qn2r", [1, 1024], F32, kind="ExternalOutput").ap(),
            "d_acc": nc.dram_tensor("d_acc", [128, 2 * NSLOT], F32, kind="ExternalOutput").ap(),
            "d_acomb": nc.dram_tensor("d_acomb", [4, 128, 1024], BF16, kind="ExternalOutput").ap(),
            "d_bcomb": nc.dram_tensor("d_bcomb", [4, 64, E], BF16, kind="ExternalOutput").ap(),
            "d_axt": nc.dram_tensor("d_axt", [64, 1024], BF16, kind="ExternalOutput").ap(),
            "d_lastb": nc.dram_tensor("d_lastb", [128, 1024], BF16, kind="ExternalOutput").ap(),
            "d_gbc": nc.dram_tensor("d_gbc", [128, 64], F32, kind="ExternalOutput").ap(),
        }

    xnsp = dscr("xnsp", (32, 128, S), BF16)
    xosp = dscr("xosp", (32, 128, 1024), BF16)
    krsp = dscr("krsp", (KVH, 128, S), BF16)
    vsp = dscr("vsp", (KVH, 128, 32, 128), BF16)
    qrsp = dscr("qrsp", (32, 128, 1024), BF16)
    ctxsp = dscr("ctxsp", (H, 128, 1024), BF16)
    qn2sp = dscr("qn2sp", (32, 1024), F32)

    ctx = ExitStack()
    with ctx:
        P = Prog(nc, ctx)

        def sb(name, shape, dt):
            t = ctx.enter_context(nc.sbuf_tensor(name, list(shape), dt))
            return t, Res()

        zero_c, r_zero = sb("zero_c", (128, 1), F32)
        onescol, r_onescol = sb("onescol", (128, 1), BF16)
        ones1b, r_ones1b = sb("ones1b", (1, 128), BF16)
        ones1F, r_ones1F = sb("ones1F", (1, 128), F32)
        idb_sb, r_idb = sb("idb_sb", (128, 128), BF16)
        idf_sb, r_idf = sb("idf_sb", (128, 128), F32)
        maskc_sb, r_maskc = sb("maskc_sb", (128, 512), BF16)
        cc_sb, r_cc = sb("cc_sb", (128, NSLOT), F32)
        cr_sb, r_cr = sb("cr_sb", (128, NSLOT), F32)
        kmax_acc, r_kmax = sb("kmax_acc", (1, 1), F32)
        negkb, r_negkb = sb("negkb", (128, 1), F32)
        negb_sb, r_negb = sb("negb_sb", (128, NSLOT * H), F32)

        P.op("pool", lambda q: [q.memset(zero_c[:], 0.0),
                                q.memset(onescol[:], 1.0),
                                q.memset(ones1b[:], 1.0),
                                q.memset(ones1F[:], 1.0),
                                q.memset(kmax_acc[:], 0.0)],
             writes=[r_zero, r_onescol, r_ones1b, r_ones1F, r_kmax])
        P.op("syn", lambda q: [q.dma_start(out=idb_sb[:], in_=identb[:, :]),
                               q.dma_start(out=idf_sb[:], in_=identf[:, :]),
                               q.dma_start(out=maskc_sb[:], in_=maskc[:, :]),
                               q.dma_start(out=cc_sb[:], in_=cc1640[:, :]),
                               q.dma_start(out=cr_sb[:], in_=cr820[:, :])],
             writes=[r_idb, r_idf, r_maskc, r_cc, r_cr], ndma=5)

        r_xnsp, r_xosp, r_krsp, r_vsp, r_qrsp, r_ctxsp, r_qn2sp = (
            Res(), Res(), Res(), Res(), Res(), Res(), Res())
        r_xnsp_tc = [Res() for _ in range(32)]
        r_out = Res()

        def emit_ln(bx, r_bx, bxn, r_bxn, st48, r_st48, mv, r_mv, rstd, r_rstd,
                    nmr, r_nmr):
            P.op("dve", lambda q: [
                q.bn_stats(st48[:, 6 * i:6 * i + 6], bx[:, 512 * i:512 * i + 512])
                for i in range(8)], reads=[r_bx], writes=[r_st48])
            P.op("dve", lambda q: q.bn_aggr(mv[:], st48[:]),
                 reads=[r_st48], writes=[r_mv])
            P.op("dve", lambda q: q.tensor_scalar(
                out=mv[:, 1:2], in0=mv[:, 1:2], scalar1=1e-5, scalar2=None,
                op0=ALU.add), writes=[r_mv])
            P.op("act", lambda q: q.activation(
                out=rstd[:], in_=mv[:, 1:2], func=AF.Sqrt, bias=zero_c[:],
                scale=1.0), reads=[r_mv, r_zero], writes=[r_rstd])
            P.op("dve", lambda q: q.reciprocal(rstd[:], rstd[:]), writes=[r_rstd])
            P.op("dve", lambda q: q.tensor_tensor(
                out=nmr[:], in0=mv[:, 0:1], in1=rstd[:], op=ALU.mult),
                reads=[r_mv, r_rstd], writes=[r_nmr])
            P.op("dve", lambda q: q.tensor_scalar(
                out=nmr[:], in0=nmr[:], scalar1=-1.0, scalar2=None, op0=ALU.mult),
                writes=[r_nmr])
            P.op("act", lambda q: q.activation(
                out=bxn[:], in_=bx[:], func=AF.Identity, bias=nmr[:], scale=rstd[:]),
                reads=[r_bx, r_nmr, r_rstd], writes=[r_bxn])

        def _phase1():
            with ExitStack() as ph:
                def sbl(name, shape, dt):
                    t = ph.enter_context(nc.sbuf_tensor(name, list(shape), dt))
                    return t, Res()

                def psl(name, shape, dt):
                    t = ph.enter_context(nc.psum_tensor(name, list(shape), dt))
                    return t, Res()

                xt = [sbl("a0x%d" % i, (128, E), F32) for i in range(2)]
                xnt = [sbl("a0n%d" % i, (128, E), BF16) for i in range(2)]
                xo = [sbl("a0o%d" % i, (128, 4096), BF16) for i in range(2)]
                st48, r_st48 = sbl("a0s", (128, 48), F32)
                mv, r_mv = sbl("a0m", (128, 2), F32)
                rstd, r_rstd = sbl("a0r", (128, 1), F32)
                nmr, r_nmr = sbl("a0nm", (128, 1), F32)
                pst, r_pst = psl("a0p", (128, 1024), BF16)

                for tc in range(32):
                    bx, r_bx = xt[tc % 2]
                    bxn, r_bxn = xnt[tc % 2]
                    bo, r_bo = xo[tc % 2]
                    P.op("syn", lambda q, tc=tc, bx=bx: q.dma_start(
                        out=bx[:], in_=xfull[128 * tc:128 * tc + 128, :]),
                        writes=[r_bx], ndma=1)
                    emit_ln(bx, r_bx, bxn, r_bxn, st48, r_st48, mv, r_mv,
                            rstd, r_rstd, nmr, r_nmr)
                    for g in range(4):
                        P.op("pe", lambda q, bxn=bxn, g=g: [
                            q.transpose(pst[:, 128 * j:128 * j + 128],
                                        bxn[:, 128 * (8 * g + j):128 * (8 * g + j) + 128],
                                        idb_sb[:, :]) for j in range(8)],
                            reads=[r_bxn, r_idb], writes=[r_pst])
                        P.op("dve", lambda q, bo=bo, g=g: q.tensor_copy(
                            bo[:, 1024 * g:1024 * g + 1024], pst[:]),
                            reads=[r_pst], writes_acc=[r_bo])
                    P.op("pool", lambda q, bo=bo, tc=tc: [
                        q.dma_start(out=xnsp[ec, :, 128 * tc:128 * tc + 128],
                                    in_=bo[:, 128 * ec:128 * ec + 128])
                        for ec in range(32)], reads=[r_bo],
                        writes_acc=[r_xnsp, r_xnsp_tc[tc]], ndma=32)

        _phase1()
        P.phase_barrier()

        def _phase2():
            with ExitStack() as ph:
                def sbl(name, shape, dt):
                    t = ph.enter_context(nc.sbuf_tensor(name, list(shape), dt))
                    return t, Res()

                def psl(name, shape, dt):
                    t = ph.enter_context(nc.psum_tensor(name, list(shape), dt))
                    return t, Res()

                wk_sb, r_wk = sbl("a1w", (128, 32 * 1024), BF16)
                xnu = [sbl("a1x%d" % i, (128, 32 * 512), BF16) for i in range(2)]
                cosk_sb, r_cosk = sbl("a1c", (128, S), F32)
                sink_sb, r_sink = sbl("a1s", (128, S), F32)
                tmpc, r_tmpc = sbl("a1tc", (128, 512), F32)
                tmps, r_tmps = sbl("a1ts", (128, 512), F32)
                krbf, r_krbf = sbl("a1kb", (128, 512), BF16)
                kr2, r_kr2 = sbl("a1k2", (128, 512), BF16)
                kn1, r_kn1 = sbl("a1kn", (1, 1), F32)
                psk = [psl("a1p%d" % i, (128, 512), F32) for i in range(2)]
                psm, r_psm = psl("a1m", (128, 512), F32)

                P.op("syn", lambda q: [q.dma_start(out=cosk_sb[:], in_=coskf[:, :]),
                                       q.dma_start(out=sink_sb[:], in_=sinkf[:, :])],
                     writes=[r_cosk, r_sink], ndma=2)
                for j in range(8):
                    P.op("pool", lambda q, j=j: q.dma_start(
                        out=wk_sb[:].rearrange("p (ec o) -> p ec o", ec=32)[
                            :, :, 128 * j:128 * j + 128],
                        in_=wkt[:, 128 * j:128 * j + 128].rearrange(
                            "(ec p) o -> p ec o", p=128)),
                        writes_acc=[r_wk], ndma=1)
                for u in range(8):
                    bx, r_bx = xnu[u % 2]
                    P.op("syn", lambda q, u=u, bx=bx: [
                        q.dma_start(out=bx[:, 512 * ec:512 * ec + 512],
                                    in_=xnsp[ec, :, 512 * u:512 * u + 512])
                        for ec in range(32)],
                        reads=[r_xnsp_tc[t] for t in range(4 * u, 4 * u + 4)],
                        writes=[r_bx], ndma=32)
                    for kvh in range(KVH):
                        pk, r_pk = psk[kvh % 2]
                        P.op("pe", lambda q, bx=bx, kvh=kvh, pk=pk: [
                            q.matmul(pk[:, 0:512],
                                     wk_sb[:, 1024 * ec + 128 * kvh:1024 * ec + 128 * kvh + 128],
                                     bx[:, 512 * ec:512 * ec + 512],
                                     start=(ec == 0), stop=(ec == 31))
                            for ec in range(32)], reads=[r_wk, r_bx], writes=[r_pk])
                        P.op("dve", lambda q, pk=pk, u=u: q.tensor_tensor(
                            out=tmpc[:], in0=pk[:, 0:512],
                            in1=cosk_sb[:, 512 * u:512 * u + 512], op=ALU.mult),
                            reads=[r_pk, r_cosk], writes=[r_tmpc])
                        P.op("dve", lambda q, pk=pk, u=u: [
                            q.tensor_tensor(out=tmps[0:64, :], in0=pk[64:128, 0:512],
                                            in1=sink_sb[0:64, 512 * u:512 * u + 512],
                                            op=ALU.mult),
                            q.tensor_tensor(out=tmps[64:128, :], in0=pk[0:64, 0:512],
                                            in1=sink_sb[64:128, 512 * u:512 * u + 512],
                                            op=ALU.mult)],
                            reads=[r_pk, r_sink], writes=[r_tmps])
                        P.op("dve", lambda q: q.tensor_tensor(
                            out=krbf[:], in0=tmpc[:], in1=tmps[:], op=ALU.add),
                            reads=[r_tmpc, r_tmps], writes=[r_krbf])
                        P.op("pool", lambda q, kvh=kvh, u=u: q.dma_start(
                            out=krsp[kvh, :, 512 * u:512 * u + 512], in_=krbf[:]),
                            reads=[r_krbf], writes_acc=[r_krsp], ndma=1)
                        P.op("act", lambda q: q.activation(
                            out=kr2[:], in_=krbf[:], func=AF.Square, bias=zero_c[:],
                            scale=1.0), reads=[r_krbf, r_zero], writes=[r_kr2])
                        P.op("pe", lambda q: q.matmul(
                            psm[0:1, 0:512], onescol[:, :], kr2[:], start=True,
                            stop=True), reads=[r_onescol, r_kr2], writes=[r_psm])
                        P.op("dve", lambda q: q.tensor_reduce(
                            out=kn1[:], in_=psm[0:1, 0:512], axis=AXX, op=ALU.max),
                            reads=[r_psm], writes=[r_kn1])
                        P.op("dve", lambda q: q.tensor_tensor(
                            out=kmax_acc[:], in0=kmax_acc[:], in1=kn1[:], op=ALU.max),
                            reads=[r_kn1], writes=[r_kmax])
                if debug:
                    P.op("syn", lambda q: [
                        q.dma_start(out=dbg["d_cosk"][:, :], in_=cosk_sb[:]),
                        q.dma_start(out=dbg["d_sink"][:, :], in_=sink_sb[:])],
                        reads=[r_cosk, r_sink], ndma=2)

        _phase2()
        P.phase_barrier()

        def _phase3():
            with ExitStack() as ph:
                def sbl(name, shape, dt):
                    t = ph.enter_context(nc.sbuf_tensor(name, list(shape), dt))
                    return t, Res()

                def psl(name, shape, dt):
                    t = ph.enter_context(nc.psum_tensor(name, list(shape), dt))
                    return t, Res()

                wv_sb, r_wv = sbl("a2w", (128, 32 * 1024), BF16)
                xnu = [sbl("a2x%d" % i, (128, 32 * 512), BF16) for i in range(2)]
                vbf = [sbl("a2v%d" % i, (128, 1024), BF16) for i in range(2)]
                psvs = [psl("a2p%d" % i, (128, 1024), F32) for i in range(2)]

                for j in range(8):
                    P.op("pool", lambda q, j=j: q.dma_start(
                        out=wv_sb[:].rearrange("p (ec o) -> p ec o", ec=32)[
                            :, :, 128 * j:128 * j + 128],
                        in_=wvt[:, 128 * j:128 * j + 128].rearrange(
                            "(ec p) o -> p ec o", p=128)),
                        writes_acc=[r_wv], ndma=1)
                for u in range(8):
                    bx, r_bx = xnu[u % 2]
                    P.op("syn", lambda q, u=u, bx=bx: [
                        q.dma_start(out=bx[:, 512 * ec:512 * ec + 512],
                                    in_=xnsp[ec, :, 512 * u:512 * u + 512])
                        for ec in range(32)],
                        reads=[r_xnsp_tc[t] for t in range(4 * u, 4 * u + 4)],
                        writes=[r_bx], ndma=32)
                    for tsub in range(4):
                        bv, r_bv = vbf[tsub % 2]
                        psv, r_psv = psvs[tsub % 2]
                        P.op("pe", lambda q, bx=bx, tsub=tsub, psv=psv: [
                            q.matmul(psv[:, 512 * oh:512 * oh + 512],
                                     bx[:, 512 * ec + 128 * tsub:512 * ec + 128 * tsub + 128],
                                     wv_sb[:, 1024 * ec + 512 * oh:1024 * ec + 512 * oh + 512],
                                     start=(ec == 0), stop=(ec == 31))
                            for oh in range(2) for ec in range(32)],
                            reads=[r_wv, r_bx], writes=[r_psv])
                        P.op("act", lambda q, bv=bv, psv=psv: q.copy(out=bv[:], in_=psv[:]),
                             reads=[r_psv], writes=[r_bv])
                        P.op("pool", lambda q, bv=bv, u=u, tsub=tsub: [
                            q.dma_start(out=vsp[kvh, :, 4 * u + tsub, :],
                                        in_=bv[:, 128 * kvh:128 * kvh + 128])
                            for kvh in range(KVH)],
                            reads=[r_bv], writes_acc=[r_vsp], ndma=KVH)

        _phase3()
        P.phase_barrier()

        def _phase4():
            with ExitStack() as ph:
                def sbl(name, shape, dt):
                    t = ph.enter_context(nc.sbuf_tensor(name, list(shape), dt))
                    return t, Res()

                def psl(name, shape, dt):
                    t = ph.enter_context(nc.psum_tensor(name, list(shape), dt))
                    return t, Res()

                xno_sb, r_xno = sbl("a3xo", (128, 32 * 1024), BF16)
                xt = [sbl("a3x%d" % i, (128, E), F32) for i in range(2)]
                xnt = [sbl("a3n%d" % i, (128, E), BF16) for i in range(2)]
                st48, r_st48 = sbl("a3s", (128, 48), F32)
                mv, r_mv = sbl("a3m", (128, 2), F32)
                rstd, r_rstd = sbl("a3r", (128, 1), F32)
                nmr, r_nmr = sbl("a3nm", (128, 1), F32)
                pst, r_pst = psl("a3p", (128, 1024), BF16)

                for sl in range(8):
                    bx, r_bx = xt[sl % 2]
                    bxn, r_bxn = xnt[sl % 2]
                    P.op("syn", lambda q, sl=sl, bx=bx: q.dma_start(
                        out=bx[:], in_=xown[128 * sl:128 * sl + 128, :]),
                        writes=[r_bx], ndma=1)
                    emit_ln(bx, r_bx, bxn, r_bxn, st48, r_st48, mv, r_mv,
                            rstd, r_rstd, nmr, r_nmr)
                    for g in range(4):
                        P.op("pe", lambda q, bxn=bxn, g=g: [
                            q.transpose(pst[:, 128 * j:128 * j + 128],
                                        bxn[:, 128 * (8 * g + j):128 * (8 * g + j) + 128],
                                        idb_sb[:, :]) for j in range(8)],
                            reads=[r_bxn, r_idb], writes=[r_pst])
                        P.op("dve", lambda q, g=g, sl=sl: [
                            q.tensor_copy(
                                xno_sb[:, 1024 * (8 * g + j) + 128 * sl:
                                       1024 * (8 * g + j) + 128 * sl + 128],
                                pst[:, 128 * j:128 * j + 128]) for j in range(8)],
                            reads=[r_pst], writes_acc=[r_xno])
                P.op("pool", lambda q: [
                    q.dma_start(out=xosp[ec, :, :],
                                in_=xno_sb[:, 1024 * ec:1024 * ec + 1024])
                    for ec in range(32)], reads=[r_xno], writes_acc=[r_xosp], ndma=32)

                wqb = [sbl("a3wb%d" % i, (128, 4096), BF16) for i in range(2)]
                cosq_sb, r_cosq = sbl("a3c", (128, 1024), F32)
                sinq_sb, r_sinq = sbl("a3si", (128, 1024), F32)
                tmpc, r_tmpc = sbl("a3tc", (128, 512), F32)
                tmps, r_tmps = sbl("a3ts", (128, 512), F32)
                qrbf, r_qrbf = sbl("a3qb", (128, 512), BF16)
                qr2, r_qr2 = sbl("a3q2", (128, 512), BF16)
                qn2row, r_qn2row = sbl("a3qn", (1, 1024), F32)
                psq = [psl("a3pq%d" % i, (128, 512), F32) for i in range(2)]
                psm, r_psm = psl("a3pm", (128, 512), F32)

                P.op("syn", lambda q: [q.dma_start(out=cosq_sb[:], in_=cosqf[:, :]),
                                       q.dma_start(out=sinq_sb[:], in_=sinqf[:, :])],
                     writes=[r_cosq, r_sinq], ndma=2)
                for oc in range(32):
                    bb, r_bb = wqb[oc % 2]
                    P.op("pool", lambda q, oc=oc, bb=bb: [
                        q.dma_start(out=bb[:, 1024 * jj:1024 * jj + 1024].rearrange(
                                        "p (ec o) -> p ec o", ec=8),
                                    in_=wqt[1024 * jj:1024 * jj + 1024,
                                            128 * oc:128 * oc + 128].rearrange(
                                        "(ec p) o -> p ec o", p=128))
                        for jj in range(4)], writes=[r_bb], ndma=4)
                    for th in range(2):
                        pq, r_pq = psq[th]
                        P.op("pe", lambda q, bb=bb, th=th, pq=pq: [
                            q.matmul(pq[:, 0:512],
                                     bb[:, 128 * ec:128 * ec + 128],
                                     xno_sb[:, 1024 * ec + 512 * th:1024 * ec + 512 * th + 512],
                                     start=(ec == 0), stop=(ec == 31))
                            for ec in range(32)], reads=[r_bb, r_xno], writes=[r_pq])
                        P.op("dve", lambda q, pq=pq, th=th: q.tensor_tensor(
                            out=tmpc[:], in0=pq[:, 0:512],
                            in1=cosq_sb[:, 512 * th:512 * th + 512], op=ALU.mult),
                            reads=[r_pq, r_cosq], writes=[r_tmpc])
                        P.op("dve", lambda q, pq=pq, th=th: [
                            q.tensor_tensor(out=tmps[0:64, :], in0=pq[64:128, 0:512],
                                            in1=sinq_sb[0:64, 512 * th:512 * th + 512],
                                            op=ALU.mult),
                            q.tensor_tensor(out=tmps[64:128, :], in0=pq[0:64, 0:512],
                                            in1=sinq_sb[64:128, 512 * th:512 * th + 512],
                                            op=ALU.mult)],
                            reads=[r_pq, r_sinq], writes=[r_tmps])
                        P.op("dve", lambda q: q.tensor_tensor(
                            out=qrbf[:], in0=tmpc[:], in1=tmps[:], op=ALU.add),
                            reads=[r_tmpc, r_tmps], writes=[r_qrbf])
                        P.op("pool", lambda q, oc=oc, th=th: q.dma_start(
                            out=qrsp[oc, :, 512 * th:512 * th + 512], in_=qrbf[:]),
                            reads=[r_qrbf], writes_acc=[r_qrsp], ndma=1)
                        P.op("act", lambda q: q.activation(
                            out=qr2[:], in_=qrbf[:], func=AF.Square, bias=zero_c[:],
                            scale=1.0), reads=[r_qrbf, r_zero], writes=[r_qr2])
                        P.op("pe", lambda q, th=th: q.matmul(
                            psm[0:1, 0:512], onescol[:, :], qr2[:], start=True,
                            stop=True), reads=[r_onescol, r_qr2], writes=[r_psm])
                        P.op("dve", lambda q, th=th: q.tensor_copy(
                            qn2row[0:1, 512 * th:512 * th + 512], psm[0:1, 0:512]),
                            reads=[r_psm], writes_acc=[r_qn2row])
                    P.op("pool", lambda q, oc=oc: q.dma_start(
                        out=qn2sp[oc:oc + 1, :], in_=qn2row[:]),
                        reads=[r_qn2row], writes_acc=[r_qn2sp], ndma=1)

                if debug:
                    P.op("syn", lambda q: [
                        q.dma_start(out=dbg["d_cosq"][:, :], in_=cosq_sb[:]),
                        q.dma_start(out=dbg["d_qn2r"][:, :], in_=qn2row[:])],
                        reads=[r_cosq, r_qn2row], ndma=2)

                psb, r_psb = psl("a3pb", (128, 512), F32)
                sqk, r_sqk = sbl("a3sq", (1, 1), F32)
                qnall, r_qnall = sbl("a3qa", (32, 1024), F32)
                sqn, r_sqn = sbl("a3sn", (128, 32), F32)

                P.op("act", lambda q: q.activation(
                    out=sqk[:], in_=kmax_acc[:], func=AF.Sqrt, bias=zero_c[0:1, :],
                    scale=1.0), reads=[r_kmax, r_zero], writes=[r_sqk])
                P.op("dve", lambda q: q.tensor_scalar(
                    out=sqk[:], in0=sqk[:], scalar1=-1.001 / SQD, scalar2=None,
                    op0=ALU.mult), writes=[r_sqk])
                P.op("pe", lambda q: q.matmul(psb[:, 64:65], ones1F[0:1, :],
                                              sqk[0:1, :], start=True, stop=True),
                     reads=[r_ones1F, r_sqk], writes=[r_psb])
                P.op("dve", lambda q: q.tensor_copy(negkb[:], psb[:, 64:65]),
                     reads=[r_psb], writes=[r_negkb])
                P.op("syn", lambda q: q.dma_start(out=qnall[:], in_=qn2sp[:, :]),
                     reads=[r_qn2sp], writes=[r_qnall], ndma=1)
                for s in range(NSLOT):
                    P.op("pe", lambda q, s=s: q.transpose(
                        psb[:, 96:128], qnall[0:32, 128 * s:128 * s + 128],
                        idf_sb[0:32, 0:32]),
                        reads=[r_qnall, r_idf], writes=[r_psb])
                    P.op("act", lambda q, s=s: q.activation(
                        out=sqn[:], in_=psb[:, 96:128], func=AF.Sqrt,
                        bias=zero_c[:], scale=1.0),
                        reads=[r_psb, r_zero], writes=[r_sqn])
                    P.op("dve", lambda q, s=s: q.tensor_scalar(
                        out=negb_sb[:, H * s:H * s + 32], in0=sqn[:],
                        scalar1=negkb[:], scalar2=None, op0=ALU.mult),
                        reads=[r_sqn, r_negkb], writes_acc=[r_negb])

        _phase4()
        P.phase_barrier()
        if debug:
            P.op("syn", lambda q: [
                q.dma_start(out=dbg["d_xn"][:, :, :], in_=xnsp[:, :, :]),
                q.dma_start(out=dbg["d_kr"][:, :, :], in_=krsp[:, :, :]),
                q.dma_start(out=dbg["d_v"][:, :, :, :], in_=vsp[:, :, :, :]),
                q.dma_start(out=dbg["d_qr"][:, :, :], in_=qrsp[:, :, :]),
                q.dma_start(out=dbg["d_qn2"][:, :], in_=qn2sp[:, :]),
                q.dma_start(out=dbg["d_xo"][:, :, :], in_=xosp[:, :, :]),
                q.dma_start(out=dbg["d_negb"][:, :], in_=negb_sb[:])],
                reads=[r_xnsp, r_krsp, r_vsp, r_qrsp, r_qn2sp, r_negb, r_xosp],
                ndma=7)

        def _phase5():
            with ExitStack() as ph:
                def sbl(name, shape, dt):
                    t = ph.enter_context(nc.sbuf_tensor(name, list(shape), dt))
                    return t, Res()

                def psl(name, shape, dt):
                    t = ph.enter_context(nc.psum_tensor(name, list(shape), dt))
                    return t, Res()

                sim_sb, r_sim = sbl("bsim", (128, TOT), BF16)
                kr_sb0, r_kr0 = sbl("bkr0", (128, S), BF16)
                kr_sb1, r_kr1 = sbl("bkr1", (128, S), BF16)
                v_sb0, r_v0 = sbl("bv0", (128, S), BF16)
                v_sb1, r_v1 = sbl("bv1", (128, S), BF16)
                KRV = [((kr_sb0, r_kr0), (v_sb0, r_v0)),
                       ((kr_sb1, r_kr1), (v_sb1, r_v1))]
                qr_sb, r_qr = sbl("bqr", (128, 1024), BF16)
                hw1b, r_hw1b = sbl("bh1b", (128, 128), BF16)
                hw2b, r_hw2b = sbl("bh2b", (128, 128), BF16)
                hb1_sb, r_hb1 = sbl("bhb1", (128, 1), F32)
                hb2_sb, r_hb2 = sbl("bhb2", (128, 1), F32)
                y1p_sb, r_y1p = sbl("by1p", (128, 1024), BF16)
                zp_sb, r_zp = sbl("bzp", (128, 1024), F32)
                y_sb, r_y = sbl("by", (128, 5120), BF16)
                hq_sb, r_hq = sbl("bhq", (128, 5120), BF16)
                p0_sb, r_p0 = sbl("bp0", (128, 4096), BF16)
                ps_sb, r_ps = sbl("bps", (128, 4096), F16)
                sel_sb, r_sel = sbl("bsel", (128, 4096), BF16)
                p_sb0, r_p0r = sbl("bpA0", (128, 4096), BF16)
                p_sb1, r_p1r = sbl("bpA1", (128, 4096), BF16)
                PBUF = [(p_sb0, r_p0r), (p_sb1, r_p1r)]
                pt_sb, r_pt = sbl("bpt", (128, 4096), BF16)
                rr_sb, r_rr = sbl("brr", (128, 4096), F16)
                trash_a = sel_sb
                trash_d = ps_sb
                loT, r_lo = sbl("blo", (128, NSLOT), F32)
                midT, r_mid = sbl("bmid", (128, NSLOT), F32)
                ngm, r_ngm = sbl("bngm", (128, NSLOT), F32)
                accA, r_accA = sbl("baca", (128, NSLOT), F32)
                accD, r_accD = sbl("bacd", (128, NSLOT), F32)
                t2, r_t2 = sbl("bt2", (128, NSLOT), F32)
                geT, r_ge = sbl("bge", (128, NSLOT), F32)
                geb, r_geb = sbl("bgeb", (128, NSLOT), F32)
                cntT, r_cnt = sbl("bcnt", (128, NSLOT), F32)
                t3, r_t3 = sbl("bt3", (128, NSLOT), F32)
                chiT, r_chi = sbl("bchi", (128, NSLOT), F32)
                rmin, r_rmin = sbl("brm", (128, NSLOT), F32)
                zacc, r_zacc = sbl("bza", (128, NSLOT), F32)
                zrT, r_zr = sbl("bzr", (128, NSLOT), F32)
                cstg, r_cstg = sbl("bcs", (128, 1024), BF16)
                psA, r_psA = psl("bpA", (128, 2048), F32)
                psB, r_psB = psl("bpB", (128, 1024), F32)
                psC, r_psC = psl("bpC", (128, 512), F32)
                psT, r_psT = psl("bpT", (128, 1024), BF16)
                r_psTh = [Res(), Res()]

                sim2_sb, r_sim2 = sbl("bsim2", (128, TOT), BF16)
                qr2_sb, r_qr2 = sbl("bqr2", (128, 1024), BF16)
                SIM = [(sim_sb, r_sim), (sim2_sb, r_sim2)]
                QR = [(qr_sb, r_qr), (qr2_sb, r_qr2)]

                def front_load(h):
                    qrb, r_qrb = QR[h % 2]
                    kvh = h // GRP
                    (kr_sb, r_kr), (v_sb, r_v) = KRV[kvh % 2]
                    if h % GRP == 0:
                        P.op("syn", lambda q, kvh=kvh, kr_sb=kr_sb: q.dma_start(
                            out=kr_sb[:], in_=krsp[kvh, :, :]),
                            reads=[r_krsp], writes=[r_kr], ndma=1)
                        P.op("syn", lambda q, kvh=kvh, v_sb=v_sb: q.dma_start(
                            out=v_sb[:], in_=vsp[kvh, :, :, :].rearrange(
                                "p tc d -> p (tc d)")),
                            reads=[r_vsp], writes=[r_v], ndma=1)
                    P.op("syn", lambda q, h=h, qrb=qrb: [
                        q.dma_start(out=qrb[:], in_=qrsp[h, :, :]),
                        q.dma_start(out=hb1_sb[:], in_=hb1[h, :, :]),
                        q.dma_start(out=hb2_sb[:], in_=hb2[h, :, :])],
                        reads=[r_qrsp], writes=[r_qrb, r_hb1, r_hb2], ndma=3)
                    P.op("pool", lambda q, h=h: [
                        q.dma_start(out=hw1b[:], in_=hp1[h, :, :]),
                        q.dma_start(out=hw2b[:], in_=hp2[h, :, :])],
                        writes=[r_hw1b, r_hw2b], ndma=2)

                def front_hash(h):
                    qrb, r_qrb = QR[h % 2]
                    (kr_sb, r_kr), (v_sb, r_v) = KRV[(h // GRP) % 2]
                    P.op("dve", lambda q, qrb=qrb, kr_sb=kr_sb: [
                        q.tensor_copy(y_sb[:, 0:4096], kr_sb[:]),
                        q.tensor_copy(y_sb[:, 4096:5120], qrb[:])],
                        reads=[r_kr, r_qrb], writes=[r_y])
                    for pc in range(5):
                        base = 1024 * pc
                        bank = psA[:, 1024 * (pc % 2):1024 * (pc % 2) + 1024]
                        P.op("pe", lambda q, base=base, bank=bank: [
                            q.matmul(bank[:, 512 * j:512 * j + 512], hw1b[:, :],
                                     y_sb[:, base + 512 * j:base + 512 * j + 512],
                                     start=True, stop=True) for j in range(2)],
                            reads=[r_hw1b, r_y], writes=[r_psA])
                        P.op("act", lambda q, bank=bank: q.activation(
                            out=y1p_sb[:], in_=bank, func=AF.Silu, bias=hb1_sb[:],
                            scale=1.0), reads=[r_psA, r_hb1], writes=[r_y1p])
                        P.op("pool", lambda q, base=base: q.tensor_tensor(
                            out=y_sb[:, base:base + 1024], in0=y1p_sb[:],
                            in1=y_sb[:, base:base + 1024], op=ALU.add),
                            reads=[r_y1p], writes_acc=[r_y])
                        P.op("pe", lambda q, base=base, bank=bank: [
                            q.matmul(bank[:, 512 * j:512 * j + 512], hw2b[:, :],
                                     y_sb[:, base + 512 * j:base + 512 * j + 512],
                                     start=True, stop=True) for j in range(2)],
                            reads=[r_hw2b, r_y], writes=[r_psA])
                        P.op("act", lambda q, bank=bank: q.activation(
                            out=zp_sb[:], in_=bank, func=AF.Identity, bias=hb2_sb[:],
                            scale=1.0), reads=[r_psA, r_hb2], writes=[r_zp])
                        P.op("dve", lambda q, base=base: q.tensor_tensor(
                            out=zp_sb[:], in0=zp_sb[:],
                            in1=y_sb[:, base:base + 1024], op=ALU.add),
                            reads=[r_y], writes=[r_zp])
                        P.op("act", lambda q, base=base: q.activation(
                            out=hq_sb[:, base:base + 1024], in_=zp_sb[:],
                            func=AF.Sign, bias=zero_c[:], scale=1.0),
                            reads=[r_zp, r_zero], writes_acc=[r_hq])

                def front_sim(h, s):
                    simb, r_simb = SIM[h % 2]
                    if True:
                        Es = EXT[s]
                        qsl = hq_sb[:, 4096 + 128 * s:4096 + 128 * s + 128]
                        for piece in range(0, Es, 2048):
                            plen = min(2048, Es - piece)
                            nj = plen // 512

                            def mmfn(q, s=s, piece=piece, nj=nj, Es=Es, qsl=qsl):
                                ins = []
                                for j in range(nj):
                                    t0 = piece + 512 * j
                                    is_diag = (t0 == Es - 512)
                                    ins.append(q.matmul(
                                        psA[:, 512 * j:512 * j + 512], qsl,
                                        hq_sb[:, t0:t0 + 512],
                                        start=True, stop=not is_diag))
                                    if is_diag:
                                        ins.append(q.matmul(
                                            psA[:, 512 * j:512 * j + 512],
                                            idb_sb[:, :], maskc_sb[:, :],
                                            start=False, stop=True))
                                return ins
                            P.op("pe", mmfn, reads=[r_hq, r_idb, r_maskc],
                                 writes=[r_psA])
                            P.op("act", lambda q, s=s, piece=piece, plen=plen,
                                 simb=simb:
                                 q.activation(
                                     out=simb[:, OFF[s] + piece:OFF[s] + piece + plen],
                                     in_=psA[:, 0:plen], func=AF.Identity,
                                     bias=zero_c[:], scale=1.0),
                                 reads=[r_psA, r_zero], writes_acc=[r_simb])
                    if debug and h == 0 and s == NSLOT - 1:
                        P.op("syn", lambda q, simb=simb: q.dma_start(
                            out=dbg["d_sim"][:, :], in_=simb[:]),
                            reads=[r_simb], ndma=1)

                def front(h):
                    front_load(h)
                    front_hash(h)
                    for s in range(NSLOT):
                        front_sim(h, s)

                def mid_a(h):
                    simb, r_simb = SIM[h % 2]
                    P.op("dve", lambda q: [q.memset(loT[:], -130.0),
                                           q.memset(chiT[:], 0.0)],
                         writes=[r_lo, r_chi])
                    for rnd, step in enumerate(STEPS):
                        P.op("dve", lambda q, step=step: q.tensor_scalar(
                            out=midT[:], in0=loT[:], scalar1=step, scalar2=None,
                            op0=ALU.add), reads=[r_lo], writes=[r_mid])
                        P.op("dve", lambda q: q.tensor_scalar(
                            out=ngm[:], in0=midT[:], scalar1=-1.0, scalar2=0.5,
                            op0=ALU.mult, op1=ALU.add),
                            reads=[r_mid], writes=[r_ngm])
                        P.op("act", lambda q, simb=simb: [
                            q.activation(out=trash_a[:, 0:SPL[s]],
                                         in_=simb[:, OFF[s]:OFF[s] + SPL[s]],
                                         func=AF.Sign, bias=ngm[:, s:s + 1],
                                         scale=1.0,
                                         accum_out=accA[:, s:s + 1])
                            for s in range(1, NSLOT)] + [
                            q.activation(out=trash_a[:, 0:512],
                                         in_=simb[:, 0:512], func=AF.Sign,
                                         bias=zero_c[:], scale=1.0)],
                            reads=[r_simb, r_ngm], writes=[r_accA])
                        P.op("dve", lambda q, simb=simb: [
                            q.tensor_scalar(out=trash_d[:, 0:EXT[s] - SPL[s]],
                                            in0=simb[:, OFF[s] + SPL[s]:OFF[s] + EXT[s]],
                                            scalar1=midT[:, s:s + 1], scalar2=None,
                                            op0=ALU.is_ge, op1=ALU.add,
                                            accum_out=accD[:, s:s + 1])
                            for s in range(1, NSLOT)] + [
                            q.tensor_copy(trash_d[:, 0:512], simb[:, 0:512])],
                            reads=[r_simb, r_mid], writes=[r_accD])
                        P.op("dve", lambda q: q.scalar_tensor_tensor(
                            out=t2[:], in0=accD[:], scalar=2.0, in1=accA[:],
                            op0=ALU.mult, op1=ALU.add),
                            reads=[r_accD, r_accA], writes=[r_t2])
                        P.op("dve", lambda q: q.tensor_tensor(
                            out=geT[:], in0=t2[:], in1=cc_sb[:], op=ALU.is_ge),
                            reads=[r_t2, r_cc], writes=[r_ge])
                        P.op("dve", lambda q, step=step: q.scalar_tensor_tensor(
                            out=loT[:], in0=geT[:], scalar=step, in1=loT[:],
                            op0=ALU.mult, op1=ALU.add),
                            reads=[r_ge], writes=[r_lo])
                        P.op("dve", lambda q: q.scalar_tensor_tensor(
                            out=cntT[:], in0=t2[:], scalar=0.5, in1=cr_sb[:],
                            op0=ALU.mult, op1=ALU.add),
                            reads=[r_t2, r_cr], writes=[r_cnt])
                        P.op("dve", lambda q: q.tensor_scalar(
                            out=geb[:], in0=geT[:], scalar1=-1.0, scalar2=1.0,
                            op0=ALU.mult, op1=ALU.add),
                            reads=[r_ge], writes=[r_geb])
                        P.op("dve", lambda q: q.tensor_tensor(
                            out=t3[:], in0=cntT[:], in1=chiT[:], op=ALU.subtract),
                            reads=[r_cnt, r_chi], writes=[r_t3])
                        P.op("dve", lambda q: q.tensor_tensor(
                            out=t3[:], in0=t3[:], in1=geb[:], op=ALU.mult),
                            reads=[r_geb], writes=[r_t3])
                        P.op("dve", lambda q: q.tensor_tensor(
                            out=chiT[:], in0=chiT[:], in1=t3[:], op=ALU.add),
                            reads=[r_t3], writes=[r_chi])
                    P.op("dve", lambda q: q.scalar_tensor_tensor(
                        out=rmin[:], in0=loT[:], scalar=-4096.0, in1=chiT[:],
                        op0=ALU.mult, op1=ALU.subtract),
                        reads=[r_lo, r_chi], writes=[r_rmin])
                    P.op("dve", lambda q: q.tensor_scalar(
                        out=rmin[:], in0=rmin[:], scalar1=820.0, scalar2=None,
                        op0=ALU.add), writes=[r_rmin])
                    P.op("dve", lambda q: [q.memset(loT[:, 0:1], -130.0),
                                           q.memset(rmin[:, 0:1], 533300.0)],
                         writes_acc=[r_lo, r_rmin])
                    if debug and h == 0:
                        P.op("syn", lambda q: [
                            q.dma_start(out=dbg["d_acc"][:, 0:NSLOT], in_=accA[:]),
                            q.dma_start(out=dbg["d_acc"][:, NSLOT:2 * NSLOT],
                                        in_=accD[:])],
                            reads=[r_accA, r_accD], ndma=2)
                        P.op("syn", lambda q: [
                            q.dma_start(out=dbg["d_book"][:, 0:NSLOT], in_=loT[:]),
                            q.dma_start(out=dbg["d_book"][:, NSLOT:2 * NSLOT],
                                        in_=rmin[:])],
                            reads=[r_lo, r_rmin], ndma=2)

                def mid_b(h):
                    simb, r_simb = SIM[h % 2]
                    qrb, r_qrb = QR[h % 2]
                    (kr_sb, r_kr), (v_sb, r_v) = KRV[(h // GRP) % 2]

                    def scores_exp(s):
                        Es = EXT[s]
                        ncol = H * s + h
                        qsl = qrb[:, 128 * s:128 * s + 128]
                        for pb in range(0, Es, 1024):
                            plen = min(1024, Es - pb)
                            P.op("pe", lambda q, qsl=qsl, pb=pb, plen=plen,
                                 kr_sb=kr_sb: [
                                q.matmul(psB[:, 512 * j:512 * j + 512], qsl,
                                         kr_sb[:, pb + 512 * j:pb + 512 * j + 512],
                                         start=True, stop=True)
                                for j in range(plen // 512)],
                                reads=[r_qrb, r_kr], writes=[r_psB])
                            P.op("act", lambda q, pb=pb, plen=plen, ncol=ncol:
                                 q.activation(
                                     out=p0_sb[:, pb:pb + plen], in_=psB[:, 0:plen],
                                     func=AF.Exp, scale=1.0 / SQD,
                                     bias=negb_sb[:, ncol:ncol + 1]),
                                 reads=[r_psB, r_negb], writes_acc=[r_p0])

                    def chain(s):
                        Es = EXT[s]
                        p_sb, r_p = PBUF[s % 2]
                        eq_sb, r_eq = p_sb, r_p
                        if s == 0:
                            P.op("dve", lambda q, Es=Es, simb=simb: q.tensor_scalar(
                                out=sel_sb[:, 0:Es], in0=simb[:, OFF[0]:OFF[0] + Es],
                                scalar1=-129.0, scalar2=None, op0=ALU.is_ge),
                                reads=[r_simb], writes=[r_sel])
                        else:
                            P.op("dve", lambda q, s=s, Es=Es, simb=simb: q.tensor_scalar(
                                out=eq_sb[:, 0:Es], in0=simb[:, OFF[s]:OFF[s] + Es],
                                scalar1=loT[:, s:s + 1], scalar2=None, op0=ALU.is_equal),
                                reads=[r_simb, r_lo], writes=[r_eq])
                            P.op("dve", lambda q, Es=Es: q.tensor_tensor_scan(
                                out=ps_sb[:, 0:Es], data0=eq_sb[:, 0:Es],
                                data1=eq_sb[:, 0:Es], initial=0.0, op0=ALU.add,
                                op1=ALU.bypass), reads=[r_eq], writes=[r_ps])
                            P.op("dve", lambda q, s=s, Es=Es, simb=simb: q.tensor_scalar(
                                out=rr_sb[:, 0:Es], in0=simb[:, OFF[s]:OFF[s] + Es],
                                scalar1=4096.0, scalar2=rmin[:, s:s + 1], op0=ALU.mult,
                                op1=ALU.add), reads=[r_simb, r_rmin], writes=[r_rr])
                            P.op("dve", lambda q, Es=Es: q.tensor_tensor(
                                out=sel_sb[:, 0:Es], in0=ps_sb[:, 0:Es],
                                in1=rr_sb[:, 0:Es], op=ALU.is_le),
                                reads=[r_ps, r_rr], writes=[r_sel])
                        P.op("dve", lambda q, s=s, Es=Es: q.scalar_tensor_tensor(
                            out=p_sb[:, 0:Es], in0=p0_sb[:, 0:Es], scalar=1.0,
                            in1=sel_sb[:, 0:Es], op0=ALU.mult, op1=ALU.mult,
                            accum_out=zacc[:, s:s + 1]),
                            reads=[r_p0, r_sel], writes=[r_p], writes_acc=[r_zacc])
                        P.op("dve", lambda q, simb=simb: q.tensor_copy(
                            trash_d[:, 0:512], simb[:, 0:512]),
                            reads=[r_simb])
                        P.op("dve", lambda q, s=s: q.reciprocal(
                            zrT[:, s:s + 1], zacc[:, s:s + 1]),
                            reads=[r_zacc], writes_acc=[r_zr])
                        P.op("dve", lambda q, s=s, Es=Es: q.tensor_scalar(
                            out=p_sb[:, 0:Es], in0=p_sb[:, 0:Es],
                            scalar1=zrT[:, s:s + 1], scalar2=None, op0=ALU.mult),
                            reads=[r_zr], writes=[r_p])

                    def tail(s):
                        Es = EXT[s]
                        p_sb, r_p = PBUF[s % 2]
                        for pb in range(0, Es, 1024):
                            plen = min(1024, Es - pb)
                            P.op("pe", lambda q, pb=pb, plen=plen, p_sb=p_sb: [
                                q.transpose(psT[:, 128 * j:128 * j + 128],
                                            p_sb[:, pb + 128 * j:pb + 128 * j + 128],
                                            idb_sb[:, :])
                                for j in range(plen // 128)],
                                reads=[r_p, r_idb], writes=[r_psT])
                            P.op("act", lambda q, pb=pb, plen=plen: q.copy(
                                out=pt_sb[:, pb:pb + plen], in_=psT[:, 0:plen]),
                                reads=[r_psT], writes_acc=[r_pt])
                        ntc = Es // 128

                        def avfn(q, ntc=ntc, v_sb=v_sb):
                            return [q.matmul(
                                psC[:, 0:128], v_sb[:, 128 * tc:128 * tc + 128],
                                pt_sb[:, 128 * tc:128 * tc + 128],
                                start=(tc == 0), stop=(tc == ntc - 1))
                                for tc in range(ntc)]
                        P.op("pe", avfn, reads=[r_v, r_pt], writes=[r_psC])
                        P.op("dve", lambda q, s=s: q.tensor_copy(
                            cstg[:, 128 * s:128 * s + 128], psC[:, 0:128]),
                            reads=[r_psC], writes_acc=[r_cstg])

                    nx = h + 1 < n_heads
                    sim_order = [7, 6, 5, 4, 3, 2, 1, 0]
                    for s in range(NSLOT):
                        scores_exp(s)
                        chain(s)
                        if nx and s == 0:
                            front_load(h + 1)
                        if nx and s == 1:
                            front_hash(h + 1)
                        if nx and 2 <= s:
                            front_sim(h + 1, sim_order[s - 2])
                        if s > 0:
                            tail(s - 1)
                    tail(NSLOT - 1)
                    if nx:
                        for s2 in sim_order[6:]:
                            front_sim(h + 1, s2)
                    if debug and h == 0:
                        P.op("syn", lambda q: [
                            q.dma_start(out=dbg["d_sel"][:, :], in_=sel_sb[:]),
                            q.dma_start(out=dbg["d_p"][:, :],
                                        in_=PBUF[(NSLOT - 1) % 2][0][:])],
                            reads=[r_sel, PBUF[(NSLOT - 1) % 2][1]], ndma=2)
                    P.op("pool", lambda q, h=h: q.dma_start(
                        out=ctxsp[h, :, :], in_=cstg[:]),
                        reads=[r_cstg], writes_acc=[r_ctxsp], ndma=1)

                front(0)
                for h in range(n_heads):
                    mid_a(h)
                    mid_b(h)
        _phase5()
        P.phase_barrier()

        if debug:
            P.op("syn", lambda q: q.dma_start(
                out=dbg["d_ctx"][:, :, :], in_=ctxsp[:, :, :]),
                reads=[r_ctxsp], ndma=1)
        def _phase6():
            with ExitStack() as ph:
                def sbl(name, shape, dt):
                    t = ph.enter_context(nc.sbuf_tensor(name, list(shape), dt))
                    return t, Res()

                def psl(name, shape, dt):
                    t = ph.enter_context(nc.psum_tensor(name, list(shape), dt))
                    return t, Res()

                acomb, bcomb = [], []
                for g in range(4):
                    acomb.append(sbl("acomb%d" % g, (128, 1024), BF16))
                    bcomb.append(sbl("bcomb%d" % g, (64, E), BF16))
                last_b, r_lastb = sbl("last_b", (128, 1024), BF16)
                fbias_sb, r_fbias = sbl("fbias_sb", (1, E), BF16)
                pcs = [psl("cp%d" % s, (128, 512), F32) for s in range(8)]
                cm1 = ExitStack()
                def sbl1(name, shape, dt):
                    t = cm1.enter_context(nc.sbuf_tensor(name, list(shape), dt))
                    return t, Res()
                gbc_sb, r_gbc = sbl1("cg", (128, 64), F32)
                g1, r_g1 = sbl1("cg1", (1, 64), F32)
                lal, r_lal = sbl1("cla", (128, 1024), F32)
                lbl, r_lbl = sbl1("clb", (32, E // 2), F32)

                P.op("syn", lambda q: q.dma_start(out=g1[:], in_=gateown[:, :]),
                     writes=[r_g1], ndma=1)
                P.op("pe", lambda q: q.matmul(pcs[0][0][:, 0:64], ones1F[0:1, :],
                                              g1[0:1, :], start=True, stop=True),
                     reads=[r_ones1F, r_g1], writes=[pcs[0][1]])
                P.op("dve", lambda q: q.tensor_copy(gbc_sb[:], pcs[0][0][:, 0:64]),
                     reads=[pcs[0][1]], writes=[r_gbc])
                for g in range(4):
                    P.op("dve", lambda q, g=g: q.memset(acomb[g][0][:], 0.0),
                         writes=[acomb[g][1]])
                    P.op("dve", lambda q, g=g: q.memset(bcomb[g][0][0:32, :], 0.0),
                         writes=[bcomb[g][1]])
                for n in range(16):
                    P.op("syn", lambda q, n=n: q.dma_start(out=lal[:], in_=lat[n, :, :]),
                         writes=[r_lal], ndma=1)
                    for g in range(4):
                        P.op("dve", lambda q, g=g, n=n: q.scalar_tensor_tensor(
                            out=acomb[g][0][:], in0=lal[:],
                            scalar=gbc_sb[:, 16 * g + n:16 * g + n + 1],
                            in1=acomb[g][0][:], op0=ALU.mult, op1=ALU.add),
                            reads=[r_lal, r_gbc], writes=[acomb[g][1]])
                    for hh in range(2):
                        P.op("syn", lambda q, n=n, hh=hh: q.dma_start(
                            out=lbl[:], in_=lbt[n, :, 2048 * hh:2048 * hh + 2048]),
                            writes=[r_lbl], ndma=1)
                        for g in range(4):
                            P.op("dve", lambda q, g=g, n=n, hh=hh: q.scalar_tensor_tensor(
                                out=bcomb[g][0][0:32, 2048 * hh:2048 * hh + 2048],
                                in0=lbl[:],
                                scalar=gbc_sb[0:32, 16 * g + n:16 * g + n + 1],
                                in1=bcomb[g][0][0:32, 2048 * hh:2048 * hh + 2048],
                                op0=ALU.mult, op1=ALU.add),
                                reads=[r_lbl, r_gbc], writes=[bcomb[g][1]])
                P.op("syn", lambda q: q.dma_start(out=lal[:], in_=last[:, :]),
                     writes=[r_lal], ndma=1)
                P.op("act", lambda q: q.copy(out=last_b[:], in_=lal[:]),
                     reads=[r_lal], writes=[r_lastb])
                for hh in range(2):
                    P.op("syn", lambda q, hh=hh: q.dma_start(
                        out=lbl[:], in_=lbst[:, 2048 * hh:2048 * hh + 2048]),
                        writes=[r_lbl], ndma=1)
                    for g in range(4):
                        P.op("act", lambda q, g=g, hh=hh: q.copy(
                            out=bcomb[g][0][32:64, 2048 * hh:2048 * hh + 2048],
                            in_=lbl[:]),
                            reads=[r_lbl], writes_acc=[bcomb[g][1]])
                for hh in range(2):
                    P.op("syn", lambda q, hh=hh: q.dma_start(
                        out=lbl[0:1, :], in_=fbias[:, 2048 * hh:2048 * hh + 2048]),
                        writes=[r_lbl], ndma=1)
                    P.op("act", lambda q, hh=hh: q.copy(
                        out=fbias_sb[0:1, 2048 * hh:2048 * hh + 2048],
                        in_=lbl[0:1, :]),
                        reads=[r_lbl], writes_acc=[r_fbias])
                cm1.close()

                ctx_sb, r_ctx = sbl("cctx", (128, 32 * 1024), BF16)
                xno_sb, r_xno2 = sbl("cxno", (128, 32 * 1024), BF16)
                axt_sb, r_axt = sbl("caxt", (64, 8 * 128), BF16)
                wob = [sbl("cwob%d" % i, (128, 512), BF16) for i in range(2)]
                knb = [sbl("cknb%d" % i, (128, 512), BF16) for i in range(2)]
                xog = [sbl("cxo%d" % i, (128, 512), F32) for i in range(2)]
                ost = [sbl("cos%d" % i, (128, 512), F32) for i in range(2)]

                P.op("syn", lambda q: [
                    q.dma_start(out=ctx_sb[:, 1024 * hh:1024 * hh + 1024],
                                in_=ctxsp[hh, :, :]) for hh in range(H)],
                    reads=[r_ctxsp], writes=[r_ctx], ndma=H)
                P.op("syn", lambda q: [
                    q.dma_start(out=xno_sb[:, 1024 * ec:1024 * ec + 1024],
                                in_=xosp[ec, :, :]) for ec in range(32)],
                    reads=[r_xosp], writes=[r_xno2], ndma=32)

                for s in range(NSLOT):
                    g = s // 2
                    pc, r_pc = pcs[0]
                    def axfn(q, s=s, g=g, pc=pc):
                        ins = []
                        for ic in range(32):
                            ins.append(q.matmul(
                                pc[0:32, 0:128], acomb[g][0][:, 32 * ic:32 * ic + 32],
                                xno_sb[:, 1024 * ic + 128 * s:1024 * ic + 128 * s + 128],
                                start=(ic == 0), stop=(ic == 31)))
                            ins.append(q.matmul(
                                pc[32:64, 0:128], last_b[:, 32 * ic:32 * ic + 32],
                                xno_sb[:, 1024 * ic + 128 * s:1024 * ic + 128 * s + 128],
                                start=(ic == 0), stop=(ic == 31)))
                        return ins
                    P.op("pe", axfn, reads=[acomb[g][1], r_lastb, r_xno2],
                         writes=[r_pc])
                    P.op("dve", lambda q, s=s, pc=pc: q.tensor_copy(
                        axt_sb[:, 128 * s:128 * s + 128], pc[0:64, 0:128]),
                        reads=[r_pc], writes_acc=[r_axt])
                if debug:
                    P.op("syn", lambda q: [
                        q.dma_start(out=dbg["d_acomb"][g, :, :], in_=acomb[g][0][:])
                        for g in range(4)] + [
                        q.dma_start(out=dbg["d_bcomb"][g, :, :], in_=bcomb[g][0][:])
                        for g in range(4)] + [
                        q.dma_start(out=dbg["d_axt"][:, :], in_=axt_sb[:]),
                        q.dma_start(out=dbg["d_lastb"][:, :], in_=last_b[:])],
                        reads=[acomb[g][1] for g in range(4)] +
                              [bcomb[g][1] for g in range(4)] +
                              [r_axt, r_lastb], ndma=10)

                for ec in range(8):
                    for hc in range(32):
                        bb, r_bb = wob[hc % 2]
                        P.op("pool", lambda q, ec=ec, hc=hc, bb=bb: q.dma_start(
                            out=bb[:], in_=wot[128 * hc:128 * hc + 128,
                                               512 * ec:512 * ec + 512]),
                            writes=[r_bb], ndma=1)
                        P.op("pe", lambda q, hc=hc, bb=bb: [
                            q.matmul(pcs[s][0][:, 0:512],
                                     ctx_sb[:, 1024 * hc + 128 * s:1024 * hc + 128 * s + 128],
                                     bb[:], start=(hc == 0), stop=False)
                            for s in range(8)],
                            reads=[r_bb, r_ctx], writes=[pcs[s][1] for s in range(8)])
                    for ic in range(32):
                        bb, r_bb = knb[ic % 2]
                        P.op("pool", lambda q, ec=ec, ic=ic, bb=bb: q.dma_start(
                            out=bb[:], in_=fker[128 * ic:128 * ic + 128,
                                                512 * ec:512 * ec + 512]),
                            writes=[r_bb], ndma=1)
                        P.op("pe", lambda q, ic=ic, bb=bb: [
                            q.matmul(pcs[s][0][:, 0:512],
                                     xno_sb[:, 1024 * ic + 128 * s:1024 * ic + 128 * s + 128],
                                     bb[:], start=False, stop=False)
                            for s in range(8)],
                            reads=[r_bb, r_xno2], writes=[pcs[s][1] for s in range(8)])
                    P.op("pe", lambda q, ec=ec: [
                        q.matmul(pcs[s][0][:, 0:512], axt_sb[:, 128 * s:128 * s + 128],
                                 bcomb[s // 2][0][:, 512 * ec:512 * ec + 512],
                                 start=False, stop=False) for s in range(8)],
                        reads=[r_axt] + [bcomb[g][1] for g in range(4)],
                        writes=[pcs[s][1] for s in range(8)])
                    P.op("pe", lambda q, ec=ec: [
                        q.matmul(pcs[s][0][:, 0:512], ones1b[0:1, :],
                                 fbias_sb[0:1, 512 * ec:512 * ec + 512],
                                 start=False, stop=True) for s in range(8)],
                        reads=[r_ones1b, r_fbias],
                        writes=[pcs[s][1] for s in range(8)])
                    for s in range(8):
                        bx, r_bx = xog[s % 2]
                        bo, r_bo = ost[s % 2]
                        P.op("syn", lambda q, s=s, ec=ec, bx=bx: q.dma_start(
                            out=bx[:], in_=xown[128 * s:128 * s + 128,
                                                512 * ec:512 * ec + 512]),
                            writes=[r_bx], ndma=1)
                        P.op("dve", lambda q, s=s, bx=bx, bo=bo: q.tensor_tensor(
                            out=bo[:], in0=pcs[s][0][:, 0:512], in1=bx[:],
                            op=ALU.add), reads=[pcs[s][1], r_bx], writes=[r_bo])
                        P.op("syn", lambda q, s=s, ec=ec, bo=bo: q.dma_start(
                            out=out[128 * s:128 * s + 128, 512 * ec:512 * ec + 512],
                            in_=bo[:]), reads=[r_bo], writes_acc=[r_out], ndma=1)

                P.op("syn", lambda q: q.nop(), reads=[r_out])

        _phase6()
        block = ctx.enter_context(nc.Block())
        P.emit(block)

    return nc


_NC_CACHE = None


def _get_program():
    global _NC_CACHE
    if _NC_CACHE is None:
        _NC_CACHE = build_program()
    return _NC_CACHE


def _host_tables(kcore):
    inv_freq = 1.0 / (10000.0 ** (np.arange(0, D, 2, dtype=np.float32) / D))
    t = np.arange(S, dtype=np.float32)
    freqs = np.outer(t, inv_freq)
    emb = np.concatenate([freqs, freqs], -1)
    cos = np.cos(emb).astype(np.float32)
    sin = np.sin(emb).astype(np.float32)
    cosT = np.ascontiguousarray(cos.T)
    sinT = sin.T
    sinF = sinT.copy()
    sinF[0:64] = -sinF[0:64]
    sinF = np.ascontiguousarray(sinF)
    rows = np.concatenate([np.arange(128 * (4 * s + kcore),
                                     128 * (4 * s + kcore) + 128)
                           for s in range(NSLOT)])
    cosq = np.ascontiguousarray(cosT[:, rows])
    sinq = np.ascontiguousarray(sinF[:, rows])
    pp, jj = np.meshgrid(np.arange(128), np.arange(512), indexing="ij")
    mk = np.where(jj <= 128 * kcore + pp, 0.0, -10000.0).astype(np.float32)
    import ml_dtypes
    return (cosT, sinF, cosq, sinq, rows,
            np.eye(128, dtype=np.float32).astype(ml_dtypes.bfloat16),
            mk.astype(ml_dtypes.bfloat16))


def kernel(**inputs):
    x = np.asarray(inputs["L0_attn_hidden_states"], np.float32)
    Wq = np.ascontiguousarray(np.asarray(inputs["L0_attn_Wq"], np.float32).T)
    Wk = np.ascontiguousarray(np.asarray(inputs["L0_attn_Wk"], np.float32).T)
    Wv = np.ascontiguousarray(np.asarray(inputs["L0_attn_Wv"], np.float32).T)
    Wo = np.ascontiguousarray(np.asarray(inputs["L0_attn_Wo"], np.float32).T)
    fker = np.ascontiguousarray(np.asarray(inputs["L0_ffn_kernel"], np.float32))
    fbias = np.asarray(inputs["L0_ffn_bias"], np.float32).reshape(1, E)
    hp1 = np.ascontiguousarray(np.asarray(inputs["L0_attn_hash_proj1"], np.float32))
    hp2 = np.ascontiguousarray(np.asarray(inputs["L0_attn_hash_proj2"], np.float32))
    hb1 = np.ascontiguousarray(
        np.asarray(inputs["L0_attn_hash_bias1"], np.float32).transpose(0, 2, 1))
    hb2 = np.ascontiguousarray(
        np.asarray(inputs["L0_attn_hash_bias2"], np.float32).transpose(0, 2, 1))
    gate = np.asarray(inputs["L0_ffn_gate"], np.float32)
    lA = np.asarray(inputs["L0_ffn_lora_A"], np.float32)
    lB = np.asarray(inputs["L0_ffn_lora_B"], np.float32)
    lAs = np.asarray(inputs["L0_ffn_lora_A_shared"], np.float32)
    lBs = np.asarray(inputs["L0_ffn_lora_B_shared"], np.float32)

    lat = np.ascontiguousarray(
        lA.transpose(0, 2, 1).reshape(16, 32, 128, 32).transpose(0, 2, 1, 3)
        .reshape(16, 128, 1024))
    last = np.ascontiguousarray(
        lAs.T.reshape(32, 128, 32).transpose(1, 0, 2).reshape(128, 1024))
    lbt = np.ascontiguousarray(lB.transpose(0, 2, 1))
    lbst = np.ascontiguousarray(lBs.T)

    cc = np.broadcast_to(
        np.array([1e9] + [1640.0 - SPL[s] for s in range(1, NSLOT)], np.float32),
        (128, NSLOT)).copy()
    cr = np.broadcast_to(
        np.array([0.0] + [SPL[s] / 2.0 for s in range(1, NSLOT)], np.float32),
        (128, NSLOT)).copy()

    nc = _get_program()
    in_maps = []
    rows_all = []
    for c in range(8):
        b, k = c // 4, c % 4
        cosT, sinF, cosq, sinq, rows, identb, maskc = _host_tables(k)
        rows_all.append(rows)
        xb = np.ascontiguousarray(x[b])
        in_maps.append({
            "xfull": xb,
            "xown": np.ascontiguousarray(xb[rows]),
            "wqt": Wq, "wkt": Wk, "wvt": Wv, "wot": Wo,
            "fker": fker, "fbias": fbias,
            "hp1": hp1, "hb1": hb1, "hp2": hp2, "hb2": hb2,
            "lat": lat, "last": last, "lbt": lbt, "lbst": lbst,
            "gateown": np.ascontiguousarray(
                gate[4 * b:4 * b + 4].reshape(1, 64)),
            "coskf": np.ascontiguousarray(cosT),
            "sinkf": np.ascontiguousarray(sinF),
            "cosqf": cosq, "sinqf": sinq,
            "identb": identb,
            "identf": np.eye(128, dtype=np.float32), "maskc": maskc,
            "cc1640": cc, "cr820": cr,
        })
    res = run_bass_kernel_spmd(nc, in_maps, core_ids=list(range(8)))
    outf = np.empty((B, S, E), np.float32)
    for c in range(8):
        outf[c // 4][rows_all[c]] = res.results[c]["out"]
    return outf


# version 13
# speedup vs baseline: 1.1869x; 1.1869x over previous
import numpy as np
from contextlib import ExitStack

import concourse.bass as bass
import concourse.mybir as mybir
from concourse.bass_utils import run_bass_kernel_spmd

F32 = mybir.dt.float32
BF16 = mybir.dt.bfloat16
F16 = mybir.dt.float16
AF = mybir.ActivationFunctionType
ALU = mybir.AluOpType
AXX = mybir.AxisListType.X

B, S, E, H, KVH, D = 2, 4096, 4096, 32, 8, 128
GRP = H // KVH
NSLOT = 8
EXT = [512 * (s + 1) for s in range(NSLOT)]
OFF = [0]
for _e in EXT:
    OFF.append(OFF[-1] + _e)
TOT = OFF[-1]
NUM_REMAIN = 820
SQD = float(np.sqrt(D))
STEPS = [128.0, 64.0, 32.0, 16.0, 8.0, 4.0, 2.0]
SPL = [0, 496, 736, 992, 1232, 1472, 1728, 1968]


class Res:
    __slots__ = ("writers", "readers")

    def __init__(self):
        self.writers = []
        self.readers = []


class Prog:
    ENGS = ("pe", "dve", "act", "pool", "syn")
    PARANOID = False

    def __init__(self, nc, ctx):
        self.nc = nc
        self.sem = {e: ctx.enter_context(nc.semaphore("sem_" + e)) for e in self.ENGS}
        self.cnt = {e: 0 for e in self.ENGS}
        self.sched = {e: [] for e in self.ENGS}
        self.waited = {e: {} for e in self.ENGS}
        self.barrier = []
        self.pad_sb = ctx.enter_context(nc.sbuf_tensor("pe_pad", [128, 2048], BF16))

    def op(self, eng, fn, reads=(), writes=(), writes_acc=(), ndma=0):
        deps = {}

        def add(tok):
            if tok is None:
                return
            e2, v2 = tok
            if e2 == eng and eng == "pe":
                return
            if deps.get(e2, 0) < v2:
                deps[e2] = v2

        for t in self.barrier:
            add(t)
        if Prog.PARANOID:
            for e2 in self.ENGS:
                if self.cnt[e2] > 0 and e2 != eng:
                    add((e2, self.cnt[e2]))
        for r in reads:
            for t in r.writers:
                add(t)
        for w in writes:
            for t in w.writers:
                add(t)
            for t in w.readers:
                add(t)
        for w in writes_acc:
            for t in w.readers:
                add(t)
        waits = []
        wt = self.waited[eng]
        for e2 in self.ENGS:
            v2 = deps.get(e2)
            if v2 is None:
                continue
            if wt.get(e2, -1) < v2:
                waits.append((e2, v2))
                wt[e2] = v2
        if eng in ("dve", "act") and any(e2 == "pe" for e2, _ in waits):
            pad = self.pad_sb
            inner = fn

            def fn(q, _inner=inner, _pad=pad, _eng=eng):
                if _eng == "dve":
                    p = q.tensor_copy(_pad[:, 0:512], _pad[:, 512:1024])
                else:
                    p = q.copy(out=_pad[:, 1024:1536], in_=_pad[:, 1536:2048])
                ins = _inner(q)
                if not isinstance(ins, (list, tuple)):
                    ins = [ins]
                return [p] + list(ins)
        inc = 16 * ndma if ndma else 1
        self.cnt[eng] += inc
        self.sched[eng].append((waits, fn, bool(ndma)))
        tok = (eng, self.cnt[eng])
        for r in reads:
            r.readers.append(tok)
        for w in writes:
            w.writers = [tok]
            w.readers = []
        for w in writes_acc:
            w.writers.append(tok)
        return tok

    def phase_barrier(self):
        self.barrier = [(e, self.cnt[e]) for e in self.ENGS if self.cnt[e] > 0]

    def emit(self, block):
        sections = {"pe": block.tensor, "dve": block.vector, "act": block.scalar,
                    "pool": block.gpsimd, "syn": block.sync}

        def mk(eng):
            sched = self.sched[eng]
            sems = self.sem

            def emit_stream(q):
                for waits, fn, is_dma in sched:
                    for e2, v2 in waits:
                        q.wait_ge(sems[e2], v2)
                    instrs = fn(q)
                    if not isinstance(instrs, (list, tuple)):
                        instrs = [instrs]
                    if is_dma:
                        for ins in instrs:
                            ins.then_inc(sems[eng], 16)
                    else:
                        instrs[-1].then_inc(sems[eng], 1)
            return emit_stream

        for eng, dec in sections.items():
            dec(mk(eng))


def build_program(n_heads=H, debug=False):
    nc = bass.Bass("TRN2", target_bir_lowering=False)

    def din(name, shape, dt=F32):
        return nc.dram_tensor(name, list(shape), dt, kind="ExternalInput").ap()

    def dscr(name, shape, dt):
        return nc.dram_tensor(name, list(shape), dt, kind="Internal").ap()

    xfull = din("xfull", (S, E))
    xown = din("xown", (1024, E))
    wqt = din("wqt", (E, E))
    wkt = din("wkt", (E, KVH * D))
    wvt = din("wvt", (E, KVH * D))
    wot = din("wot", (E, E))
    fker = din("fker", (E, E))
    fbias = din("fbias", (1, E))
    hp1 = din("hp1", (H, D, D))
    hb1 = din("hb1", (H, D, 1))
    hp2 = din("hp2", (H, D, D))
    hb2 = din("hb2", (H, D, 1))
    lat = din("lat", (16, 128, 1024))
    last = din("last", (128, 1024))
    lbt = din("lbt", (16, 32, E))
    lbst = din("lbst", (32, E))
    gateown = din("gateown", (1, 64))
    coskf = din("coskf", (D, S))
    sinkf = din("sinkf", (D, S))
    cosqf = din("cosqf", (D, 1024))
    sinqf = din("sinqf", (D, 1024))
    identb = din("identb", (128, 128), BF16)
    identf = din("identf", (128, 128))
    maskc = din("maskc", (128, 512), BF16)
    cc1640 = din("cc1640", (128, NSLOT))
    cr820 = din("cr820", (128, NSLOT))

    out = nc.dram_tensor("out", [1024, E], F32, kind="ExternalOutput").ap()
    if debug:
        dbg = {
            "d_xn": nc.dram_tensor("d_xn", [32, 128, S], BF16, kind="ExternalOutput").ap(),
            "d_kr": nc.dram_tensor("d_kr", [KVH, 128, S], BF16, kind="ExternalOutput").ap(),
            "d_v": nc.dram_tensor("d_v", [KVH, 128, 32, 128], BF16, kind="ExternalOutput").ap(),
            "d_qr": nc.dram_tensor("d_qr", [32, 128, 1024], BF16, kind="ExternalOutput").ap(),
            "d_qn2": nc.dram_tensor("d_qn2", [32, 1024], F32, kind="ExternalOutput").ap(),
            "d_ctx": nc.dram_tensor("d_ctx", [H, 128, 1024], BF16, kind="ExternalOutput").ap(),
            "d_negb": nc.dram_tensor("d_negb", [128, NSLOT * H], F32, kind="ExternalOutput").ap(),
            "d_sim": nc.dram_tensor("d_sim", [128, TOT], BF16, kind="ExternalOutput").ap(),
            "d_book": nc.dram_tensor("d_book", [128, 2 * NSLOT], F32, kind="ExternalOutput").ap(),
            "d_sel": nc.dram_tensor("d_sel", [128, 4096], BF16, kind="ExternalOutput").ap(),
            "d_p": nc.dram_tensor("d_p", [128, 4096], BF16, kind="ExternalOutput").ap(),
            "d_xo": nc.dram_tensor("d_xo", [32, 128, 1024], BF16, kind="ExternalOutput").ap(),
            "d_cosk": nc.dram_tensor("d_cosk", [128, S], F32, kind="ExternalOutput").ap(),
            "d_sink": nc.dram_tensor("d_sink", [128, S], F32, kind="ExternalOutput").ap(),
            "d_cosq": nc.dram_tensor("d_cosq", [128, 1024], F32, kind="ExternalOutput").ap(),
            "d_qn2r": nc.dram_tensor("d_qn2r", [1, 1024], F32, kind="ExternalOutput").ap(),
            "d_acc": nc.dram_tensor("d_acc", [128, 2 * NSLOT], F32, kind="ExternalOutput").ap(),
            "d_acomb": nc.dram_tensor("d_acomb", [4, 128, 1024], BF16, kind="ExternalOutput").ap(),
            "d_bcomb": nc.dram_tensor("d_bcomb", [4, 64, E], BF16, kind="ExternalOutput").ap(),
            "d_axt": nc.dram_tensor("d_axt", [64, 1024], BF16, kind="ExternalOutput").ap(),
            "d_lastb": nc.dram_tensor("d_lastb", [128, 1024], BF16, kind="ExternalOutput").ap(),
            "d_gbc": nc.dram_tensor("d_gbc", [128, 64], F32, kind="ExternalOutput").ap(),
        }

    xnsp = dscr("xnsp", (32, 128, S), BF16)
    xosp = dscr("xosp", (32, 128, 1024), BF16)
    krsp = dscr("krsp", (KVH, 128, S), BF16)
    vsp = dscr("vsp", (KVH, 128, 32, 128), BF16)
    qrsp = dscr("qrsp", (32, 128, 1024), BF16)
    ctxsp = dscr("ctxsp", (H, 128, 1024), BF16)
    qn2sp = dscr("qn2sp", (32, 1024), F32)

    ctx = ExitStack()
    with ctx:
        P = Prog(nc, ctx)

        def sb(name, shape, dt):
            t = ctx.enter_context(nc.sbuf_tensor(name, list(shape), dt))
            return t, Res()

        zero_c, r_zero = sb("zero_c", (128, 1), F32)
        onescol, r_onescol = sb("onescol", (128, 1), BF16)
        ones1b, r_ones1b = sb("ones1b", (1, 128), BF16)
        ones1F, r_ones1F = sb("ones1F", (1, 128), F32)
        idb_sb, r_idb = sb("idb_sb", (128, 128), BF16)
        idf_sb, r_idf = sb("idf_sb", (128, 128), F32)
        maskc_sb, r_maskc = sb("maskc_sb", (128, 512), BF16)
        cc_sb, r_cc = sb("cc_sb", (128, NSLOT), F32)
        cr_sb, r_cr = sb("cr_sb", (128, NSLOT), F32)
        kmax_acc, r_kmax = sb("kmax_acc", (1, 1), F32)
        negkb, r_negkb = sb("negkb", (128, 1), F32)
        negb_sb, r_negb = sb("negb_sb", (128, NSLOT * H), F32)

        P.op("pool", lambda q: [q.memset(zero_c[:], 0.0),
                                q.memset(onescol[:], 1.0),
                                q.memset(ones1b[:], 1.0),
                                q.memset(ones1F[:], 1.0),
                                q.memset(kmax_acc[:], 0.0)],
             writes=[r_zero, r_onescol, r_ones1b, r_ones1F, r_kmax])
        P.op("syn", lambda q: [q.dma_start(out=idb_sb[:], in_=identb[:, :]),
                               q.dma_start(out=idf_sb[:], in_=identf[:, :]),
                               q.dma_start(out=maskc_sb[:], in_=maskc[:, :]),
                               q.dma_start(out=cc_sb[:], in_=cc1640[:, :]),
                               q.dma_start(out=cr_sb[:], in_=cr820[:, :])],
             writes=[r_idb, r_idf, r_maskc, r_cc, r_cr], ndma=5)

        r_xnsp, r_xosp, r_krsp, r_vsp, r_qrsp, r_ctxsp, r_qn2sp = (
            Res(), Res(), Res(), Res(), Res(), Res(), Res())
        r_xnsp_tc = [Res() for _ in range(32)]
        r_out = Res()

        def emit_ln(bx, r_bx, bxn, r_bxn, st48, r_st48, mv, r_mv, rstd, r_rstd,
                    nmr, r_nmr):
            P.op("dve", lambda q: [
                q.bn_stats(st48[:, 6 * i:6 * i + 6], bx[:, 512 * i:512 * i + 512])
                for i in range(8)], reads=[r_bx], writes=[r_st48])
            P.op("dve", lambda q: q.bn_aggr(mv[:], st48[:]),
                 reads=[r_st48], writes=[r_mv])
            P.op("dve", lambda q: q.tensor_scalar(
                out=mv[:, 1:2], in0=mv[:, 1:2], scalar1=1e-5, scalar2=None,
                op0=ALU.add), writes=[r_mv])
            P.op("act", lambda q: q.activation(
                out=rstd[:], in_=mv[:, 1:2], func=AF.Sqrt, bias=zero_c[:],
                scale=1.0), reads=[r_mv, r_zero], writes=[r_rstd])
            P.op("dve", lambda q: q.reciprocal(rstd[:], rstd[:]), writes=[r_rstd])
            P.op("dve", lambda q: q.tensor_tensor(
                out=nmr[:], in0=mv[:, 0:1], in1=rstd[:], op=ALU.mult),
                reads=[r_mv, r_rstd], writes=[r_nmr])
            P.op("dve", lambda q: q.tensor_scalar(
                out=nmr[:], in0=nmr[:], scalar1=-1.0, scalar2=None, op0=ALU.mult),
                writes=[r_nmr])
            P.op("act", lambda q: q.activation(
                out=bxn[:], in_=bx[:], func=AF.Identity, bias=nmr[:], scale=rstd[:]),
                reads=[r_bx, r_nmr, r_rstd], writes=[r_bxn])

        def _phase1():
            with ExitStack() as ph:
                def sbl(name, shape, dt):
                    t = ph.enter_context(nc.sbuf_tensor(name, list(shape), dt))
                    return t, Res()

                def psl(name, shape, dt):
                    t = ph.enter_context(nc.psum_tensor(name, list(shape), dt))
                    return t, Res()

                xt = [sbl("a0x%d" % i, (128, E), F32) for i in range(2)]
                xnt = [sbl("a0n%d" % i, (128, E), BF16) for i in range(2)]
                xo = [sbl("a0o%d" % i, (128, 4096), BF16) for i in range(2)]
                st48, r_st48 = sbl("a0s", (128, 48), F32)
                mv, r_mv = sbl("a0m", (128, 2), F32)
                rstd, r_rstd = sbl("a0r", (128, 1), F32)
                nmr, r_nmr = sbl("a0nm", (128, 1), F32)
                pst, r_pst = psl("a0p", (128, 1024), BF16)

                for tc in range(32):
                    bx, r_bx = xt[tc % 2]
                    bxn, r_bxn = xnt[tc % 2]
                    bo, r_bo = xo[tc % 2]
                    P.op("syn", lambda q, tc=tc, bx=bx: q.dma_start(
                        out=bx[:], in_=xfull[128 * tc:128 * tc + 128, :]),
                        writes=[r_bx], ndma=1)
                    emit_ln(bx, r_bx, bxn, r_bxn, st48, r_st48, mv, r_mv,
                            rstd, r_rstd, nmr, r_nmr)
                    for g in range(4):
                        P.op("pe", lambda q, bxn=bxn, g=g: [
                            q.transpose(pst[:, 128 * j:128 * j + 128],
                                        bxn[:, 128 * (8 * g + j):128 * (8 * g + j) + 128],
                                        idb_sb[:, :]) for j in range(8)],
                            reads=[r_bxn, r_idb], writes=[r_pst])
                        P.op("dve", lambda q, bo=bo, g=g: q.tensor_copy(
                            bo[:, 1024 * g:1024 * g + 1024], pst[:]),
                            reads=[r_pst], writes_acc=[r_bo])
                    P.op("pool", lambda q, bo=bo, tc=tc: [
                        q.dma_start(out=xnsp[ec, :, 128 * tc:128 * tc + 128],
                                    in_=bo[:, 128 * ec:128 * ec + 128])
                        for ec in range(32)], reads=[r_bo],
                        writes_acc=[r_xnsp, r_xnsp_tc[tc]], ndma=32)

        _phase1()
        P.phase_barrier()

        def _phase2():
            with ExitStack() as ph:
                def sbl(name, shape, dt):
                    t = ph.enter_context(nc.sbuf_tensor(name, list(shape), dt))
                    return t, Res()

                def psl(name, shape, dt):
                    t = ph.enter_context(nc.psum_tensor(name, list(shape), dt))
                    return t, Res()

                wk_sb, r_wk = sbl("a1w", (128, 32 * 1024), BF16)
                xnu = [sbl("a1x%d" % i, (128, 32 * 512), BF16) for i in range(2)]
                cosk_sb, r_cosk = sbl("a1c", (128, S), F32)
                sink_sb, r_sink = sbl("a1s", (128, S), F32)
                tmpc, r_tmpc = sbl("a1tc", (128, 512), F32)
                tmps, r_tmps = sbl("a1ts", (128, 512), F32)
                krbf, r_krbf = sbl("a1kb", (128, 512), BF16)
                kr2, r_kr2 = sbl("a1k2", (128, 512), BF16)
                kn1, r_kn1 = sbl("a1kn", (1, 1), F32)
                psk = [psl("a1p%d" % i, (128, 512), F32) for i in range(2)]
                psm, r_psm = psl("a1m", (128, 512), F32)

                P.op("syn", lambda q: [q.dma_start(out=cosk_sb[:], in_=coskf[:, :]),
                                       q.dma_start(out=sink_sb[:], in_=sinkf[:, :])],
                     writes=[r_cosk, r_sink], ndma=2)
                for j in range(8):
                    P.op("pool", lambda q, j=j: q.dma_start(
                        out=wk_sb[:].rearrange("p (ec o) -> p ec o", ec=32)[
                            :, :, 128 * j:128 * j + 128],
                        in_=wkt[:, 128 * j:128 * j + 128].rearrange(
                            "(ec p) o -> p ec o", p=128)),
                        writes_acc=[r_wk], ndma=1)
                for u in range(8):
                    bx, r_bx = xnu[u % 2]
                    P.op("syn", lambda q, u=u, bx=bx: [
                        q.dma_start(out=bx[:, 512 * ec:512 * ec + 512],
                                    in_=xnsp[ec, :, 512 * u:512 * u + 512])
                        for ec in range(32)],
                        reads=[r_xnsp_tc[t] for t in range(4 * u, 4 * u + 4)],
                        writes=[r_bx], ndma=32)
                    for kvh in range(KVH):
                        pk, r_pk = psk[kvh % 2]
                        P.op("pe", lambda q, bx=bx, kvh=kvh, pk=pk: [
                            q.matmul(pk[:, 0:512],
                                     wk_sb[:, 1024 * ec + 128 * kvh:1024 * ec + 128 * kvh + 128],
                                     bx[:, 512 * ec:512 * ec + 512],
                                     start=(ec == 0), stop=(ec == 31))
                            for ec in range(32)], reads=[r_wk, r_bx], writes=[r_pk])
                        P.op("dve", lambda q, pk=pk, u=u: q.tensor_tensor(
                            out=tmpc[:], in0=pk[:, 0:512],
                            in1=cosk_sb[:, 512 * u:512 * u + 512], op=ALU.mult),
                            reads=[r_pk, r_cosk], writes=[r_tmpc])
                        P.op("dve", lambda q, pk=pk, u=u: [
                            q.tensor_tensor(out=tmps[0:64, :], in0=pk[64:128, 0:512],
                                            in1=sink_sb[0:64, 512 * u:512 * u + 512],
                                            op=ALU.mult),
                            q.tensor_tensor(out=tmps[64:128, :], in0=pk[0:64, 0:512],
                                            in1=sink_sb[64:128, 512 * u:512 * u + 512],
                                            op=ALU.mult)],
                            reads=[r_pk, r_sink], writes=[r_tmps])
                        P.op("dve", lambda q: q.tensor_tensor(
                            out=krbf[:], in0=tmpc[:], in1=tmps[:], op=ALU.add),
                            reads=[r_tmpc, r_tmps], writes=[r_krbf])
                        P.op("pool", lambda q, kvh=kvh, u=u: q.dma_start(
                            out=krsp[kvh, :, 512 * u:512 * u + 512], in_=krbf[:]),
                            reads=[r_krbf], writes_acc=[r_krsp], ndma=1)
                        P.op("act", lambda q: q.activation(
                            out=kr2[:], in_=krbf[:], func=AF.Square, bias=zero_c[:],
                            scale=1.0), reads=[r_krbf, r_zero], writes=[r_kr2])
                        P.op("pe", lambda q: q.matmul(
                            psm[0:1, 0:512], onescol[:, :], kr2[:], start=True,
                            stop=True), reads=[r_onescol, r_kr2], writes=[r_psm])
                        P.op("dve", lambda q: q.tensor_reduce(
                            out=kn1[:], in_=psm[0:1, 0:512], axis=AXX, op=ALU.max),
                            reads=[r_psm], writes=[r_kn1])
                        P.op("dve", lambda q: q.tensor_tensor(
                            out=kmax_acc[:], in0=kmax_acc[:], in1=kn1[:], op=ALU.max),
                            reads=[r_kn1], writes=[r_kmax])
                if debug:
                    P.op("syn", lambda q: [
                        q.dma_start(out=dbg["d_cosk"][:, :], in_=cosk_sb[:]),
                        q.dma_start(out=dbg["d_sink"][:, :], in_=sink_sb[:])],
                        reads=[r_cosk, r_sink], ndma=2)

        _phase2()
        P.phase_barrier()

        def _phase3():
            with ExitStack() as ph:
                def sbl(name, shape, dt):
                    t = ph.enter_context(nc.sbuf_tensor(name, list(shape), dt))
                    return t, Res()

                def psl(name, shape, dt):
                    t = ph.enter_context(nc.psum_tensor(name, list(shape), dt))
                    return t, Res()

                wv_sb, r_wv = sbl("a2w", (128, 32 * 1024), BF16)
                xnu = [sbl("a2x%d" % i, (128, 32 * 512), BF16) for i in range(2)]
                vbf = [sbl("a2v%d" % i, (128, 1024), BF16) for i in range(2)]
                psvs = [psl("a2p%d" % i, (128, 1024), F32) for i in range(2)]

                for j in range(8):
                    P.op("pool", lambda q, j=j: q.dma_start(
                        out=wv_sb[:].rearrange("p (ec o) -> p ec o", ec=32)[
                            :, :, 128 * j:128 * j + 128],
                        in_=wvt[:, 128 * j:128 * j + 128].rearrange(
                            "(ec p) o -> p ec o", p=128)),
                        writes_acc=[r_wv], ndma=1)
                for u in range(8):
                    bx, r_bx = xnu[u % 2]
                    P.op("syn", lambda q, u=u, bx=bx: [
                        q.dma_start(out=bx[:, 512 * ec:512 * ec + 512],
                                    in_=xnsp[ec, :, 512 * u:512 * u + 512])
                        for ec in range(32)],
                        reads=[r_xnsp_tc[t] for t in range(4 * u, 4 * u + 4)],
                        writes=[r_bx], ndma=32)
                    for tsub in range(4):
                        bv, r_bv = vbf[tsub % 2]
                        psv, r_psv = psvs[tsub % 2]
                        P.op("pe", lambda q, bx=bx, tsub=tsub, psv=psv: [
                            q.matmul(psv[:, 512 * oh:512 * oh + 512],
                                     bx[:, 512 * ec + 128 * tsub:512 * ec + 128 * tsub + 128],
                                     wv_sb[:, 1024 * ec + 512 * oh:1024 * ec + 512 * oh + 512],
                                     start=(ec == 0), stop=(ec == 31))
                            for oh in range(2) for ec in range(32)],
                            reads=[r_wv, r_bx], writes=[r_psv])
                        P.op("act", lambda q, bv=bv, psv=psv: q.copy(out=bv[:], in_=psv[:]),
                             reads=[r_psv], writes=[r_bv])
                        P.op("pool", lambda q, bv=bv, u=u, tsub=tsub: [
                            q.dma_start(out=vsp[kvh, :, 4 * u + tsub, :],
                                        in_=bv[:, 128 * kvh:128 * kvh + 128])
                            for kvh in range(KVH)],
                            reads=[r_bv], writes_acc=[r_vsp], ndma=KVH)

        _phase3()
        P.phase_barrier()

        def _phase4():
            with ExitStack() as ph:
                def sbl(name, shape, dt):
                    t = ph.enter_context(nc.sbuf_tensor(name, list(shape), dt))
                    return t, Res()

                def psl(name, shape, dt):
                    t = ph.enter_context(nc.psum_tensor(name, list(shape), dt))
                    return t, Res()

                xno_sb, r_xno = sbl("a3xo", (128, 32 * 1024), BF16)
                xt = [sbl("a3x%d" % i, (128, E), F32) for i in range(2)]
                xnt = [sbl("a3n%d" % i, (128, E), BF16) for i in range(2)]
                st48, r_st48 = sbl("a3s", (128, 48), F32)
                mv, r_mv = sbl("a3m", (128, 2), F32)
                rstd, r_rstd = sbl("a3r", (128, 1), F32)
                nmr, r_nmr = sbl("a3nm", (128, 1), F32)
                pst, r_pst = psl("a3p", (128, 1024), BF16)

                for sl in range(8):
                    bx, r_bx = xt[sl % 2]
                    bxn, r_bxn = xnt[sl % 2]
                    P.op("syn", lambda q, sl=sl, bx=bx: q.dma_start(
                        out=bx[:], in_=xown[128 * sl:128 * sl + 128, :]),
                        writes=[r_bx], ndma=1)
                    emit_ln(bx, r_bx, bxn, r_bxn, st48, r_st48, mv, r_mv,
                            rstd, r_rstd, nmr, r_nmr)
                    for g in range(4):
                        P.op("pe", lambda q, bxn=bxn, g=g: [
                            q.transpose(pst[:, 128 * j:128 * j + 128],
                                        bxn[:, 128 * (8 * g + j):128 * (8 * g + j) + 128],
                                        idb_sb[:, :]) for j in range(8)],
                            reads=[r_bxn, r_idb], writes=[r_pst])
                        P.op("dve", lambda q, g=g, sl=sl: [
                            q.tensor_copy(
                                xno_sb[:, 1024 * (8 * g + j) + 128 * sl:
                                       1024 * (8 * g + j) + 128 * sl + 128],
                                pst[:, 128 * j:128 * j + 128]) for j in range(8)],
                            reads=[r_pst], writes_acc=[r_xno])
                P.op("pool", lambda q: [
                    q.dma_start(out=xosp[ec, :, :],
                                in_=xno_sb[:, 1024 * ec:1024 * ec + 1024])
                    for ec in range(32)], reads=[r_xno], writes_acc=[r_xosp], ndma=32)

                wqb = [sbl("a3wb%d" % i, (128, 4096), BF16) for i in range(2)]
                cosq_sb, r_cosq = sbl("a3c", (128, 1024), F32)
                sinq_sb, r_sinq = sbl("a3si", (128, 1024), F32)
                tmpc, r_tmpc = sbl("a3tc", (128, 512), F32)
                tmps, r_tmps = sbl("a3ts", (128, 512), F32)
                qrbf, r_qrbf = sbl("a3qb", (128, 512), BF16)
                qr2, r_qr2 = sbl("a3q2", (128, 512), BF16)
                qn2row, r_qn2row = sbl("a3qn", (1, 1024), F32)
                psq = [psl("a3pq%d" % i, (128, 512), F32) for i in range(2)]
                psm, r_psm = psl("a3pm", (128, 512), F32)

                P.op("syn", lambda q: [q.dma_start(out=cosq_sb[:], in_=cosqf[:, :]),
                                       q.dma_start(out=sinq_sb[:], in_=sinqf[:, :])],
                     writes=[r_cosq, r_sinq], ndma=2)
                for oc in range(32):
                    bb, r_bb = wqb[oc % 2]
                    P.op("pool", lambda q, oc=oc, bb=bb: [
                        q.dma_start(out=bb[:, 1024 * jj:1024 * jj + 1024].rearrange(
                                        "p (ec o) -> p ec o", ec=8),
                                    in_=wqt[1024 * jj:1024 * jj + 1024,
                                            128 * oc:128 * oc + 128].rearrange(
                                        "(ec p) o -> p ec o", p=128))
                        for jj in range(4)], writes=[r_bb], ndma=4)
                    for th in range(2):
                        pq, r_pq = psq[th]
                        P.op("pe", lambda q, bb=bb, th=th, pq=pq: [
                            q.matmul(pq[:, 0:512],
                                     bb[:, 128 * ec:128 * ec + 128],
                                     xno_sb[:, 1024 * ec + 512 * th:1024 * ec + 512 * th + 512],
                                     start=(ec == 0), stop=(ec == 31))
                            for ec in range(32)], reads=[r_bb, r_xno], writes=[r_pq])
                        P.op("dve", lambda q, pq=pq, th=th: q.tensor_tensor(
                            out=tmpc[:], in0=pq[:, 0:512],
                            in1=cosq_sb[:, 512 * th:512 * th + 512], op=ALU.mult),
                            reads=[r_pq, r_cosq], writes=[r_tmpc])
                        P.op("dve", lambda q, pq=pq, th=th: [
                            q.tensor_tensor(out=tmps[0:64, :], in0=pq[64:128, 0:512],
                                            in1=sinq_sb[0:64, 512 * th:512 * th + 512],
                                            op=ALU.mult),
                            q.tensor_tensor(out=tmps[64:128, :], in0=pq[0:64, 0:512],
                                            in1=sinq_sb[64:128, 512 * th:512 * th + 512],
                                            op=ALU.mult)],
                            reads=[r_pq, r_sinq], writes=[r_tmps])
                        P.op("dve", lambda q: q.tensor_tensor(
                            out=qrbf[:], in0=tmpc[:], in1=tmps[:], op=ALU.add),
                            reads=[r_tmpc, r_tmps], writes=[r_qrbf])
                        P.op("pool", lambda q, oc=oc, th=th: q.dma_start(
                            out=qrsp[oc, :, 512 * th:512 * th + 512], in_=qrbf[:]),
                            reads=[r_qrbf], writes_acc=[r_qrsp], ndma=1)
                        P.op("act", lambda q: q.activation(
                            out=qr2[:], in_=qrbf[:], func=AF.Square, bias=zero_c[:],
                            scale=1.0), reads=[r_qrbf, r_zero], writes=[r_qr2])
                        P.op("pe", lambda q, th=th: q.matmul(
                            psm[0:1, 0:512], onescol[:, :], qr2[:], start=True,
                            stop=True), reads=[r_onescol, r_qr2], writes=[r_psm])
                        P.op("dve", lambda q, th=th: q.tensor_copy(
                            qn2row[0:1, 512 * th:512 * th + 512], psm[0:1, 0:512]),
                            reads=[r_psm], writes_acc=[r_qn2row])
                    P.op("pool", lambda q, oc=oc: q.dma_start(
                        out=qn2sp[oc:oc + 1, :], in_=qn2row[:]),
                        reads=[r_qn2row], writes_acc=[r_qn2sp], ndma=1)

                if debug:
                    P.op("syn", lambda q: [
                        q.dma_start(out=dbg["d_cosq"][:, :], in_=cosq_sb[:]),
                        q.dma_start(out=dbg["d_qn2r"][:, :], in_=qn2row[:])],
                        reads=[r_cosq, r_qn2row], ndma=2)

                psb, r_psb = psl("a3pb", (128, 512), F32)
                sqk, r_sqk = sbl("a3sq", (1, 1), F32)
                qnall, r_qnall = sbl("a3qa", (32, 1024), F32)
                sqn, r_sqn = sbl("a3sn", (128, 32), F32)

                P.op("act", lambda q: q.activation(
                    out=sqk[:], in_=kmax_acc[:], func=AF.Sqrt, bias=zero_c[0:1, :],
                    scale=1.0), reads=[r_kmax, r_zero], writes=[r_sqk])
                P.op("dve", lambda q: q.tensor_scalar(
                    out=sqk[:], in0=sqk[:], scalar1=-1.001 / SQD, scalar2=None,
                    op0=ALU.mult), writes=[r_sqk])
                P.op("pe", lambda q: q.matmul(psb[:, 64:65], ones1F[0:1, :],
                                              sqk[0:1, :], start=True, stop=True),
                     reads=[r_ones1F, r_sqk], writes=[r_psb])
                P.op("dve", lambda q: q.tensor_copy(negkb[:], psb[:, 64:65]),
                     reads=[r_psb], writes=[r_negkb])
                P.op("syn", lambda q: q.dma_start(out=qnall[:], in_=qn2sp[:, :]),
                     reads=[r_qn2sp], writes=[r_qnall], ndma=1)
                for s in range(NSLOT):
                    P.op("pe", lambda q, s=s: q.transpose(
                        psb[:, 96:128], qnall[0:32, 128 * s:128 * s + 128],
                        idf_sb[0:32, 0:32]),
                        reads=[r_qnall, r_idf], writes=[r_psb])
                    P.op("act", lambda q, s=s: q.activation(
                        out=sqn[:], in_=psb[:, 96:128], func=AF.Sqrt,
                        bias=zero_c[:], scale=1.0),
                        reads=[r_psb, r_zero], writes=[r_sqn])
                    P.op("dve", lambda q, s=s: q.tensor_scalar(
                        out=negb_sb[:, H * s:H * s + 32], in0=sqn[:],
                        scalar1=negkb[:], scalar2=None, op0=ALU.mult),
                        reads=[r_sqn, r_negkb], writes_acc=[r_negb])

        _phase4()
        P.phase_barrier()
        if debug:
            P.op("syn", lambda q: [
                q.dma_start(out=dbg["d_xn"][:, :, :], in_=xnsp[:, :, :]),
                q.dma_start(out=dbg["d_kr"][:, :, :], in_=krsp[:, :, :]),
                q.dma_start(out=dbg["d_v"][:, :, :, :], in_=vsp[:, :, :, :]),
                q.dma_start(out=dbg["d_qr"][:, :, :], in_=qrsp[:, :, :]),
                q.dma_start(out=dbg["d_qn2"][:, :], in_=qn2sp[:, :]),
                q.dma_start(out=dbg["d_xo"][:, :, :], in_=xosp[:, :, :]),
                q.dma_start(out=dbg["d_negb"][:, :], in_=negb_sb[:])],
                reads=[r_xnsp, r_krsp, r_vsp, r_qrsp, r_qn2sp, r_negb, r_xosp],
                ndma=7)

        def _phase5():
            with ExitStack() as ph:
                def sbl(name, shape, dt):
                    t = ph.enter_context(nc.sbuf_tensor(name, list(shape), dt))
                    return t, Res()

                def psl(name, shape, dt):
                    t = ph.enter_context(nc.psum_tensor(name, list(shape), dt))
                    return t, Res()

                sim_sb, r_sim = sbl("bsim", (128, TOT), BF16)
                kr_sb0, r_kr0 = sbl("bkr0", (128, S), BF16)
                kr_sb1, r_kr1 = sbl("bkr1", (128, S), BF16)
                v_sb0, r_v0 = sbl("bv0", (128, S), BF16)
                v_sb1, r_v1 = sbl("bv1", (128, S), BF16)
                KRV = [((kr_sb0, r_kr0), (v_sb0, r_v0)),
                       ((kr_sb1, r_kr1), (v_sb1, r_v1))]
                qr_sb, r_qr = sbl("bqr", (128, 1024), BF16)
                hw1b, r_hw1b = sbl("bh1b", (128, 128), BF16)
                hw2b, r_hw2b = sbl("bh2b", (128, 128), BF16)
                hb1_sb, r_hb1 = sbl("bhb1", (128, 1), F32)
                hb2_sb, r_hb2 = sbl("bhb2", (128, 1), F32)
                y1p_sb, r_y1p = sbl("by1p", (128, 1024), BF16)
                zp_sb, r_zp = sbl("bzp", (128, 1024), F32)
                y_sb, r_y = sbl("by", (128, 5120), BF16)
                hq_sb, r_hq = sbl("bhq", (128, 5120), BF16)
                p0_sb, r_p0 = sbl("bp0", (128, 4096), BF16)
                ps_sb, r_ps = sbl("bps", (128, 4096), F16)
                sel_sb, r_sel = sbl("bsel", (128, 4096), BF16)
                p_sb0, r_p0r = sbl("bpA0", (128, 4096), BF16)
                p_sb1, r_p1r = sbl("bpA1", (128, 4096), BF16)
                PBUF = [(p_sb0, r_p0r), (p_sb1, r_p1r)]
                pt_sb, r_pt = sbl("bpt", (128, 4096), BF16)
                rr_sb, r_rr = sbl("brr", (128, 4096), F16)
                trash_a = sel_sb
                trash_d = ps_sb
                loT, r_lo = sbl("blo", (128, NSLOT), F32)
                midT, r_mid = sbl("bmid", (128, NSLOT), F32)
                ngm, r_ngm = sbl("bngm", (128, NSLOT), F32)
                accA, r_accA = sbl("baca", (128, NSLOT), F32)
                accD, r_accD = sbl("bacd", (128, NSLOT), F32)
                t2, r_t2 = sbl("bt2", (128, NSLOT), F32)
                geT, r_ge = sbl("bge", (128, NSLOT), F32)
                geb, r_geb = sbl("bgeb", (128, NSLOT), F32)
                cntT, r_cnt = sbl("bcnt", (128, NSLOT), F32)
                t3, r_t3 = sbl("bt3", (128, NSLOT), F32)
                chiT, r_chi = sbl("bchi", (128, NSLOT), F32)
                rmin, r_rmin = sbl("brm", (128, NSLOT), F32)
                zacc, r_zacc = sbl("bza", (128, NSLOT), F32)
                zrT, r_zr = sbl("bzr", (128, NSLOT), F32)
                cstg, r_cstg = sbl("bcs", (128, 1024), BF16)
                psA, r_psA = psl("bpA", (128, 2048), F32)
                psB, r_psB = psl("bpB", (128, 1024), F32)
                psC, r_psC = psl("bpC", (128, 512), F32)
                psT, r_psT = psl("bpT", (128, 1024), BF16)
                r_psTh = [Res(), Res()]

                sim2_sb, r_sim2 = sbl("bsim2", (128, TOT), BF16)
                qr2_sb, r_qr2 = sbl("bqr2", (128, 1024), BF16)
                SIM = [(sim_sb, r_sim), (sim2_sb, r_sim2)]
                QR = [(qr_sb, r_qr), (qr2_sb, r_qr2)]

                def front_load(h):
                    qrb, r_qrb = QR[h % 2]
                    kvh = h // GRP
                    (kr_sb, r_kr), (v_sb, r_v) = KRV[kvh % 2]
                    if h % GRP == 0:
                        P.op("syn", lambda q, kvh=kvh, kr_sb=kr_sb: q.dma_start(
                            out=kr_sb[:], in_=krsp[kvh, :, :]),
                            reads=[r_krsp], writes=[r_kr], ndma=1)
                        P.op("syn", lambda q, kvh=kvh, v_sb=v_sb: q.dma_start(
                            out=v_sb[:], in_=vsp[kvh, :, :, :].rearrange(
                                "p tc d -> p (tc d)")),
                            reads=[r_vsp], writes=[r_v], ndma=1)
                    P.op("syn", lambda q, h=h, qrb=qrb: [
                        q.dma_start(out=qrb[:], in_=qrsp[h, :, :]),
                        q.dma_start(out=hb1_sb[:], in_=hb1[h, :, :]),
                        q.dma_start(out=hb2_sb[:], in_=hb2[h, :, :])],
                        reads=[r_qrsp], writes=[r_qrb, r_hb1, r_hb2], ndma=3)
                    P.op("pool", lambda q, h=h: [
                        q.dma_start(out=hw1b[:], in_=hp1[h, :, :]),
                        q.dma_start(out=hw2b[:], in_=hp2[h, :, :])],
                        writes=[r_hw1b, r_hw2b], ndma=2)

                def front_hash(h):
                    qrb, r_qrb = QR[h % 2]
                    (kr_sb, r_kr), (v_sb, r_v) = KRV[(h // GRP) % 2]
                    P.op("dve", lambda q, qrb=qrb, kr_sb=kr_sb: [
                        q.tensor_copy(y_sb[:, 0:4096], kr_sb[:]),
                        q.tensor_copy(y_sb[:, 4096:5120], qrb[:])],
                        reads=[r_kr, r_qrb], writes=[r_y])
                    for pc in range(5):
                        base = 1024 * pc
                        bank = psA[:, 1024 * (pc % 2):1024 * (pc % 2) + 1024]
                        P.op("pe", lambda q, base=base, bank=bank: [
                            q.matmul(bank[:, 512 * j:512 * j + 512], hw1b[:, :],
                                     y_sb[:, base + 512 * j:base + 512 * j + 512],
                                     start=True, stop=True) for j in range(2)],
                            reads=[r_hw1b, r_y], writes=[r_psA])
                        P.op("act", lambda q, bank=bank: q.activation(
                            out=y1p_sb[:], in_=bank, func=AF.Silu, bias=hb1_sb[:],
                            scale=1.0), reads=[r_psA, r_hb1], writes=[r_y1p])
                        P.op("pool", lambda q, base=base: q.tensor_tensor(
                            out=y_sb[:, base:base + 1024], in0=y1p_sb[:],
                            in1=y_sb[:, base:base + 1024], op=ALU.add),
                            reads=[r_y1p], writes_acc=[r_y])
                        P.op("pe", lambda q, base=base, bank=bank: [
                            q.matmul(bank[:, 512 * j:512 * j + 512], hw2b[:, :],
                                     y_sb[:, base + 512 * j:base + 512 * j + 512],
                                     start=True, stop=True) for j in range(2)],
                            reads=[r_hw2b, r_y], writes=[r_psA])
                        P.op("act", lambda q, bank=bank: q.activation(
                            out=zp_sb[:], in_=bank, func=AF.Identity, bias=hb2_sb[:],
                            scale=1.0), reads=[r_psA, r_hb2], writes=[r_zp])
                        P.op("dve", lambda q, base=base: q.tensor_tensor(
                            out=zp_sb[:], in0=zp_sb[:],
                            in1=y_sb[:, base:base + 1024], op=ALU.add),
                            reads=[r_y], writes=[r_zp])
                        P.op("act", lambda q, base=base: q.activation(
                            out=hq_sb[:, base:base + 1024], in_=zp_sb[:],
                            func=AF.Sign, bias=zero_c[:], scale=1.0),
                            reads=[r_zp, r_zero], writes_acc=[r_hq])

                def front_sim(h, s):
                    simb, r_simb = SIM[h % 2]
                    if True:
                        Es = EXT[s]
                        qsl = hq_sb[:, 4096 + 128 * s:4096 + 128 * s + 128]
                        for piece in range(0, Es, 2048):
                            plen = min(2048, Es - piece)
                            nj = plen // 512

                            def mmfn(q, s=s, piece=piece, nj=nj, Es=Es, qsl=qsl):
                                ins = []
                                for j in range(nj):
                                    t0 = piece + 512 * j
                                    is_diag = (t0 == Es - 512)
                                    ins.append(q.matmul(
                                        psA[:, 512 * j:512 * j + 512], qsl,
                                        hq_sb[:, t0:t0 + 512],
                                        start=True, stop=not is_diag))
                                    if is_diag:
                                        ins.append(q.matmul(
                                            psA[:, 512 * j:512 * j + 512],
                                            idb_sb[:, :], maskc_sb[:, :],
                                            start=False, stop=True))
                                return ins
                            P.op("pe", mmfn, reads=[r_hq, r_idb, r_maskc],
                                 writes=[r_psA])
                            P.op("act", lambda q, s=s, piece=piece, plen=plen,
                                 simb=simb:
                                 q.activation(
                                     out=simb[:, OFF[s] + piece:OFF[s] + piece + plen],
                                     in_=psA[:, 0:plen], func=AF.Identity,
                                     bias=zero_c[:], scale=1.0),
                                 reads=[r_psA, r_zero], writes_acc=[r_simb])
                    if debug and h == 0 and s == NSLOT - 1:
                        P.op("syn", lambda q, simb=simb: q.dma_start(
                            out=dbg["d_sim"][:, :], in_=simb[:]),
                            reads=[r_simb], ndma=1)

                def front(h):
                    front_load(h)
                    front_hash(h)
                    for s in range(NSLOT):
                        front_sim(h, s)

                def mid_a(h):
                    simb, r_simb = SIM[h % 2]
                    P.op("dve", lambda q: [q.memset(loT[:], -130.0),
                                           q.memset(chiT[:], 0.0)],
                         writes=[r_lo, r_chi])
                    for rnd, step in enumerate(STEPS):
                        P.op("dve", lambda q, step=step: q.tensor_scalar(
                            out=midT[:], in0=loT[:], scalar1=step, scalar2=None,
                            op0=ALU.add), reads=[r_lo], writes=[r_mid])
                        P.op("dve", lambda q: q.tensor_scalar(
                            out=ngm[:], in0=midT[:], scalar1=-1.0, scalar2=0.5,
                            op0=ALU.mult, op1=ALU.add),
                            reads=[r_mid], writes=[r_ngm])
                        P.op("act", lambda q, simb=simb: [
                            q.activation(out=trash_a[:, 0:SPL[s]],
                                         in_=simb[:, OFF[s]:OFF[s] + SPL[s]],
                                         func=AF.Sign, bias=ngm[:, s:s + 1],
                                         scale=1.0,
                                         accum_out=accA[:, s:s + 1])
                            for s in range(1, NSLOT)] + [
                            q.activation(out=trash_a[:, 0:512],
                                         in_=simb[:, 0:512], func=AF.Sign,
                                         bias=zero_c[:], scale=1.0)],
                            reads=[r_simb, r_ngm], writes=[r_accA])
                        P.op("dve", lambda q, simb=simb: [
                            q.tensor_scalar(out=trash_d[:, 0:EXT[s] - SPL[s]],
                                            in0=simb[:, OFF[s] + SPL[s]:OFF[s] + EXT[s]],
                                            scalar1=midT[:, s:s + 1], scalar2=None,
                                            op0=ALU.is_ge, op1=ALU.add,
                                            accum_out=accD[:, s:s + 1])
                            for s in range(1, NSLOT)] + [
                            q.tensor_copy(trash_d[:, 0:512], simb[:, 0:512])],
                            reads=[r_simb, r_mid], writes=[r_accD])
                        P.op("dve", lambda q: q.scalar_tensor_tensor(
                            out=t2[:], in0=accD[:], scalar=2.0, in1=accA[:],
                            op0=ALU.mult, op1=ALU.add),
                            reads=[r_accD, r_accA], writes=[r_t2])
                        P.op("dve", lambda q: q.tensor_tensor(
                            out=geT[:], in0=t2[:], in1=cc_sb[:], op=ALU.is_ge),
                            reads=[r_t2, r_cc], writes=[r_ge])
                        P.op("dve", lambda q, step=step: q.scalar_tensor_tensor(
                            out=loT[:], in0=geT[:], scalar=step, in1=loT[:],
                            op0=ALU.mult, op1=ALU.add),
                            reads=[r_ge], writes=[r_lo])
                        P.op("dve", lambda q: q.scalar_tensor_tensor(
                            out=cntT[:], in0=t2[:], scalar=0.5, in1=cr_sb[:],
                            op0=ALU.mult, op1=ALU.add),
                            reads=[r_t2, r_cr], writes=[r_cnt])
                        P.op("dve", lambda q: q.tensor_scalar(
                            out=geb[:], in0=geT[:], scalar1=-1.0, scalar2=1.0,
                            op0=ALU.mult, op1=ALU.add),
                            reads=[r_ge], writes=[r_geb])
                        P.op("dve", lambda q: q.tensor_tensor(
                            out=t3[:], in0=cntT[:], in1=chiT[:], op=ALU.subtract),
                            reads=[r_cnt, r_chi], writes=[r_t3])
                        P.op("dve", lambda q: q.tensor_tensor(
                            out=t3[:], in0=t3[:], in1=geb[:], op=ALU.mult),
                            reads=[r_geb], writes=[r_t3])
                        P.op("dve", lambda q: q.tensor_tensor(
                            out=chiT[:], in0=chiT[:], in1=t3[:], op=ALU.add),
                            reads=[r_t3], writes=[r_chi])
                    P.op("dve", lambda q: q.scalar_tensor_tensor(
                        out=rmin[:], in0=loT[:], scalar=-4096.0, in1=chiT[:],
                        op0=ALU.mult, op1=ALU.subtract),
                        reads=[r_lo, r_chi], writes=[r_rmin])
                    P.op("dve", lambda q: q.tensor_scalar(
                        out=rmin[:], in0=rmin[:], scalar1=820.0, scalar2=None,
                        op0=ALU.add), writes=[r_rmin])
                    P.op("dve", lambda q: [q.memset(loT[:, 0:1], -130.0),
                                           q.memset(rmin[:, 0:1], 533300.0)],
                         writes_acc=[r_lo, r_rmin])
                    if debug and h == 0:
                        P.op("syn", lambda q: [
                            q.dma_start(out=dbg["d_acc"][:, 0:NSLOT], in_=accA[:]),
                            q.dma_start(out=dbg["d_acc"][:, NSLOT:2 * NSLOT],
                                        in_=accD[:])],
                            reads=[r_accA, r_accD], ndma=2)
                        P.op("syn", lambda q: [
                            q.dma_start(out=dbg["d_book"][:, 0:NSLOT], in_=loT[:]),
                            q.dma_start(out=dbg["d_book"][:, NSLOT:2 * NSLOT],
                                        in_=rmin[:])],
                            reads=[r_lo, r_rmin], ndma=2)

                def mid_b(h):
                    simb, r_simb = SIM[h % 2]
                    qrb, r_qrb = QR[h % 2]
                    (kr_sb, r_kr), (v_sb, r_v) = KRV[(h // GRP) % 2]

                    def scores_exp(s):
                        Es = EXT[s]
                        ncol = H * s + h
                        qsl = qrb[:, 128 * s:128 * s + 128]
                        for pb in range(0, Es, 1024):
                            plen = min(1024, Es - pb)
                            P.op("pe", lambda q, qsl=qsl, pb=pb, plen=plen,
                                 kr_sb=kr_sb: [
                                q.matmul(psB[:, 512 * j:512 * j + 512], qsl,
                                         kr_sb[:, pb + 512 * j:pb + 512 * j + 512],
                                         start=True, stop=True)
                                for j in range(plen // 512)],
                                reads=[r_qrb, r_kr], writes=[r_psB])
                            P.op("act", lambda q, pb=pb, plen=plen, ncol=ncol:
                                 q.activation(
                                     out=p0_sb[:, pb:pb + plen], in_=psB[:, 0:plen],
                                     func=AF.Exp, scale=1.0 / SQD,
                                     bias=negb_sb[:, ncol:ncol + 1]),
                                 reads=[r_psB, r_negb], writes_acc=[r_p0])

                    def chain(s):
                        Es = EXT[s]
                        p_sb, r_p = PBUF[s % 2]
                        eq_sb, r_eq = p_sb, r_p
                        if s == 0:
                            P.op("dve", lambda q, Es=Es, simb=simb: q.tensor_scalar(
                                out=sel_sb[:, 0:Es], in0=simb[:, OFF[0]:OFF[0] + Es],
                                scalar1=-129.0, scalar2=None, op0=ALU.is_ge),
                                reads=[r_simb], writes=[r_sel])
                        else:
                            P.op("dve", lambda q, s=s, Es=Es, simb=simb: q.tensor_scalar(
                                out=eq_sb[:, 0:Es], in0=simb[:, OFF[s]:OFF[s] + Es],
                                scalar1=loT[:, s:s + 1], scalar2=None, op0=ALU.is_equal),
                                reads=[r_simb, r_lo], writes=[r_eq])
                            P.op("dve", lambda q, Es=Es: q.tensor_tensor_scan(
                                out=ps_sb[:, 0:Es], data0=eq_sb[:, 0:Es],
                                data1=eq_sb[:, 0:Es], initial=0.0, op0=ALU.add,
                                op1=ALU.bypass), reads=[r_eq], writes=[r_ps])
                            P.op("dve", lambda q, s=s, Es=Es, simb=simb: q.tensor_scalar(
                                out=rr_sb[:, 0:Es], in0=simb[:, OFF[s]:OFF[s] + Es],
                                scalar1=4096.0, scalar2=rmin[:, s:s + 1], op0=ALU.mult,
                                op1=ALU.add), reads=[r_simb, r_rmin], writes=[r_rr])
                            P.op("dve", lambda q, Es=Es: q.tensor_tensor(
                                out=sel_sb[:, 0:Es], in0=ps_sb[:, 0:Es],
                                in1=rr_sb[:, 0:Es], op=ALU.is_le),
                                reads=[r_ps, r_rr], writes=[r_sel])
                        P.op("dve", lambda q, s=s, Es=Es: q.scalar_tensor_tensor(
                            out=p_sb[:, 0:Es], in0=p0_sb[:, 0:Es], scalar=1.0,
                            in1=sel_sb[:, 0:Es], op0=ALU.mult, op1=ALU.mult,
                            accum_out=zacc[:, s:s + 1]),
                            reads=[r_p0, r_sel], writes=[r_p], writes_acc=[r_zacc])
                        P.op("dve", lambda q, simb=simb: q.tensor_copy(
                            trash_d[:, 0:512], simb[:, 0:512]),
                            reads=[r_simb])
                        P.op("dve", lambda q, s=s: q.reciprocal(
                            zrT[:, s:s + 1], zacc[:, s:s + 1]),
                            reads=[r_zacc], writes_acc=[r_zr])
                        P.op("dve", lambda q, s=s, Es=Es: q.tensor_scalar(
                            out=p_sb[:, 0:Es], in0=p_sb[:, 0:Es],
                            scalar1=zrT[:, s:s + 1], scalar2=None, op0=ALU.mult),
                            reads=[r_zr], writes=[r_p])

                    def tail(s):
                        Es = EXT[s]
                        p_sb, r_p = PBUF[s % 2]
                        for pb in range(0, Es, 1024):
                            plen = min(1024, Es - pb)
                            P.op("pe", lambda q, pb=pb, plen=plen, p_sb=p_sb: [
                                q.transpose(psT[:, 128 * j:128 * j + 128],
                                            p_sb[:, pb + 128 * j:pb + 128 * j + 128],
                                            idb_sb[:, :])
                                for j in range(plen // 128)],
                                reads=[r_p, r_idb], writes=[r_psT])
                            if (pb // 1024) % 2 == 0:
                                P.op("act", lambda q, pb=pb, plen=plen: q.copy(
                                    out=pt_sb[:, pb:pb + plen], in_=psT[:, 0:plen]),
                                    reads=[r_psT], writes_acc=[r_pt])
                            else:
                                P.op("dve", lambda q, pb=pb, plen=plen: q.tensor_copy(
                                    pt_sb[:, pb:pb + plen], psT[:, 0:plen]),
                                    reads=[r_psT], writes_acc=[r_pt])
                        ntc = Es // 128

                        def avfn(q, ntc=ntc, v_sb=v_sb):
                            return [q.matmul(
                                psC[:, 0:128], v_sb[:, 128 * tc:128 * tc + 128],
                                pt_sb[:, 128 * tc:128 * tc + 128],
                                start=(tc == 0), stop=(tc == ntc - 1))
                                for tc in range(ntc)]
                        P.op("pe", avfn, reads=[r_v, r_pt], writes=[r_psC])
                        P.op("dve", lambda q, s=s: q.tensor_copy(
                            cstg[:, 128 * s:128 * s + 128], psC[:, 0:128]),
                            reads=[r_psC], writes_acc=[r_cstg])

                    nx = h + 1 < n_heads
                    sim_order = [7, 6, 5, 4, 3, 2, 1, 0]
                    for s in range(NSLOT):
                        scores_exp(s)
                        chain(s)
                        if nx and s == 0:
                            front_load(h + 1)
                        if nx and s == 1:
                            front_hash(h + 1)
                        if nx and 2 <= s:
                            front_sim(h + 1, sim_order[s - 2])
                        if s > 0:
                            tail(s - 1)
                    tail(NSLOT - 1)
                    if nx:
                        for s2 in sim_order[6:]:
                            front_sim(h + 1, s2)
                    if debug and h == 0:
                        P.op("syn", lambda q: [
                            q.dma_start(out=dbg["d_sel"][:, :], in_=sel_sb[:]),
                            q.dma_start(out=dbg["d_p"][:, :],
                                        in_=PBUF[(NSLOT - 1) % 2][0][:])],
                            reads=[r_sel, PBUF[(NSLOT - 1) % 2][1]], ndma=2)
                    P.op("pool", lambda q, h=h: q.dma_start(
                        out=ctxsp[h, :, :], in_=cstg[:]),
                        reads=[r_cstg], writes_acc=[r_ctxsp], ndma=1)

                front(0)
                for h in range(n_heads):
                    mid_a(h)
                    mid_b(h)
        _phase5()
        P.phase_barrier()

        if debug:
            P.op("syn", lambda q: q.dma_start(
                out=dbg["d_ctx"][:, :, :], in_=ctxsp[:, :, :]),
                reads=[r_ctxsp], ndma=1)
        def _phase6():
            with ExitStack() as ph:
                def sbl(name, shape, dt):
                    t = ph.enter_context(nc.sbuf_tensor(name, list(shape), dt))
                    return t, Res()

                def psl(name, shape, dt):
                    t = ph.enter_context(nc.psum_tensor(name, list(shape), dt))
                    return t, Res()

                acomb, bcomb = [], []
                for g in range(4):
                    acomb.append(sbl("acomb%d" % g, (128, 1024), BF16))
                    bcomb.append(sbl("bcomb%d" % g, (64, E), BF16))
                last_b, r_lastb = sbl("last_b", (128, 1024), BF16)
                fbias_sb, r_fbias = sbl("fbias_sb", (1, E), BF16)
                pcs = [psl("cp%d" % s, (128, 512), F32) for s in range(8)]
                cm1 = ExitStack()
                def sbl1(name, shape, dt):
                    t = cm1.enter_context(nc.sbuf_tensor(name, list(shape), dt))
                    return t, Res()
                gbc_sb, r_gbc = sbl1("cg", (128, 64), F32)
                g1, r_g1 = sbl1("cg1", (1, 64), F32)
                lal, r_lal = sbl1("cla", (128, 1024), F32)
                lbl, r_lbl = sbl1("clb", (32, E // 2), F32)

                P.op("syn", lambda q: q.dma_start(out=g1[:], in_=gateown[:, :]),
                     writes=[r_g1], ndma=1)
                P.op("pe", lambda q: q.matmul(pcs[0][0][:, 0:64], ones1F[0:1, :],
                                              g1[0:1, :], start=True, stop=True),
                     reads=[r_ones1F, r_g1], writes=[pcs[0][1]])
                P.op("dve", lambda q: q.tensor_copy(gbc_sb[:], pcs[0][0][:, 0:64]),
                     reads=[pcs[0][1]], writes=[r_gbc])
                for g in range(4):
                    P.op("dve", lambda q, g=g: q.memset(acomb[g][0][:], 0.0),
                         writes=[acomb[g][1]])
                    P.op("dve", lambda q, g=g: q.memset(bcomb[g][0][0:32, :], 0.0),
                         writes=[bcomb[g][1]])
                for n in range(16):
                    P.op("syn", lambda q, n=n: q.dma_start(out=lal[:], in_=lat[n, :, :]),
                         writes=[r_lal], ndma=1)
                    for g in range(4):
                        P.op("dve", lambda q, g=g, n=n: q.scalar_tensor_tensor(
                            out=acomb[g][0][:], in0=lal[:],
                            scalar=gbc_sb[:, 16 * g + n:16 * g + n + 1],
                            in1=acomb[g][0][:], op0=ALU.mult, op1=ALU.add),
                            reads=[r_lal, r_gbc], writes=[acomb[g][1]])
                    for hh in range(2):
                        P.op("syn", lambda q, n=n, hh=hh: q.dma_start(
                            out=lbl[:], in_=lbt[n, :, 2048 * hh:2048 * hh + 2048]),
                            writes=[r_lbl], ndma=1)
                        for g in range(4):
                            P.op("dve", lambda q, g=g, n=n, hh=hh: q.scalar_tensor_tensor(
                                out=bcomb[g][0][0:32, 2048 * hh:2048 * hh + 2048],
                                in0=lbl[:],
                                scalar=gbc_sb[0:32, 16 * g + n:16 * g + n + 1],
                                in1=bcomb[g][0][0:32, 2048 * hh:2048 * hh + 2048],
                                op0=ALU.mult, op1=ALU.add),
                                reads=[r_lbl, r_gbc], writes=[bcomb[g][1]])
                P.op("syn", lambda q: q.dma_start(out=lal[:], in_=last[:, :]),
                     writes=[r_lal], ndma=1)
                P.op("act", lambda q: q.copy(out=last_b[:], in_=lal[:]),
                     reads=[r_lal], writes=[r_lastb])
                for hh in range(2):
                    P.op("syn", lambda q, hh=hh: q.dma_start(
                        out=lbl[:], in_=lbst[:, 2048 * hh:2048 * hh + 2048]),
                        writes=[r_lbl], ndma=1)
                    for g in range(4):
                        P.op("act", lambda q, g=g, hh=hh: q.copy(
                            out=bcomb[g][0][32:64, 2048 * hh:2048 * hh + 2048],
                            in_=lbl[:]),
                            reads=[r_lbl], writes_acc=[bcomb[g][1]])
                for hh in range(2):
                    P.op("syn", lambda q, hh=hh: q.dma_start(
                        out=lbl[0:1, :], in_=fbias[:, 2048 * hh:2048 * hh + 2048]),
                        writes=[r_lbl], ndma=1)
                    P.op("act", lambda q, hh=hh: q.copy(
                        out=fbias_sb[0:1, 2048 * hh:2048 * hh + 2048],
                        in_=lbl[0:1, :]),
                        reads=[r_lbl], writes_acc=[r_fbias])
                cm1.close()

                ctx_sb, r_ctx = sbl("cctx", (128, 32 * 1024), BF16)
                xno_sb, r_xno2 = sbl("cxno", (128, 32 * 1024), BF16)
                axt_sb, r_axt = sbl("caxt", (64, 8 * 128), BF16)
                wob = [sbl("cwob%d" % i, (128, 512), BF16) for i in range(2)]
                knb = [sbl("cknb%d" % i, (128, 512), BF16) for i in range(2)]
                xog = [sbl("cxo%d" % i, (128, 512), F32) for i in range(2)]
                ost = [sbl("cos%d" % i, (128, 512), F32) for i in range(2)]

                P.op("syn", lambda q: [
                    q.dma_start(out=ctx_sb[:, 1024 * hh:1024 * hh + 1024],
                                in_=ctxsp[hh, :, :]) for hh in range(H)],
                    reads=[r_ctxsp], writes=[r_ctx], ndma=H)
                P.op("syn", lambda q: [
                    q.dma_start(out=xno_sb[:, 1024 * ec:1024 * ec + 1024],
                                in_=xosp[ec, :, :]) for ec in range(32)],
                    reads=[r_xosp], writes=[r_xno2], ndma=32)

                for s in range(NSLOT):
                    g = s // 2
                    pc, r_pc = pcs[0]
                    def axfn(q, s=s, g=g, pc=pc):
                        ins = []
                        for ic in range(32):
                            ins.append(q.matmul(
                                pc[0:32, 0:128], acomb[g][0][:, 32 * ic:32 * ic + 32],
                                xno_sb[:, 1024 * ic + 128 * s:1024 * ic + 128 * s + 128],
                                start=(ic == 0), stop=(ic == 31)))
                            ins.append(q.matmul(
                                pc[32:64, 0:128], last_b[:, 32 * ic:32 * ic + 32],
                                xno_sb[:, 1024 * ic + 128 * s:1024 * ic + 128 * s + 128],
                                start=(ic == 0), stop=(ic == 31)))
                        return ins
                    P.op("pe", axfn, reads=[acomb[g][1], r_lastb, r_xno2],
                         writes=[r_pc])
                    P.op("dve", lambda q, s=s, pc=pc: q.tensor_copy(
                        axt_sb[:, 128 * s:128 * s + 128], pc[0:64, 0:128]),
                        reads=[r_pc], writes_acc=[r_axt])
                if debug:
                    P.op("syn", lambda q: [
                        q.dma_start(out=dbg["d_acomb"][g, :, :], in_=acomb[g][0][:])
                        for g in range(4)] + [
                        q.dma_start(out=dbg["d_bcomb"][g, :, :], in_=bcomb[g][0][:])
                        for g in range(4)] + [
                        q.dma_start(out=dbg["d_axt"][:, :], in_=axt_sb[:]),
                        q.dma_start(out=dbg["d_lastb"][:, :], in_=last_b[:])],
                        reads=[acomb[g][1] for g in range(4)] +
                              [bcomb[g][1] for g in range(4)] +
                              [r_axt, r_lastb], ndma=10)

                for ec in range(8):
                    for hc in range(32):
                        bb, r_bb = wob[hc % 2]
                        P.op("pool", lambda q, ec=ec, hc=hc, bb=bb: q.dma_start(
                            out=bb[:], in_=wot[128 * hc:128 * hc + 128,
                                               512 * ec:512 * ec + 512]),
                            writes=[r_bb], ndma=1)
                        P.op("pe", lambda q, hc=hc, bb=bb: [
                            q.matmul(pcs[s][0][:, 0:512],
                                     ctx_sb[:, 1024 * hc + 128 * s:1024 * hc + 128 * s + 128],
                                     bb[:], start=(hc == 0), stop=False)
                            for s in range(8)],
                            reads=[r_bb, r_ctx], writes=[pcs[s][1] for s in range(8)])
                    for ic in range(32):
                        bb, r_bb = knb[ic % 2]
                        P.op("pool", lambda q, ec=ec, ic=ic, bb=bb: q.dma_start(
                            out=bb[:], in_=fker[128 * ic:128 * ic + 128,
                                                512 * ec:512 * ec + 512]),
                            writes=[r_bb], ndma=1)
                        P.op("pe", lambda q, ic=ic, bb=bb: [
                            q.matmul(pcs[s][0][:, 0:512],
                                     xno_sb[:, 1024 * ic + 128 * s:1024 * ic + 128 * s + 128],
                                     bb[:], start=False, stop=False)
                            for s in range(8)],
                            reads=[r_bb, r_xno2], writes=[pcs[s][1] for s in range(8)])
                    P.op("pe", lambda q, ec=ec: [
                        q.matmul(pcs[s][0][:, 0:512], axt_sb[:, 128 * s:128 * s + 128],
                                 bcomb[s // 2][0][:, 512 * ec:512 * ec + 512],
                                 start=False, stop=False) for s in range(8)],
                        reads=[r_axt] + [bcomb[g][1] for g in range(4)],
                        writes=[pcs[s][1] for s in range(8)])
                    P.op("pe", lambda q, ec=ec: [
                        q.matmul(pcs[s][0][:, 0:512], ones1b[0:1, :],
                                 fbias_sb[0:1, 512 * ec:512 * ec + 512],
                                 start=False, stop=True) for s in range(8)],
                        reads=[r_ones1b, r_fbias],
                        writes=[pcs[s][1] for s in range(8)])
                    for s in range(8):
                        bx, r_bx = xog[s % 2]
                        bo, r_bo = ost[s % 2]
                        P.op("syn", lambda q, s=s, ec=ec, bx=bx: q.dma_start(
                            out=bx[:], in_=xown[128 * s:128 * s + 128,
                                                512 * ec:512 * ec + 512]),
                            writes=[r_bx], ndma=1)
                        P.op("dve", lambda q, s=s, bx=bx, bo=bo: q.tensor_tensor(
                            out=bo[:], in0=pcs[s][0][:, 0:512], in1=bx[:],
                            op=ALU.add), reads=[pcs[s][1], r_bx], writes=[r_bo])
                        P.op("syn", lambda q, s=s, ec=ec, bo=bo: q.dma_start(
                            out=out[128 * s:128 * s + 128, 512 * ec:512 * ec + 512],
                            in_=bo[:]), reads=[r_bo], writes_acc=[r_out], ndma=1)

                P.op("syn", lambda q: q.nop(), reads=[r_out])

        _phase6()
        block = ctx.enter_context(nc.Block())
        P.emit(block)

    return nc


_NC_CACHE = None


def _get_program():
    global _NC_CACHE
    if _NC_CACHE is None:
        _NC_CACHE = build_program()
    return _NC_CACHE


def _host_tables(kcore):
    inv_freq = 1.0 / (10000.0 ** (np.arange(0, D, 2, dtype=np.float32) / D))
    t = np.arange(S, dtype=np.float32)
    freqs = np.outer(t, inv_freq)
    emb = np.concatenate([freqs, freqs], -1)
    cos = np.cos(emb).astype(np.float32)
    sin = np.sin(emb).astype(np.float32)
    cosT = np.ascontiguousarray(cos.T)
    sinT = sin.T
    sinF = sinT.copy()
    sinF[0:64] = -sinF[0:64]
    sinF = np.ascontiguousarray(sinF)
    rows = np.concatenate([np.arange(128 * (4 * s + kcore),
                                     128 * (4 * s + kcore) + 128)
                           for s in range(NSLOT)])
    cosq = np.ascontiguousarray(cosT[:, rows])
    sinq = np.ascontiguousarray(sinF[:, rows])
    pp, jj = np.meshgrid(np.arange(128), np.arange(512), indexing="ij")
    mk = np.where(jj <= 128 * kcore + pp, 0.0, -10000.0).astype(np.float32)
    import ml_dtypes
    return (cosT, sinF, cosq, sinq, rows,
            np.eye(128, dtype=np.float32).astype(ml_dtypes.bfloat16),
            mk.astype(ml_dtypes.bfloat16))


def kernel(**inputs):
    x = np.asarray(inputs["L0_attn_hidden_states"], np.float32)
    Wq = np.ascontiguousarray(np.asarray(inputs["L0_attn_Wq"], np.float32).T)
    Wk = np.ascontiguousarray(np.asarray(inputs["L0_attn_Wk"], np.float32).T)
    Wv = np.ascontiguousarray(np.asarray(inputs["L0_attn_Wv"], np.float32).T)
    Wo = np.ascontiguousarray(np.asarray(inputs["L0_attn_Wo"], np.float32).T)
    fker = np.ascontiguousarray(np.asarray(inputs["L0_ffn_kernel"], np.float32))
    fbias = np.asarray(inputs["L0_ffn_bias"], np.float32).reshape(1, E)
    hp1 = np.ascontiguousarray(np.asarray(inputs["L0_attn_hash_proj1"], np.float32))
    hp2 = np.ascontiguousarray(np.asarray(inputs["L0_attn_hash_proj2"], np.float32))
    hb1 = np.ascontiguousarray(
        np.asarray(inputs["L0_attn_hash_bias1"], np.float32).transpose(0, 2, 1))
    hb2 = np.ascontiguousarray(
        np.asarray(inputs["L0_attn_hash_bias2"], np.float32).transpose(0, 2, 1))
    gate = np.asarray(inputs["L0_ffn_gate"], np.float32)
    lA = np.asarray(inputs["L0_ffn_lora_A"], np.float32)
    lB = np.asarray(inputs["L0_ffn_lora_B"], np.float32)
    lAs = np.asarray(inputs["L0_ffn_lora_A_shared"], np.float32)
    lBs = np.asarray(inputs["L0_ffn_lora_B_shared"], np.float32)

    lat = np.ascontiguousarray(
        lA.transpose(0, 2, 1).reshape(16, 32, 128, 32).transpose(0, 2, 1, 3)
        .reshape(16, 128, 1024))
    last = np.ascontiguousarray(
        lAs.T.reshape(32, 128, 32).transpose(1, 0, 2).reshape(128, 1024))
    lbt = np.ascontiguousarray(lB.transpose(0, 2, 1))
    lbst = np.ascontiguousarray(lBs.T)

    cc = np.broadcast_to(
        np.array([1e9] + [1640.0 - SPL[s] for s in range(1, NSLOT)], np.float32),
        (128, NSLOT)).copy()
    cr = np.broadcast_to(
        np.array([0.0] + [SPL[s] / 2.0 for s in range(1, NSLOT)], np.float32),
        (128, NSLOT)).copy()

    nc = _get_program()
    in_maps = []
    rows_all = []
    for c in range(8):
        b, k = c // 4, c % 4
        cosT, sinF, cosq, sinq, rows, identb, maskc = _host_tables(k)
        rows_all.append(rows)
        xb = np.ascontiguousarray(x[b])
        in_maps.append({
            "xfull": xb,
            "xown": np.ascontiguousarray(xb[rows]),
            "wqt": Wq, "wkt": Wk, "wvt": Wv, "wot": Wo,
            "fker": fker, "fbias": fbias,
            "hp1": hp1, "hb1": hb1, "hp2": hp2, "hb2": hb2,
            "lat": lat, "last": last, "lbt": lbt, "lbst": lbst,
            "gateown": np.ascontiguousarray(
                gate[4 * b:4 * b + 4].reshape(1, 64)),
            "coskf": np.ascontiguousarray(cosT),
            "sinkf": np.ascontiguousarray(sinF),
            "cosqf": cosq, "sinqf": sinq,
            "identb": identb,
            "identf": np.eye(128, dtype=np.float32), "maskc": maskc,
            "cc1640": cc, "cr820": cr,
        })
    res = run_bass_kernel_spmd(nc, in_maps, core_ids=list(range(8)))
    outf = np.empty((B, S, E), np.float32)
    for c in range(8):
        outf[c // 4][rows_all[c]] = res.results[c]["out"]
    return outf


# version 14
# speedup vs baseline: 1.2028x; 1.0134x over previous
import numpy as np
from contextlib import ExitStack

import concourse.bass as bass
import concourse.mybir as mybir
from concourse.bass_utils import run_bass_kernel_spmd

F32 = mybir.dt.float32
BF16 = mybir.dt.bfloat16
F16 = mybir.dt.float16
AF = mybir.ActivationFunctionType
ALU = mybir.AluOpType
AXX = mybir.AxisListType.X

B, S, E, H, KVH, D = 2, 4096, 4096, 32, 8, 128
GRP = H // KVH
NSLOT = 8
EXT = [512 * (s + 1) for s in range(NSLOT)]
OFF = [0]
for _e in EXT:
    OFF.append(OFF[-1] + _e)
TOT = OFF[-1]
NUM_REMAIN = 820
SQD = float(np.sqrt(D))
STEPS = [128.0, 64.0, 32.0, 16.0, 8.0, 4.0, 2.0]
SPL = [0, 496, 736, 992, 1232, 1472, 1728, 1968]


class Res:
    __slots__ = ("writers", "readers")

    def __init__(self):
        self.writers = []
        self.readers = []


class Prog:
    ENGS = ("pe", "dve", "act", "pool", "syn")
    PARANOID = False

    def __init__(self, nc, ctx):
        self.nc = nc
        self.sem = {e: ctx.enter_context(nc.semaphore("sem_" + e)) for e in self.ENGS}
        self.cnt = {e: 0 for e in self.ENGS}
        self.sched = {e: [] for e in self.ENGS}
        self.waited = {e: {} for e in self.ENGS}
        self.barrier = []
        self.pad_sb = ctx.enter_context(nc.sbuf_tensor("pe_pad", [128, 1024], BF16))

    def op(self, eng, fn, reads=(), writes=(), writes_acc=(), ndma=0):
        deps = {}

        def add(tok):
            if tok is None:
                return
            e2, v2 = tok
            if e2 == eng and eng == "pe":
                return
            if deps.get(e2, 0) < v2:
                deps[e2] = v2

        for t in self.barrier:
            add(t)
        if Prog.PARANOID:
            for e2 in self.ENGS:
                if self.cnt[e2] > 0 and e2 != eng:
                    add((e2, self.cnt[e2]))
        for r in reads:
            for t in r.writers:
                add(t)
        for w in writes:
            for t in w.writers:
                add(t)
            for t in w.readers:
                add(t)
        for w in writes_acc:
            for t in w.readers:
                add(t)
        waits = []
        wt = self.waited[eng]
        for e2 in self.ENGS:
            v2 = deps.get(e2)
            if v2 is None:
                continue
            if wt.get(e2, -1) < v2:
                waits.append((e2, v2))
                wt[e2] = v2
        if eng in ("dve", "act") and any(e2 == "pe" for e2, _ in waits):
            pad = self.pad_sb
            inner = fn

            def fn(q, _inner=inner, _pad=pad, _eng=eng):
                if _eng == "dve":
                    p = q.tensor_copy(_pad[:, 0:256], _pad[:, 256:512])
                else:
                    p = q.copy(out=_pad[:, 512:768], in_=_pad[:, 768:1024])
                ins = _inner(q)
                if not isinstance(ins, (list, tuple)):
                    ins = [ins]
                return [p] + list(ins)
        inc = 16 * ndma if ndma else 1
        self.cnt[eng] += inc
        self.sched[eng].append((waits, fn, bool(ndma)))
        tok = (eng, self.cnt[eng])
        for r in reads:
            r.readers.append(tok)
        for w in writes:
            w.writers = [tok]
            w.readers = []
        for w in writes_acc:
            w.writers.append(tok)
        return tok

    def phase_barrier(self):
        self.barrier = [(e, self.cnt[e]) for e in self.ENGS if self.cnt[e] > 0]

    def emit(self, block):
        sections = {"pe": block.tensor, "dve": block.vector, "act": block.scalar,
                    "pool": block.gpsimd, "syn": block.sync}

        def mk(eng):
            sched = self.sched[eng]
            sems = self.sem

            def emit_stream(q):
                for waits, fn, is_dma in sched:
                    for e2, v2 in waits:
                        q.wait_ge(sems[e2], v2)
                    instrs = fn(q)
                    if not isinstance(instrs, (list, tuple)):
                        instrs = [instrs]
                    if is_dma:
                        for ins in instrs:
                            ins.then_inc(sems[eng], 16)
                    else:
                        instrs[-1].then_inc(sems[eng], 1)
            return emit_stream

        for eng, dec in sections.items():
            dec(mk(eng))


def build_program(n_heads=H, debug=False):
    nc = bass.Bass("TRN2", target_bir_lowering=False)

    def din(name, shape, dt=F32):
        return nc.dram_tensor(name, list(shape), dt, kind="ExternalInput").ap()

    def dscr(name, shape, dt):
        return nc.dram_tensor(name, list(shape), dt, kind="Internal").ap()

    xfull = din("xfull", (S, E))
    xown = din("xown", (1024, E))
    wqt = din("wqt", (E, E))
    wkt = din("wkt", (E, KVH * D))
    wvt = din("wvt", (E, KVH * D))
    wot = din("wot", (E, E))
    fker = din("fker", (E, E))
    fbias = din("fbias", (1, E))
    hp1 = din("hp1", (H, D, D))
    hb1 = din("hb1", (H, D, 1))
    hp2 = din("hp2", (H, D, D))
    hb2 = din("hb2", (H, D, 1))
    lat = din("lat", (16, 128, 1024))
    last = din("last", (128, 1024))
    lbt = din("lbt", (16, 32, E))
    lbst = din("lbst", (32, E))
    gateown = din("gateown", (1, 64))
    coskf = din("coskf", (D, S))
    sinkf = din("sinkf", (D, S))
    cosqf = din("cosqf", (D, 1024))
    sinqf = din("sinqf", (D, 1024))
    identb = din("identb", (128, 128), BF16)
    identf = din("identf", (128, 128))
    maskc = din("maskc", (128, 512), BF16)
    cc1640 = din("cc1640", (128, NSLOT))
    cr820 = din("cr820", (128, NSLOT))

    out = nc.dram_tensor("out", [1024, E], F32, kind="ExternalOutput").ap()
    if debug:
        dbg = {
            "d_xn": nc.dram_tensor("d_xn", [32, 128, S], BF16, kind="ExternalOutput").ap(),
            "d_kr": nc.dram_tensor("d_kr", [KVH, 128, S], BF16, kind="ExternalOutput").ap(),
            "d_v": nc.dram_tensor("d_v", [KVH, 128, 32, 128], BF16, kind="ExternalOutput").ap(),
            "d_qr": nc.dram_tensor("d_qr", [32, 128, 1024], BF16, kind="ExternalOutput").ap(),
            "d_qn2": nc.dram_tensor("d_qn2", [32, 1024], F32, kind="ExternalOutput").ap(),
            "d_ctx": nc.dram_tensor("d_ctx", [H, 128, 1024], BF16, kind="ExternalOutput").ap(),
            "d_negb": nc.dram_tensor("d_negb", [128, NSLOT * H], F32, kind="ExternalOutput").ap(),
            "d_sim": nc.dram_tensor("d_sim", [128, TOT], BF16, kind="ExternalOutput").ap(),
            "d_book": nc.dram_tensor("d_book", [128, 2 * NSLOT], F32, kind="ExternalOutput").ap(),
            "d_sel": nc.dram_tensor("d_sel", [128, 4096], BF16, kind="ExternalOutput").ap(),
            "d_p": nc.dram_tensor("d_p", [128, 4096], BF16, kind="ExternalOutput").ap(),
            "d_xo": nc.dram_tensor("d_xo", [32, 128, 1024], BF16, kind="ExternalOutput").ap(),
            "d_cosk": nc.dram_tensor("d_cosk", [128, S], F32, kind="ExternalOutput").ap(),
            "d_sink": nc.dram_tensor("d_sink", [128, S], F32, kind="ExternalOutput").ap(),
            "d_cosq": nc.dram_tensor("d_cosq", [128, 1024], F32, kind="ExternalOutput").ap(),
            "d_qn2r": nc.dram_tensor("d_qn2r", [1, 1024], F32, kind="ExternalOutput").ap(),
            "d_acc": nc.dram_tensor("d_acc", [128, 2 * NSLOT], F32, kind="ExternalOutput").ap(),
            "d_acomb": nc.dram_tensor("d_acomb", [4, 128, 1024], BF16, kind="ExternalOutput").ap(),
            "d_bcomb": nc.dram_tensor("d_bcomb", [4, 64, E], BF16, kind="ExternalOutput").ap(),
            "d_axt": nc.dram_tensor("d_axt", [64, 1024], BF16, kind="ExternalOutput").ap(),
            "d_lastb": nc.dram_tensor("d_lastb", [128, 1024], BF16, kind="ExternalOutput").ap(),
            "d_gbc": nc.dram_tensor("d_gbc", [128, 64], F32, kind="ExternalOutput").ap(),
        }

    xnsp = dscr("xnsp", (32, 128, S), BF16)
    xosp = dscr("xosp", (32, 128, 1024), BF16)
    krsp = dscr("krsp", (KVH, 128, S), BF16)
    vsp = dscr("vsp", (KVH, 128, 32, 128), BF16)
    qrsp = dscr("qrsp", (32, 128, 1024), BF16)
    ctxsp = dscr("ctxsp", (H, 128, 1024), BF16)
    qn2sp = dscr("qn2sp", (32, 1024), F32)

    ctx = ExitStack()
    with ctx:
        P = Prog(nc, ctx)

        def sb(name, shape, dt):
            t = ctx.enter_context(nc.sbuf_tensor(name, list(shape), dt))
            return t, Res()

        zero_c, r_zero = sb("zero_c", (128, 1), F32)
        onescol, r_onescol = sb("onescol", (128, 1), BF16)
        ones1b, r_ones1b = sb("ones1b", (1, 128), BF16)
        ones1F, r_ones1F = sb("ones1F", (1, 128), F32)
        idb_sb, r_idb = sb("idb_sb", (128, 128), BF16)
        idf_sb, r_idf = sb("idf_sb", (128, 128), F32)
        maskc_sb, r_maskc = sb("maskc_sb", (128, 512), BF16)
        cc_sb, r_cc = sb("cc_sb", (128, NSLOT), F32)
        cr_sb, r_cr = sb("cr_sb", (128, NSLOT), F32)
        kmax_acc, r_kmax = sb("kmax_acc", (1, 1), F32)
        negkb, r_negkb = sb("negkb", (128, 1), F32)
        negb_sb, r_negb = sb("negb_sb", (128, NSLOT * H), F32)

        P.op("pool", lambda q: [q.memset(zero_c[:], 0.0),
                                q.memset(onescol[:], 1.0),
                                q.memset(ones1b[:], 1.0),
                                q.memset(ones1F[:], 1.0),
                                q.memset(kmax_acc[:], 0.0)],
             writes=[r_zero, r_onescol, r_ones1b, r_ones1F, r_kmax])
        P.op("syn", lambda q: [q.dma_start(out=idb_sb[:], in_=identb[:, :]),
                               q.dma_start(out=idf_sb[:], in_=identf[:, :]),
                               q.dma_start(out=maskc_sb[:], in_=maskc[:, :]),
                               q.dma_start(out=cc_sb[:], in_=cc1640[:, :]),
                               q.dma_start(out=cr_sb[:], in_=cr820[:, :])],
             writes=[r_idb, r_idf, r_maskc, r_cc, r_cr], ndma=5)

        r_xnsp, r_xosp, r_krsp, r_vsp, r_qrsp, r_ctxsp, r_qn2sp = (
            Res(), Res(), Res(), Res(), Res(), Res(), Res())
        r_xnsp_tc = [Res() for _ in range(32)]
        r_out = Res()

        def emit_ln(bx, r_bx, bxn, r_bxn, st48, r_st48, mv, r_mv, rstd, r_rstd,
                    nmr, r_nmr):
            P.op("dve", lambda q: [
                q.bn_stats(st48[:, 6 * i:6 * i + 6], bx[:, 512 * i:512 * i + 512])
                for i in range(8)], reads=[r_bx], writes=[r_st48])
            P.op("dve", lambda q: q.bn_aggr(mv[:], st48[:]),
                 reads=[r_st48], writes=[r_mv])
            P.op("dve", lambda q: q.tensor_scalar(
                out=mv[:, 1:2], in0=mv[:, 1:2], scalar1=1e-5, scalar2=None,
                op0=ALU.add), writes=[r_mv])
            P.op("act", lambda q: q.activation(
                out=rstd[:], in_=mv[:, 1:2], func=AF.Sqrt, bias=zero_c[:],
                scale=1.0), reads=[r_mv, r_zero], writes=[r_rstd])
            P.op("dve", lambda q: q.reciprocal(rstd[:], rstd[:]), writes=[r_rstd])
            P.op("dve", lambda q: q.tensor_tensor(
                out=nmr[:], in0=mv[:, 0:1], in1=rstd[:], op=ALU.mult),
                reads=[r_mv, r_rstd], writes=[r_nmr])
            P.op("dve", lambda q: q.tensor_scalar(
                out=nmr[:], in0=nmr[:], scalar1=-1.0, scalar2=None, op0=ALU.mult),
                writes=[r_nmr])
            P.op("act", lambda q: q.activation(
                out=bxn[:], in_=bx[:], func=AF.Identity, bias=nmr[:], scale=rstd[:]),
                reads=[r_bx, r_nmr, r_rstd], writes=[r_bxn])

        def _phase1():
            with ExitStack() as ph:
                def sbl(name, shape, dt):
                    t = ph.enter_context(nc.sbuf_tensor(name, list(shape), dt))
                    return t, Res()

                def psl(name, shape, dt):
                    t = ph.enter_context(nc.psum_tensor(name, list(shape), dt))
                    return t, Res()

                xt = [sbl("a0x%d" % i, (128, E), F32) for i in range(2)]
                xnt = [sbl("a0n%d" % i, (128, E), BF16) for i in range(2)]
                xo = [sbl("a0o%d" % i, (128, 4096), BF16) for i in range(2)]
                st48, r_st48 = sbl("a0s", (128, 48), F32)
                mv, r_mv = sbl("a0m", (128, 2), F32)
                rstd, r_rstd = sbl("a0r", (128, 1), F32)
                nmr, r_nmr = sbl("a0nm", (128, 1), F32)
                pst, r_pst = psl("a0p", (128, 1024), BF16)

                for tc in range(32):
                    bx, r_bx = xt[tc % 2]
                    bxn, r_bxn = xnt[tc % 2]
                    bo, r_bo = xo[tc % 2]
                    P.op("syn", lambda q, tc=tc, bx=bx: q.dma_start(
                        out=bx[:], in_=xfull[128 * tc:128 * tc + 128, :]),
                        writes=[r_bx], ndma=1)
                    emit_ln(bx, r_bx, bxn, r_bxn, st48, r_st48, mv, r_mv,
                            rstd, r_rstd, nmr, r_nmr)
                    for g in range(4):
                        P.op("pe", lambda q, bxn=bxn, g=g: [
                            q.transpose(pst[:, 128 * j:128 * j + 128],
                                        bxn[:, 128 * (8 * g + j):128 * (8 * g + j) + 128],
                                        idb_sb[:, :]) for j in range(8)],
                            reads=[r_bxn, r_idb], writes=[r_pst])
                        P.op("dve", lambda q, bo=bo, g=g: q.tensor_copy(
                            bo[:, 1024 * g:1024 * g + 1024], pst[:]),
                            reads=[r_pst], writes_acc=[r_bo])
                    P.op("pool", lambda q, bo=bo, tc=tc: [
                        q.dma_start(out=xnsp[ec, :, 128 * tc:128 * tc + 128],
                                    in_=bo[:, 128 * ec:128 * ec + 128])
                        for ec in range(32)], reads=[r_bo],
                        writes_acc=[r_xnsp, r_xnsp_tc[tc]], ndma=32)

        _phase1()
        P.phase_barrier()

        def _phase2():
            with ExitStack() as ph:
                def sbl(name, shape, dt):
                    t = ph.enter_context(nc.sbuf_tensor(name, list(shape), dt))
                    return t, Res()

                def psl(name, shape, dt):
                    t = ph.enter_context(nc.psum_tensor(name, list(shape), dt))
                    return t, Res()

                wk_sb, r_wk = sbl("a1w", (128, 32 * 1024), BF16)
                xnu = [sbl("a1x%d" % i, (128, 32 * 512), BF16) for i in range(2)]
                cosk_sb, r_cosk = sbl("a1c", (128, S), F32)
                sink_sb, r_sink = sbl("a1s", (128, S), F32)
                tmpc, r_tmpc = sbl("a1tc", (128, 512), F32)
                tmps, r_tmps = sbl("a1ts", (128, 512), F32)
                krbf, r_krbf = sbl("a1kb", (128, 512), BF16)
                kr2, r_kr2 = sbl("a1k2", (128, 512), BF16)
                kn1, r_kn1 = sbl("a1kn", (1, 1), F32)
                psk = [psl("a1p%d" % i, (128, 512), F32) for i in range(2)]
                psm, r_psm = psl("a1m", (128, 512), F32)

                P.op("syn", lambda q: [q.dma_start(out=cosk_sb[:], in_=coskf[:, :]),
                                       q.dma_start(out=sink_sb[:], in_=sinkf[:, :])],
                     writes=[r_cosk, r_sink], ndma=2)
                for j in range(8):
                    P.op("pool", lambda q, j=j: q.dma_start(
                        out=wk_sb[:].rearrange("p (ec o) -> p ec o", ec=32)[
                            :, :, 128 * j:128 * j + 128],
                        in_=wkt[:, 128 * j:128 * j + 128].rearrange(
                            "(ec p) o -> p ec o", p=128)),
                        writes_acc=[r_wk], ndma=1)
                for u in range(8):
                    bx, r_bx = xnu[u % 2]
                    P.op("syn", lambda q, u=u, bx=bx: [
                        q.dma_start(out=bx[:, 512 * ec:512 * ec + 512],
                                    in_=xnsp[ec, :, 512 * u:512 * u + 512])
                        for ec in range(32)],
                        reads=[r_xnsp_tc[t] for t in range(4 * u, 4 * u + 4)],
                        writes=[r_bx], ndma=32)
                    for kvh in range(KVH):
                        pk, r_pk = psk[kvh % 2]
                        P.op("pe", lambda q, bx=bx, kvh=kvh, pk=pk: [
                            q.matmul(pk[:, 0:512],
                                     wk_sb[:, 1024 * ec + 128 * kvh:1024 * ec + 128 * kvh + 128],
                                     bx[:, 512 * ec:512 * ec + 512],
                                     start=(ec == 0), stop=(ec == 31))
                            for ec in range(32)], reads=[r_wk, r_bx], writes=[r_pk])
                        P.op("dve", lambda q, pk=pk, u=u: q.tensor_tensor(
                            out=tmpc[:], in0=pk[:, 0:512],
                            in1=cosk_sb[:, 512 * u:512 * u + 512], op=ALU.mult),
                            reads=[r_pk, r_cosk], writes=[r_tmpc])
                        P.op("dve", lambda q, pk=pk, u=u: [
                            q.tensor_tensor(out=tmps[0:64, :], in0=pk[64:128, 0:512],
                                            in1=sink_sb[0:64, 512 * u:512 * u + 512],
                                            op=ALU.mult),
                            q.tensor_tensor(out=tmps[64:128, :], in0=pk[0:64, 0:512],
                                            in1=sink_sb[64:128, 512 * u:512 * u + 512],
                                            op=ALU.mult)],
                            reads=[r_pk, r_sink], writes=[r_tmps])
                        P.op("dve", lambda q: q.tensor_tensor(
                            out=krbf[:], in0=tmpc[:], in1=tmps[:], op=ALU.add),
                            reads=[r_tmpc, r_tmps], writes=[r_krbf])
                        P.op("pool", lambda q, kvh=kvh, u=u: q.dma_start(
                            out=krsp[kvh, :, 512 * u:512 * u + 512], in_=krbf[:]),
                            reads=[r_krbf], writes_acc=[r_krsp], ndma=1)
                        P.op("act", lambda q: q.activation(
                            out=kr2[:], in_=krbf[:], func=AF.Square, bias=zero_c[:],
                            scale=1.0), reads=[r_krbf, r_zero], writes=[r_kr2])
                        P.op("pe", lambda q: q.matmul(
                            psm[0:1, 0:512], onescol[:, :], kr2[:], start=True,
                            stop=True), reads=[r_onescol, r_kr2], writes=[r_psm])
                        P.op("dve", lambda q: q.tensor_reduce(
                            out=kn1[:], in_=psm[0:1, 0:512], axis=AXX, op=ALU.max),
                            reads=[r_psm], writes=[r_kn1])
                        P.op("dve", lambda q: q.tensor_tensor(
                            out=kmax_acc[:], in0=kmax_acc[:], in1=kn1[:], op=ALU.max),
                            reads=[r_kn1], writes=[r_kmax])
                if debug:
                    P.op("syn", lambda q: [
                        q.dma_start(out=dbg["d_cosk"][:, :], in_=cosk_sb[:]),
                        q.dma_start(out=dbg["d_sink"][:, :], in_=sink_sb[:])],
                        reads=[r_cosk, r_sink], ndma=2)

        _phase2()
        P.phase_barrier()

        def _phase3():
            with ExitStack() as ph:
                def sbl(name, shape, dt):
                    t = ph.enter_context(nc.sbuf_tensor(name, list(shape), dt))
                    return t, Res()

                def psl(name, shape, dt):
                    t = ph.enter_context(nc.psum_tensor(name, list(shape), dt))
                    return t, Res()

                wv_sb, r_wv = sbl("a2w", (128, 32 * 1024), BF16)
                xnu = [sbl("a2x%d" % i, (128, 32 * 512), BF16) for i in range(2)]
                vbf = [sbl("a2v%d" % i, (128, 1024), BF16) for i in range(2)]
                psvs = [psl("a2p%d" % i, (128, 1024), F32) for i in range(2)]

                for j in range(8):
                    P.op("pool", lambda q, j=j: q.dma_start(
                        out=wv_sb[:].rearrange("p (ec o) -> p ec o", ec=32)[
                            :, :, 128 * j:128 * j + 128],
                        in_=wvt[:, 128 * j:128 * j + 128].rearrange(
                            "(ec p) o -> p ec o", p=128)),
                        writes_acc=[r_wv], ndma=1)
                for u in range(8):
                    bx, r_bx = xnu[u % 2]
                    P.op("syn", lambda q, u=u, bx=bx: [
                        q.dma_start(out=bx[:, 512 * ec:512 * ec + 512],
                                    in_=xnsp[ec, :, 512 * u:512 * u + 512])
                        for ec in range(32)],
                        reads=[r_xnsp_tc[t] for t in range(4 * u, 4 * u + 4)],
                        writes=[r_bx], ndma=32)
                    for tsub in range(4):
                        bv, r_bv = vbf[tsub % 2]
                        psv, r_psv = psvs[tsub % 2]
                        P.op("pe", lambda q, bx=bx, tsub=tsub, psv=psv: [
                            q.matmul(psv[:, 512 * oh:512 * oh + 512],
                                     bx[:, 512 * ec + 128 * tsub:512 * ec + 128 * tsub + 128],
                                     wv_sb[:, 1024 * ec + 512 * oh:1024 * ec + 512 * oh + 512],
                                     start=(ec == 0), stop=(ec == 31))
                            for oh in range(2) for ec in range(32)],
                            reads=[r_wv, r_bx], writes=[r_psv])
                        P.op("act", lambda q, bv=bv, psv=psv: q.copy(out=bv[:], in_=psv[:]),
                             reads=[r_psv], writes=[r_bv])
                        P.op("pool", lambda q, bv=bv, u=u, tsub=tsub: [
                            q.dma_start(out=vsp[kvh, :, 4 * u + tsub, :],
                                        in_=bv[:, 128 * kvh:128 * kvh + 128])
                            for kvh in range(KVH)],
                            reads=[r_bv], writes_acc=[r_vsp], ndma=KVH)

        _phase3()
        P.phase_barrier()

        def _phase4():
            with ExitStack() as ph:
                def sbl(name, shape, dt):
                    t = ph.enter_context(nc.sbuf_tensor(name, list(shape), dt))
                    return t, Res()

                def psl(name, shape, dt):
                    t = ph.enter_context(nc.psum_tensor(name, list(shape), dt))
                    return t, Res()

                xno_sb, r_xno = sbl("a3xo", (128, 32 * 1024), BF16)
                xt = [sbl("a3x%d" % i, (128, E), F32) for i in range(2)]
                xnt = [sbl("a3n%d" % i, (128, E), BF16) for i in range(2)]
                st48, r_st48 = sbl("a3s", (128, 48), F32)
                mv, r_mv = sbl("a3m", (128, 2), F32)
                rstd, r_rstd = sbl("a3r", (128, 1), F32)
                nmr, r_nmr = sbl("a3nm", (128, 1), F32)
                pst, r_pst = psl("a3p", (128, 1024), BF16)

                for sl in range(8):
                    bx, r_bx = xt[sl % 2]
                    bxn, r_bxn = xnt[sl % 2]
                    P.op("syn", lambda q, sl=sl, bx=bx: q.dma_start(
                        out=bx[:], in_=xown[128 * sl:128 * sl + 128, :]),
                        writes=[r_bx], ndma=1)
                    emit_ln(bx, r_bx, bxn, r_bxn, st48, r_st48, mv, r_mv,
                            rstd, r_rstd, nmr, r_nmr)
                    for g in range(4):
                        P.op("pe", lambda q, bxn=bxn, g=g: [
                            q.transpose(pst[:, 128 * j:128 * j + 128],
                                        bxn[:, 128 * (8 * g + j):128 * (8 * g + j) + 128],
                                        idb_sb[:, :]) for j in range(8)],
                            reads=[r_bxn, r_idb], writes=[r_pst])
                        P.op("dve", lambda q, g=g, sl=sl: [
                            q.tensor_copy(
                                xno_sb[:, 1024 * (8 * g + j) + 128 * sl:
                                       1024 * (8 * g + j) + 128 * sl + 128],
                                pst[:, 128 * j:128 * j + 128]) for j in range(8)],
                            reads=[r_pst], writes_acc=[r_xno])
                P.op("pool", lambda q: [
                    q.dma_start(out=xosp[ec, :, :],
                                in_=xno_sb[:, 1024 * ec:1024 * ec + 1024])
                    for ec in range(32)], reads=[r_xno], writes_acc=[r_xosp], ndma=32)

                wqb = [sbl("a3wb%d" % i, (128, 4096), BF16) for i in range(2)]
                cosq_sb, r_cosq = sbl("a3c", (128, 1024), F32)
                sinq_sb, r_sinq = sbl("a3si", (128, 1024), F32)
                tmpc, r_tmpc = sbl("a3tc", (128, 512), F32)
                tmps, r_tmps = sbl("a3ts", (128, 512), F32)
                qrbf, r_qrbf = sbl("a3qb", (128, 512), BF16)
                qr2, r_qr2 = sbl("a3q2", (128, 512), BF16)
                qn2row, r_qn2row = sbl("a3qn", (1, 1024), F32)
                psq = [psl("a3pq%d" % i, (128, 512), F32) for i in range(2)]
                psm, r_psm = psl("a3pm", (128, 512), F32)

                P.op("syn", lambda q: [q.dma_start(out=cosq_sb[:], in_=cosqf[:, :]),
                                       q.dma_start(out=sinq_sb[:], in_=sinqf[:, :])],
                     writes=[r_cosq, r_sinq], ndma=2)
                for oc in range(32):
                    bb, r_bb = wqb[oc % 2]
                    P.op("pool", lambda q, oc=oc, bb=bb: [
                        q.dma_start(out=bb[:, 1024 * jj:1024 * jj + 1024].rearrange(
                                        "p (ec o) -> p ec o", ec=8),
                                    in_=wqt[1024 * jj:1024 * jj + 1024,
                                            128 * oc:128 * oc + 128].rearrange(
                                        "(ec p) o -> p ec o", p=128))
                        for jj in range(4)], writes=[r_bb], ndma=4)
                    for th in range(2):
                        pq, r_pq = psq[th]
                        P.op("pe", lambda q, bb=bb, th=th, pq=pq: [
                            q.matmul(pq[:, 0:512],
                                     bb[:, 128 * ec:128 * ec + 128],
                                     xno_sb[:, 1024 * ec + 512 * th:1024 * ec + 512 * th + 512],
                                     start=(ec == 0), stop=(ec == 31))
                            for ec in range(32)], reads=[r_bb, r_xno], writes=[r_pq])
                        P.op("dve", lambda q, pq=pq, th=th: q.tensor_tensor(
                            out=tmpc[:], in0=pq[:, 0:512],
                            in1=cosq_sb[:, 512 * th:512 * th + 512], op=ALU.mult),
                            reads=[r_pq, r_cosq], writes=[r_tmpc])
                        P.op("dve", lambda q, pq=pq, th=th: [
                            q.tensor_tensor(out=tmps[0:64, :], in0=pq[64:128, 0:512],
                                            in1=sinq_sb[0:64, 512 * th:512 * th + 512],
                                            op=ALU.mult),
                            q.tensor_tensor(out=tmps[64:128, :], in0=pq[0:64, 0:512],
                                            in1=sinq_sb[64:128, 512 * th:512 * th + 512],
                                            op=ALU.mult)],
                            reads=[r_pq, r_sinq], writes=[r_tmps])
                        P.op("dve", lambda q: q.tensor_tensor(
                            out=qrbf[:], in0=tmpc[:], in1=tmps[:], op=ALU.add),
                            reads=[r_tmpc, r_tmps], writes=[r_qrbf])
                        P.op("pool", lambda q, oc=oc, th=th: q.dma_start(
                            out=qrsp[oc, :, 512 * th:512 * th + 512], in_=qrbf[:]),
                            reads=[r_qrbf], writes_acc=[r_qrsp], ndma=1)
                        P.op("act", lambda q: q.activation(
                            out=qr2[:], in_=qrbf[:], func=AF.Square, bias=zero_c[:],
                            scale=1.0), reads=[r_qrbf, r_zero], writes=[r_qr2])
                        P.op("pe", lambda q, th=th: q.matmul(
                            psm[0:1, 0:512], onescol[:, :], qr2[:], start=True,
                            stop=True), reads=[r_onescol, r_qr2], writes=[r_psm])
                        P.op("dve", lambda q, th=th: q.tensor_copy(
                            qn2row[0:1, 512 * th:512 * th + 512], psm[0:1, 0:512]),
                            reads=[r_psm], writes_acc=[r_qn2row])
                    P.op("pool", lambda q, oc=oc: q.dma_start(
                        out=qn2sp[oc:oc + 1, :], in_=qn2row[:]),
                        reads=[r_qn2row], writes_acc=[r_qn2sp], ndma=1)

                if debug:
                    P.op("syn", lambda q: [
                        q.dma_start(out=dbg["d_cosq"][:, :], in_=cosq_sb[:]),
                        q.dma_start(out=dbg["d_qn2r"][:, :], in_=qn2row[:])],
                        reads=[r_cosq, r_qn2row], ndma=2)

                psb, r_psb = psl("a3pb", (128, 512), F32)
                sqk, r_sqk = sbl("a3sq", (1, 1), F32)
                qnall, r_qnall = sbl("a3qa", (32, 1024), F32)
                sqn, r_sqn = sbl("a3sn", (128, 32), F32)

                P.op("act", lambda q: q.activation(
                    out=sqk[:], in_=kmax_acc[:], func=AF.Sqrt, bias=zero_c[0:1, :],
                    scale=1.0), reads=[r_kmax, r_zero], writes=[r_sqk])
                P.op("dve", lambda q: q.tensor_scalar(
                    out=sqk[:], in0=sqk[:], scalar1=-1.001 / SQD, scalar2=None,
                    op0=ALU.mult), writes=[r_sqk])
                P.op("pe", lambda q: q.matmul(psb[:, 64:65], ones1F[0:1, :],
                                              sqk[0:1, :], start=True, stop=True),
                     reads=[r_ones1F, r_sqk], writes=[r_psb])
                P.op("dve", lambda q: q.tensor_copy(negkb[:], psb[:, 64:65]),
                     reads=[r_psb], writes=[r_negkb])
                P.op("syn", lambda q: q.dma_start(out=qnall[:], in_=qn2sp[:, :]),
                     reads=[r_qn2sp], writes=[r_qnall], ndma=1)
                for s in range(NSLOT):
                    P.op("pe", lambda q, s=s: q.transpose(
                        psb[:, 96:128], qnall[0:32, 128 * s:128 * s + 128],
                        idf_sb[0:32, 0:32]),
                        reads=[r_qnall, r_idf], writes=[r_psb])
                    P.op("act", lambda q, s=s: q.activation(
                        out=sqn[:], in_=psb[:, 96:128], func=AF.Sqrt,
                        bias=zero_c[:], scale=1.0),
                        reads=[r_psb, r_zero], writes=[r_sqn])
                    P.op("dve", lambda q, s=s: q.tensor_scalar(
                        out=negb_sb[:, H * s:H * s + 32], in0=sqn[:],
                        scalar1=negkb[:], scalar2=None, op0=ALU.mult),
                        reads=[r_sqn, r_negkb], writes_acc=[r_negb])

        _phase4()
        P.phase_barrier()
        if debug:
            P.op("syn", lambda q: [
                q.dma_start(out=dbg["d_xn"][:, :, :], in_=xnsp[:, :, :]),
                q.dma_start(out=dbg["d_kr"][:, :, :], in_=krsp[:, :, :]),
                q.dma_start(out=dbg["d_v"][:, :, :, :], in_=vsp[:, :, :, :]),
                q.dma_start(out=dbg["d_qr"][:, :, :], in_=qrsp[:, :, :]),
                q.dma_start(out=dbg["d_qn2"][:, :], in_=qn2sp[:, :]),
                q.dma_start(out=dbg["d_xo"][:, :, :], in_=xosp[:, :, :]),
                q.dma_start(out=dbg["d_negb"][:, :], in_=negb_sb[:])],
                reads=[r_xnsp, r_krsp, r_vsp, r_qrsp, r_qn2sp, r_negb, r_xosp],
                ndma=7)

        def _phase5():
            with ExitStack() as ph:
                def sbl(name, shape, dt):
                    t = ph.enter_context(nc.sbuf_tensor(name, list(shape), dt))
                    return t, Res()

                def psl(name, shape, dt):
                    t = ph.enter_context(nc.psum_tensor(name, list(shape), dt))
                    return t, Res()

                sim_sb, r_sim = sbl("bsim", (128, TOT), BF16)
                kr_sb0, r_kr0 = sbl("bkr0", (128, S), BF16)
                kr_sb1, r_kr1 = sbl("bkr1", (128, S), BF16)
                v_sb0, r_v0 = sbl("bv0", (128, S), BF16)
                v_sb1, r_v1 = sbl("bv1", (128, S), BF16)
                KRV = [((kr_sb0, r_kr0), (v_sb0, r_v0)),
                       ((kr_sb1, r_kr1), (v_sb1, r_v1))]
                qr_sb, r_qr = sbl("bqr", (128, 1024), BF16)
                hw1b, r_hw1b = sbl("bh1b", (128, 128), BF16)
                hw2b, r_hw2b = sbl("bh2b", (128, 128), BF16)
                hb1_sb, r_hb1 = sbl("bhb1", (128, 1), F32)
                hb2_sb, r_hb2 = sbl("bhb2", (128, 1), F32)
                y1p_sb, r_y1p = sbl("by1p", (128, 1024), BF16)
                zp_sb, r_zp = sbl("bzp", (128, 1024), F32)
                y_sb, r_y = sbl("by", (128, 5120), BF16)
                hq_sb, r_hq = sbl("bhq", (128, 5120), BF16)
                p0a_sb, r_p0a = sbl("bp0a", (128, 4096), BF16)
                p0b_sb, r_p0b = sbl("bp0b", (128, 4096), BF16)
                P0 = [(p0a_sb, r_p0a), (p0b_sb, r_p0b)]
                ps_sb, r_ps = sbl("bps", (128, 4096), F16)
                sel_sb, r_sel = sbl("bsel", (128, 4096), BF16)
                p_sb0, r_p0r = sbl("bpA0", (128, 4096), BF16)
                p_sb1, r_p1r = sbl("bpA1", (128, 4096), BF16)
                PBUF = [(p_sb0, r_p0r), (p_sb1, r_p1r)]
                pt_sb, r_pt = sbl("bpt", (128, 4096), BF16)
                rr_sb, r_rr = sbl("brr", (128, 4096), F16)
                trash_a = sel_sb
                trash_d = ps_sb
                loT, r_lo = sbl("blo", (128, NSLOT), F32)
                midT, r_mid = sbl("bmid", (128, NSLOT), F32)
                ngm, r_ngm = sbl("bngm", (128, NSLOT), F32)
                accA, r_accA = sbl("baca", (128, NSLOT), F32)
                accD, r_accD = sbl("bacd", (128, NSLOT), F32)
                t2, r_t2 = sbl("bt2", (128, NSLOT), F32)
                geT, r_ge = sbl("bge", (128, NSLOT), F32)
                geb, r_geb = sbl("bgeb", (128, NSLOT), F32)
                cntT, r_cnt = sbl("bcnt", (128, NSLOT), F32)
                t3, r_t3 = sbl("bt3", (128, NSLOT), F32)
                chiT, r_chi = sbl("bchi", (128, NSLOT), F32)
                rmin, r_rmin = sbl("brm", (128, NSLOT), F32)
                zacc, r_zacc = sbl("bza", (128, NSLOT), F32)
                zrT, r_zr = sbl("bzr", (128, NSLOT), F32)
                cstg, r_cstg = sbl("bcs", (128, 1024), BF16)
                psA, r_psA = psl("bpA", (128, 2048), F32)
                psB, r_psB = psl("bpB", (128, 1024), F32)
                psC, r_psC = psl("bpC", (128, 512), F32)
                psT, r_psT = psl("bpT", (128, 1024), BF16)
                r_psTh = [Res(), Res()]

                sim2_sb, r_sim2 = sbl("bsim2", (128, TOT), BF16)
                qr2_sb, r_qr2 = sbl("bqr2", (128, 1024), BF16)
                SIM = [(sim_sb, r_sim), (sim2_sb, r_sim2)]
                QR = [(qr_sb, r_qr), (qr2_sb, r_qr2)]

                def front_load(h):
                    qrb, r_qrb = QR[h % 2]
                    kvh = h // GRP
                    (kr_sb, r_kr), (v_sb, r_v) = KRV[kvh % 2]
                    if h % GRP == 0:
                        P.op("syn", lambda q, kvh=kvh, kr_sb=kr_sb: q.dma_start(
                            out=kr_sb[:], in_=krsp[kvh, :, :]),
                            reads=[r_krsp], writes=[r_kr], ndma=1)
                        P.op("syn", lambda q, kvh=kvh, v_sb=v_sb: q.dma_start(
                            out=v_sb[:], in_=vsp[kvh, :, :, :].rearrange(
                                "p tc d -> p (tc d)")),
                            reads=[r_vsp], writes=[r_v], ndma=1)
                    P.op("syn", lambda q, h=h, qrb=qrb: [
                        q.dma_start(out=qrb[:], in_=qrsp[h, :, :]),
                        q.dma_start(out=hb1_sb[:], in_=hb1[h, :, :]),
                        q.dma_start(out=hb2_sb[:], in_=hb2[h, :, :])],
                        reads=[r_qrsp], writes=[r_qrb, r_hb1, r_hb2], ndma=3)
                    P.op("pool", lambda q, h=h: [
                        q.dma_start(out=hw1b[:], in_=hp1[h, :, :]),
                        q.dma_start(out=hw2b[:], in_=hp2[h, :, :])],
                        writes=[r_hw1b, r_hw2b], ndma=2)

                def front_hash(h):
                    qrb, r_qrb = QR[h % 2]
                    (kr_sb, r_kr), (v_sb, r_v) = KRV[(h // GRP) % 2]
                    P.op("dve", lambda q, qrb=qrb, kr_sb=kr_sb: [
                        q.tensor_copy(y_sb[:, 0:4096], kr_sb[:]),
                        q.tensor_copy(y_sb[:, 4096:5120], qrb[:])],
                        reads=[r_kr, r_qrb], writes=[r_y])
                    for pc in range(5):
                        base = 1024 * pc
                        bank = psA[:, 1024 * (pc % 2):1024 * (pc % 2) + 1024]
                        P.op("pe", lambda q, base=base, bank=bank: [
                            q.matmul(bank[:, 512 * j:512 * j + 512], hw1b[:, :],
                                     y_sb[:, base + 512 * j:base + 512 * j + 512],
                                     start=True, stop=True) for j in range(2)],
                            reads=[r_hw1b, r_y], writes=[r_psA])
                        P.op("act", lambda q, bank=bank: q.activation(
                            out=y1p_sb[:], in_=bank, func=AF.Silu, bias=hb1_sb[:],
                            scale=1.0), reads=[r_psA, r_hb1], writes=[r_y1p])
                        P.op("pool", lambda q, base=base: q.tensor_tensor(
                            out=y_sb[:, base:base + 1024], in0=y1p_sb[:],
                            in1=y_sb[:, base:base + 1024], op=ALU.add),
                            reads=[r_y1p], writes_acc=[r_y])
                        P.op("pe", lambda q, base=base, bank=bank: [
                            q.matmul(bank[:, 512 * j:512 * j + 512], hw2b[:, :],
                                     y_sb[:, base + 512 * j:base + 512 * j + 512],
                                     start=True, stop=True) for j in range(2)],
                            reads=[r_hw2b, r_y], writes=[r_psA])
                        P.op("act", lambda q, bank=bank: q.activation(
                            out=zp_sb[:], in_=bank, func=AF.Identity, bias=hb2_sb[:],
                            scale=1.0), reads=[r_psA, r_hb2], writes=[r_zp])
                        P.op("dve", lambda q, base=base: q.tensor_tensor(
                            out=zp_sb[:], in0=zp_sb[:],
                            in1=y_sb[:, base:base + 1024], op=ALU.add),
                            reads=[r_y], writes=[r_zp])
                        P.op("act", lambda q, base=base: q.activation(
                            out=hq_sb[:, base:base + 1024], in_=zp_sb[:],
                            func=AF.Sign, bias=zero_c[:], scale=1.0),
                            reads=[r_zp, r_zero], writes_acc=[r_hq])

                def front_sim(h, s):
                    simb, r_simb = SIM[h % 2]
                    if True:
                        Es = EXT[s]
                        qsl = hq_sb[:, 4096 + 128 * s:4096 + 128 * s + 128]
                        for piece in range(0, Es, 2048):
                            plen = min(2048, Es - piece)
                            nj = plen // 512

                            def mmfn(q, s=s, piece=piece, nj=nj, Es=Es, qsl=qsl):
                                ins = []
                                for j in range(nj):
                                    t0 = piece + 512 * j
                                    is_diag = (t0 == Es - 512)
                                    ins.append(q.matmul(
                                        psA[:, 512 * j:512 * j + 512], qsl,
                                        hq_sb[:, t0:t0 + 512],
                                        start=True, stop=not is_diag))
                                    if is_diag:
                                        ins.append(q.matmul(
                                            psA[:, 512 * j:512 * j + 512],
                                            idb_sb[:, :], maskc_sb[:, :],
                                            start=False, stop=True))
                                return ins
                            P.op("pe", mmfn, reads=[r_hq, r_idb, r_maskc],
                                 writes=[r_psA])
                            P.op("act", lambda q, s=s, piece=piece, plen=plen,
                                 simb=simb:
                                 q.activation(
                                     out=simb[:, OFF[s] + piece:OFF[s] + piece + plen],
                                     in_=psA[:, 0:plen], func=AF.Identity,
                                     bias=zero_c[:], scale=1.0),
                                 reads=[r_psA, r_zero], writes_acc=[r_simb])
                    if debug and h == 0 and s == NSLOT - 1:
                        P.op("syn", lambda q, simb=simb: q.dma_start(
                            out=dbg["d_sim"][:, :], in_=simb[:]),
                            reads=[r_simb], ndma=1)

                def front(h):
                    front_load(h)
                    front_hash(h)
                    for s in range(NSLOT):
                        front_sim(h, s)

                def mid_a(h):
                    simb, r_simb = SIM[h % 2]
                    P.op("dve", lambda q: [q.memset(loT[:], -130.0),
                                           q.memset(chiT[:], 0.0)],
                         writes=[r_lo, r_chi])
                    for rnd, step in enumerate(STEPS):
                        P.op("dve", lambda q, step=step: q.tensor_scalar(
                            out=midT[:], in0=loT[:], scalar1=step, scalar2=None,
                            op0=ALU.add), reads=[r_lo], writes=[r_mid])
                        P.op("dve", lambda q: q.tensor_scalar(
                            out=ngm[:], in0=midT[:], scalar1=-1.0, scalar2=0.5,
                            op0=ALU.mult, op1=ALU.add),
                            reads=[r_mid], writes=[r_ngm])
                        P.op("act", lambda q, simb=simb: [
                            q.activation(out=trash_a[:, 0:SPL[s]],
                                         in_=simb[:, OFF[s]:OFF[s] + SPL[s]],
                                         func=AF.Sign, bias=ngm[:, s:s + 1],
                                         scale=1.0,
                                         accum_out=accA[:, s:s + 1])
                            for s in range(1, NSLOT)] + [
                            q.activation(out=trash_a[:, 0:512],
                                         in_=simb[:, 0:512], func=AF.Sign,
                                         bias=zero_c[:], scale=1.0)],
                            reads=[r_simb, r_ngm], writes=[r_accA])
                        P.op("dve", lambda q, simb=simb: [
                            q.tensor_scalar(out=trash_d[:, 0:EXT[s] - SPL[s]],
                                            in0=simb[:, OFF[s] + SPL[s]:OFF[s] + EXT[s]],
                                            scalar1=midT[:, s:s + 1], scalar2=None,
                                            op0=ALU.is_ge, op1=ALU.add,
                                            accum_out=accD[:, s:s + 1])
                            for s in range(1, NSLOT)] + [
                            q.tensor_copy(trash_d[:, 0:512], simb[:, 0:512])],
                            reads=[r_simb, r_mid], writes=[r_accD])
                        P.op("dve", lambda q: q.scalar_tensor_tensor(
                            out=t2[:], in0=accD[:], scalar=2.0, in1=accA[:],
                            op0=ALU.mult, op1=ALU.add),
                            reads=[r_accD, r_accA], writes=[r_t2])
                        P.op("dve", lambda q: q.tensor_tensor(
                            out=geT[:], in0=t2[:], in1=cc_sb[:], op=ALU.is_ge),
                            reads=[r_t2, r_cc], writes=[r_ge])
                        P.op("dve", lambda q, step=step: q.scalar_tensor_tensor(
                            out=loT[:], in0=geT[:], scalar=step, in1=loT[:],
                            op0=ALU.mult, op1=ALU.add),
                            reads=[r_ge], writes=[r_lo])
                        P.op("dve", lambda q: q.scalar_tensor_tensor(
                            out=cntT[:], in0=t2[:], scalar=0.5, in1=cr_sb[:],
                            op0=ALU.mult, op1=ALU.add),
                            reads=[r_t2, r_cr], writes=[r_cnt])
                        P.op("dve", lambda q: q.tensor_scalar(
                            out=geb[:], in0=geT[:], scalar1=-1.0, scalar2=1.0,
                            op0=ALU.mult, op1=ALU.add),
                            reads=[r_ge], writes=[r_geb])
                        P.op("dve", lambda q: q.tensor_tensor(
                            out=t3[:], in0=cntT[:], in1=chiT[:], op=ALU.subtract),
                            reads=[r_cnt, r_chi], writes=[r_t3])
                        P.op("dve", lambda q: q.tensor_tensor(
                            out=t3[:], in0=t3[:], in1=geb[:], op=ALU.mult),
                            reads=[r_geb], writes=[r_t3])
                        P.op("dve", lambda q: q.tensor_tensor(
                            out=chiT[:], in0=chiT[:], in1=t3[:], op=ALU.add),
                            reads=[r_t3], writes=[r_chi])
                    P.op("dve", lambda q: q.scalar_tensor_tensor(
                        out=rmin[:], in0=loT[:], scalar=-4096.0, in1=chiT[:],
                        op0=ALU.mult, op1=ALU.subtract),
                        reads=[r_lo, r_chi], writes=[r_rmin])
                    P.op("dve", lambda q: q.tensor_scalar(
                        out=rmin[:], in0=rmin[:], scalar1=820.0, scalar2=None,
                        op0=ALU.add), writes=[r_rmin])
                    P.op("dve", lambda q: [q.memset(loT[:, 0:1], -130.0),
                                           q.memset(rmin[:, 0:1], 533300.0)],
                         writes_acc=[r_lo, r_rmin])
                    if debug and h == 0:
                        P.op("syn", lambda q: [
                            q.dma_start(out=dbg["d_acc"][:, 0:NSLOT], in_=accA[:]),
                            q.dma_start(out=dbg["d_acc"][:, NSLOT:2 * NSLOT],
                                        in_=accD[:])],
                            reads=[r_accA, r_accD], ndma=2)
                        P.op("syn", lambda q: [
                            q.dma_start(out=dbg["d_book"][:, 0:NSLOT], in_=loT[:]),
                            q.dma_start(out=dbg["d_book"][:, NSLOT:2 * NSLOT],
                                        in_=rmin[:])],
                            reads=[r_lo, r_rmin], ndma=2)

                def mid_b(h):
                    simb, r_simb = SIM[h % 2]
                    qrb, r_qrb = QR[h % 2]
                    (kr_sb, r_kr), (v_sb, r_v) = KRV[(h // GRP) % 2]

                    def scores_exp(s):
                        Es = EXT[s]
                        ncol = H * s + h
                        p0_sb, r_p0 = P0[s % 2]
                        qsl = qrb[:, 128 * s:128 * s + 128]
                        for pb in range(0, Es, 1024):
                            plen = min(1024, Es - pb)
                            P.op("pe", lambda q, qsl=qsl, pb=pb, plen=plen,
                                 kr_sb=kr_sb: [
                                q.matmul(psB[:, 512 * j:512 * j + 512], qsl,
                                         kr_sb[:, pb + 512 * j:pb + 512 * j + 512],
                                         start=True, stop=True)
                                for j in range(plen // 512)],
                                reads=[r_qrb, r_kr], writes=[r_psB])
                            P.op("act", lambda q, pb=pb, plen=plen, ncol=ncol,
                                 p0_sb=p0_sb:
                                 q.activation(
                                     out=p0_sb[:, pb:pb + plen], in_=psB[:, 0:plen],
                                     func=AF.Exp, scale=1.0 / SQD,
                                     bias=negb_sb[:, ncol:ncol + 1]),
                                 reads=[r_psB, r_negb], writes_acc=[r_p0])

                    def chain(s):
                        Es = EXT[s]
                        p_sb, r_p = PBUF[s % 2]
                        p0_sb, r_p0 = P0[s % 2]
                        eq_sb, r_eq = p_sb, r_p
                        if s == 0:
                            P.op("dve", lambda q, Es=Es, simb=simb: q.tensor_scalar(
                                out=sel_sb[:, 0:Es], in0=simb[:, OFF[0]:OFF[0] + Es],
                                scalar1=-129.0, scalar2=None, op0=ALU.is_ge),
                                reads=[r_simb], writes=[r_sel])
                        else:
                            P.op("dve", lambda q, s=s, Es=Es, simb=simb: q.tensor_scalar(
                                out=eq_sb[:, 0:Es], in0=simb[:, OFF[s]:OFF[s] + Es],
                                scalar1=loT[:, s:s + 1], scalar2=None, op0=ALU.is_equal),
                                reads=[r_simb, r_lo], writes=[r_eq])
                            P.op("dve", lambda q, Es=Es: q.tensor_tensor_scan(
                                out=ps_sb[:, 0:Es], data0=eq_sb[:, 0:Es],
                                data1=eq_sb[:, 0:Es], initial=0.0, op0=ALU.add,
                                op1=ALU.bypass), reads=[r_eq], writes=[r_ps])
                            P.op("dve", lambda q, s=s, Es=Es, simb=simb: q.tensor_scalar(
                                out=rr_sb[:, 0:Es], in0=simb[:, OFF[s]:OFF[s] + Es],
                                scalar1=4096.0, scalar2=rmin[:, s:s + 1], op0=ALU.mult,
                                op1=ALU.add), reads=[r_simb, r_rmin], writes=[r_rr])
                            P.op("dve", lambda q, Es=Es: q.tensor_tensor(
                                out=sel_sb[:, 0:Es], in0=ps_sb[:, 0:Es],
                                in1=rr_sb[:, 0:Es], op=ALU.is_le),
                                reads=[r_ps, r_rr], writes=[r_sel])
                        P.op("dve", lambda q, s=s, Es=Es: q.scalar_tensor_tensor(
                            out=p_sb[:, 0:Es], in0=p0_sb[:, 0:Es], scalar=1.0,
                            in1=sel_sb[:, 0:Es], op0=ALU.mult, op1=ALU.mult,
                            accum_out=zacc[:, s:s + 1]),
                            reads=[r_p0, r_sel], writes=[r_p], writes_acc=[r_zacc])
                        P.op("dve", lambda q, simb=simb: q.tensor_copy(
                            trash_d[:, 0:512], simb[:, 0:512]),
                            reads=[r_simb])
                        P.op("dve", lambda q, s=s: q.reciprocal(
                            zrT[:, s:s + 1], zacc[:, s:s + 1]),
                            reads=[r_zacc], writes_acc=[r_zr])
                        P.op("dve", lambda q, s=s, Es=Es: q.tensor_scalar(
                            out=p_sb[:, 0:Es], in0=p_sb[:, 0:Es],
                            scalar1=zrT[:, s:s + 1], scalar2=None, op0=ALU.mult),
                            reads=[r_zr], writes=[r_p])

                    def tail(s):
                        Es = EXT[s]
                        p_sb, r_p = PBUF[s % 2]
                        for pb in range(0, Es, 1024):
                            plen = min(1024, Es - pb)
                            P.op("pe", lambda q, pb=pb, plen=plen, p_sb=p_sb: [
                                q.transpose(psT[:, 128 * j:128 * j + 128],
                                            p_sb[:, pb + 128 * j:pb + 128 * j + 128],
                                            idb_sb[:, :])
                                for j in range(plen // 128)],
                                reads=[r_p, r_idb], writes=[r_psT])
                            if (pb // 1024) % 2 == 0:
                                P.op("act", lambda q, pb=pb, plen=plen: q.copy(
                                    out=pt_sb[:, pb:pb + plen], in_=psT[:, 0:plen]),
                                    reads=[r_psT], writes_acc=[r_pt])
                            else:
                                P.op("dve", lambda q, pb=pb, plen=plen: q.tensor_copy(
                                    pt_sb[:, pb:pb + plen], psT[:, 0:plen]),
                                    reads=[r_psT], writes_acc=[r_pt])
                        ntc = Es // 128

                        def avfn(q, ntc=ntc, v_sb=v_sb):
                            return [q.matmul(
                                psC[:, 0:128], v_sb[:, 128 * tc:128 * tc + 128],
                                pt_sb[:, 128 * tc:128 * tc + 128],
                                start=(tc == 0), stop=(tc == ntc - 1))
                                for tc in range(ntc)]
                        P.op("pe", avfn, reads=[r_v, r_pt], writes=[r_psC])
                        P.op("dve", lambda q, s=s: q.tensor_copy(
                            cstg[:, 128 * s:128 * s + 128], psC[:, 0:128]),
                            reads=[r_psC], writes_acc=[r_cstg])

                    nx = h + 1 < n_heads
                    sim_order = [7, 6, 5, 4, 3, 2, 1, 0]
                    scores_exp(0)
                    for s in range(NSLOT):
                        if s + 1 < NSLOT:
                            scores_exp(s + 1)
                        chain(s)
                        if nx and s == 0:
                            front_load(h + 1)
                        if nx and s == 1:
                            front_hash(h + 1)
                        if nx and 2 <= s:
                            front_sim(h + 1, sim_order[s - 2])
                        if s > 0:
                            tail(s - 1)
                    tail(NSLOT - 1)
                    if nx:
                        for s2 in sim_order[6:]:
                            front_sim(h + 1, s2)
                    if debug and h == 0:
                        P.op("syn", lambda q: [
                            q.dma_start(out=dbg["d_sel"][:, :], in_=sel_sb[:]),
                            q.dma_start(out=dbg["d_p"][:, :],
                                        in_=PBUF[(NSLOT - 1) % 2][0][:])],
                            reads=[r_sel, PBUF[(NSLOT - 1) % 2][1]], ndma=2)
                    P.op("pool", lambda q, h=h: q.dma_start(
                        out=ctxsp[h, :, :], in_=cstg[:]),
                        reads=[r_cstg], writes_acc=[r_ctxsp], ndma=1)

                front(0)
                for h in range(n_heads):
                    mid_a(h)
                    mid_b(h)
        _phase5()
        P.phase_barrier()

        if debug:
            P.op("syn", lambda q: q.dma_start(
                out=dbg["d_ctx"][:, :, :], in_=ctxsp[:, :, :]),
                reads=[r_ctxsp], ndma=1)
        def _phase6():
            with ExitStack() as ph:
                def sbl(name, shape, dt):
                    t = ph.enter_context(nc.sbuf_tensor(name, list(shape), dt))
                    return t, Res()

                def psl(name, shape, dt):
                    t = ph.enter_context(nc.psum_tensor(name, list(shape), dt))
                    return t, Res()

                acomb, bcomb = [], []
                for g in range(4):
                    acomb.append(sbl("acomb%d" % g, (128, 1024), BF16))
                    bcomb.append(sbl("bcomb%d" % g, (64, E), BF16))
                last_b, r_lastb = sbl("last_b", (128, 1024), BF16)
                fbias_sb, r_fbias = sbl("fbias_sb", (1, E), BF16)
                pcs = [psl("cp%d" % s, (128, 512), F32) for s in range(8)]
                cm1 = ExitStack()
                def sbl1(name, shape, dt):
                    t = cm1.enter_context(nc.sbuf_tensor(name, list(shape), dt))
                    return t, Res()
                gbc_sb, r_gbc = sbl1("cg", (128, 64), F32)
                g1, r_g1 = sbl1("cg1", (1, 64), F32)
                lal, r_lal = sbl1("cla", (128, 1024), F32)
                lbl, r_lbl = sbl1("clb", (32, E // 2), F32)

                P.op("syn", lambda q: q.dma_start(out=g1[:], in_=gateown[:, :]),
                     writes=[r_g1], ndma=1)
                P.op("pe", lambda q: q.matmul(pcs[0][0][:, 0:64], ones1F[0:1, :],
                                              g1[0:1, :], start=True, stop=True),
                     reads=[r_ones1F, r_g1], writes=[pcs[0][1]])
                P.op("dve", lambda q: q.tensor_copy(gbc_sb[:], pcs[0][0][:, 0:64]),
                     reads=[pcs[0][1]], writes=[r_gbc])
                for g in range(4):
                    P.op("dve", lambda q, g=g: q.memset(acomb[g][0][:], 0.0),
                         writes=[acomb[g][1]])
                    P.op("dve", lambda q, g=g: q.memset(bcomb[g][0][0:32, :], 0.0),
                         writes=[bcomb[g][1]])
                for n in range(16):
                    P.op("syn", lambda q, n=n: q.dma_start(out=lal[:], in_=lat[n, :, :]),
                         writes=[r_lal], ndma=1)
                    for g in range(4):
                        P.op("dve", lambda q, g=g, n=n: q.scalar_tensor_tensor(
                            out=acomb[g][0][:], in0=lal[:],
                            scalar=gbc_sb[:, 16 * g + n:16 * g + n + 1],
                            in1=acomb[g][0][:], op0=ALU.mult, op1=ALU.add),
                            reads=[r_lal, r_gbc], writes=[acomb[g][1]])
                    for hh in range(2):
                        P.op("syn", lambda q, n=n, hh=hh: q.dma_start(
                            out=lbl[:], in_=lbt[n, :, 2048 * hh:2048 * hh + 2048]),
                            writes=[r_lbl], ndma=1)
                        for g in range(4):
                            P.op("dve", lambda q, g=g, n=n, hh=hh: q.scalar_tensor_tensor(
                                out=bcomb[g][0][0:32, 2048 * hh:2048 * hh + 2048],
                                in0=lbl[:],
                                scalar=gbc_sb[0:32, 16 * g + n:16 * g + n + 1],
                                in1=bcomb[g][0][0:32, 2048 * hh:2048 * hh + 2048],
                                op0=ALU.mult, op1=ALU.add),
                                reads=[r_lbl, r_gbc], writes=[bcomb[g][1]])
                P.op("syn", lambda q: q.dma_start(out=lal[:], in_=last[:, :]),
                     writes=[r_lal], ndma=1)
                P.op("act", lambda q: q.copy(out=last_b[:], in_=lal[:]),
                     reads=[r_lal], writes=[r_lastb])
                for hh in range(2):
                    P.op("syn", lambda q, hh=hh: q.dma_start(
                        out=lbl[:], in_=lbst[:, 2048 * hh:2048 * hh + 2048]),
                        writes=[r_lbl], ndma=1)
                    for g in range(4):
                        P.op("act", lambda q, g=g, hh=hh: q.copy(
                            out=bcomb[g][0][32:64, 2048 * hh:2048 * hh + 2048],
                            in_=lbl[:]),
                            reads=[r_lbl], writes_acc=[bcomb[g][1]])
                for hh in range(2):
                    P.op("syn", lambda q, hh=hh: q.dma_start(
                        out=lbl[0:1, :], in_=fbias[:, 2048 * hh:2048 * hh + 2048]),
                        writes=[r_lbl], ndma=1)
                    P.op("act", lambda q, hh=hh: q.copy(
                        out=fbias_sb[0:1, 2048 * hh:2048 * hh + 2048],
                        in_=lbl[0:1, :]),
                        reads=[r_lbl], writes_acc=[r_fbias])
                cm1.close()

                ctx_sb, r_ctx = sbl("cctx", (128, 32 * 1024), BF16)
                xno_sb, r_xno2 = sbl("cxno", (128, 32 * 1024), BF16)
                axt_sb, r_axt = sbl("caxt", (64, 8 * 128), BF16)
                wob = [sbl("cwob%d" % i, (128, 512), BF16) for i in range(2)]
                knb = [sbl("cknb%d" % i, (128, 512), BF16) for i in range(2)]
                xog = [sbl("cxo%d" % i, (128, 512), F32) for i in range(2)]
                ost = [sbl("cos%d" % i, (128, 512), F32) for i in range(2)]

                P.op("syn", lambda q: [
                    q.dma_start(out=ctx_sb[:, 1024 * hh:1024 * hh + 1024],
                                in_=ctxsp[hh, :, :]) for hh in range(H)],
                    reads=[r_ctxsp], writes=[r_ctx], ndma=H)
                P.op("syn", lambda q: [
                    q.dma_start(out=xno_sb[:, 1024 * ec:1024 * ec + 1024],
                                in_=xosp[ec, :, :]) for ec in range(32)],
                    reads=[r_xosp], writes=[r_xno2], ndma=32)

                for s in range(NSLOT):
                    g = s // 2
                    pc, r_pc = pcs[0]
                    def axfn(q, s=s, g=g, pc=pc):
                        ins = []
                        for ic in range(32):
                            ins.append(q.matmul(
                                pc[0:32, 0:128], acomb[g][0][:, 32 * ic:32 * ic + 32],
                                xno_sb[:, 1024 * ic + 128 * s:1024 * ic + 128 * s + 128],
                                start=(ic == 0), stop=(ic == 31)))
                            ins.append(q.matmul(
                                pc[32:64, 0:128], last_b[:, 32 * ic:32 * ic + 32],
                                xno_sb[:, 1024 * ic + 128 * s:1024 * ic + 128 * s + 128],
                                start=(ic == 0), stop=(ic == 31)))
                        return ins
                    P.op("pe", axfn, reads=[acomb[g][1], r_lastb, r_xno2],
                         writes=[r_pc])
                    P.op("dve", lambda q, s=s, pc=pc: q.tensor_copy(
                        axt_sb[:, 128 * s:128 * s + 128], pc[0:64, 0:128]),
                        reads=[r_pc], writes_acc=[r_axt])
                if debug:
                    P.op("syn", lambda q: [
                        q.dma_start(out=dbg["d_acomb"][g, :, :], in_=acomb[g][0][:])
                        for g in range(4)] + [
                        q.dma_start(out=dbg["d_bcomb"][g, :, :], in_=bcomb[g][0][:])
                        for g in range(4)] + [
                        q.dma_start(out=dbg["d_axt"][:, :], in_=axt_sb[:]),
                        q.dma_start(out=dbg["d_lastb"][:, :], in_=last_b[:])],
                        reads=[acomb[g][1] for g in range(4)] +
                              [bcomb[g][1] for g in range(4)] +
                              [r_axt, r_lastb], ndma=10)

                for ec in range(8):
                    for hc in range(32):
                        bb, r_bb = wob[hc % 2]
                        P.op("pool", lambda q, ec=ec, hc=hc, bb=bb: q.dma_start(
                            out=bb[:], in_=wot[128 * hc:128 * hc + 128,
                                               512 * ec:512 * ec + 512]),
                            writes=[r_bb], ndma=1)
                        P.op("pe", lambda q, hc=hc, bb=bb: [
                            q.matmul(pcs[s][0][:, 0:512],
                                     ctx_sb[:, 1024 * hc + 128 * s:1024 * hc + 128 * s + 128],
                                     bb[:], start=(hc == 0), stop=False)
                            for s in range(8)],
                            reads=[r_bb, r_ctx], writes=[pcs[s][1] for s in range(8)])
                    for ic in range(32):
                        bb, r_bb = knb[ic % 2]
                        P.op("pool", lambda q, ec=ec, ic=ic, bb=bb: q.dma_start(
                            out=bb[:], in_=fker[128 * ic:128 * ic + 128,
                                                512 * ec:512 * ec + 512]),
                            writes=[r_bb], ndma=1)
                        P.op("pe", lambda q, ic=ic, bb=bb: [
                            q.matmul(pcs[s][0][:, 0:512],
                                     xno_sb[:, 1024 * ic + 128 * s:1024 * ic + 128 * s + 128],
                                     bb[:], start=False, stop=False)
                            for s in range(8)],
                            reads=[r_bb, r_xno2], writes=[pcs[s][1] for s in range(8)])
                    P.op("pe", lambda q, ec=ec: [
                        q.matmul(pcs[s][0][:, 0:512], axt_sb[:, 128 * s:128 * s + 128],
                                 bcomb[s // 2][0][:, 512 * ec:512 * ec + 512],
                                 start=False, stop=False) for s in range(8)],
                        reads=[r_axt] + [bcomb[g][1] for g in range(4)],
                        writes=[pcs[s][1] for s in range(8)])
                    P.op("pe", lambda q, ec=ec: [
                        q.matmul(pcs[s][0][:, 0:512], ones1b[0:1, :],
                                 fbias_sb[0:1, 512 * ec:512 * ec + 512],
                                 start=False, stop=True) for s in range(8)],
                        reads=[r_ones1b, r_fbias],
                        writes=[pcs[s][1] for s in range(8)])
                    for s in range(8):
                        bx, r_bx = xog[s % 2]
                        bo, r_bo = ost[s % 2]
                        P.op("syn", lambda q, s=s, ec=ec, bx=bx: q.dma_start(
                            out=bx[:], in_=xown[128 * s:128 * s + 128,
                                                512 * ec:512 * ec + 512]),
                            writes=[r_bx], ndma=1)
                        P.op("dve", lambda q, s=s, bx=bx, bo=bo: q.tensor_tensor(
                            out=bo[:], in0=pcs[s][0][:, 0:512], in1=bx[:],
                            op=ALU.add), reads=[pcs[s][1], r_bx], writes=[r_bo])
                        P.op("syn", lambda q, s=s, ec=ec, bo=bo: q.dma_start(
                            out=out[128 * s:128 * s + 128, 512 * ec:512 * ec + 512],
                            in_=bo[:]), reads=[r_bo], writes_acc=[r_out], ndma=1)

                P.op("syn", lambda q: q.nop(), reads=[r_out])

        _phase6()
        block = ctx.enter_context(nc.Block())
        P.emit(block)

    return nc


_NC_CACHE = None


def _get_program():
    global _NC_CACHE
    if _NC_CACHE is None:
        _NC_CACHE = build_program()
    return _NC_CACHE


def _host_tables(kcore):
    inv_freq = 1.0 / (10000.0 ** (np.arange(0, D, 2, dtype=np.float32) / D))
    t = np.arange(S, dtype=np.float32)
    freqs = np.outer(t, inv_freq)
    emb = np.concatenate([freqs, freqs], -1)
    cos = np.cos(emb).astype(np.float32)
    sin = np.sin(emb).astype(np.float32)
    cosT = np.ascontiguousarray(cos.T)
    sinT = sin.T
    sinF = sinT.copy()
    sinF[0:64] = -sinF[0:64]
    sinF = np.ascontiguousarray(sinF)
    rows = np.concatenate([np.arange(128 * (4 * s + kcore),
                                     128 * (4 * s + kcore) + 128)
                           for s in range(NSLOT)])
    cosq = np.ascontiguousarray(cosT[:, rows])
    sinq = np.ascontiguousarray(sinF[:, rows])
    pp, jj = np.meshgrid(np.arange(128), np.arange(512), indexing="ij")
    mk = np.where(jj <= 128 * kcore + pp, 0.0, -10000.0).astype(np.float32)
    import ml_dtypes
    return (cosT, sinF, cosq, sinq, rows,
            np.eye(128, dtype=np.float32).astype(ml_dtypes.bfloat16),
            mk.astype(ml_dtypes.bfloat16))


def kernel(**inputs):
    x = np.asarray(inputs["L0_attn_hidden_states"], np.float32)
    Wq = np.ascontiguousarray(np.asarray(inputs["L0_attn_Wq"], np.float32).T)
    Wk = np.ascontiguousarray(np.asarray(inputs["L0_attn_Wk"], np.float32).T)
    Wv = np.ascontiguousarray(np.asarray(inputs["L0_attn_Wv"], np.float32).T)
    Wo = np.ascontiguousarray(np.asarray(inputs["L0_attn_Wo"], np.float32).T)
    fker = np.ascontiguousarray(np.asarray(inputs["L0_ffn_kernel"], np.float32))
    fbias = np.asarray(inputs["L0_ffn_bias"], np.float32).reshape(1, E)
    hp1 = np.ascontiguousarray(np.asarray(inputs["L0_attn_hash_proj1"], np.float32))
    hp2 = np.ascontiguousarray(np.asarray(inputs["L0_attn_hash_proj2"], np.float32))
    hb1 = np.ascontiguousarray(
        np.asarray(inputs["L0_attn_hash_bias1"], np.float32).transpose(0, 2, 1))
    hb2 = np.ascontiguousarray(
        np.asarray(inputs["L0_attn_hash_bias2"], np.float32).transpose(0, 2, 1))
    gate = np.asarray(inputs["L0_ffn_gate"], np.float32)
    lA = np.asarray(inputs["L0_ffn_lora_A"], np.float32)
    lB = np.asarray(inputs["L0_ffn_lora_B"], np.float32)
    lAs = np.asarray(inputs["L0_ffn_lora_A_shared"], np.float32)
    lBs = np.asarray(inputs["L0_ffn_lora_B_shared"], np.float32)

    lat = np.ascontiguousarray(
        lA.transpose(0, 2, 1).reshape(16, 32, 128, 32).transpose(0, 2, 1, 3)
        .reshape(16, 128, 1024))
    last = np.ascontiguousarray(
        lAs.T.reshape(32, 128, 32).transpose(1, 0, 2).reshape(128, 1024))
    lbt = np.ascontiguousarray(lB.transpose(0, 2, 1))
    lbst = np.ascontiguousarray(lBs.T)

    cc = np.broadcast_to(
        np.array([1e9] + [1640.0 - SPL[s] for s in range(1, NSLOT)], np.float32),
        (128, NSLOT)).copy()
    cr = np.broadcast_to(
        np.array([0.0] + [SPL[s] / 2.0 for s in range(1, NSLOT)], np.float32),
        (128, NSLOT)).copy()

    nc = _get_program()
    in_maps = []
    rows_all = []
    for c in range(8):
        b, k = c // 4, c % 4
        cosT, sinF, cosq, sinq, rows, identb, maskc = _host_tables(k)
        rows_all.append(rows)
        xb = np.ascontiguousarray(x[b])
        in_maps.append({
            "xfull": xb,
            "xown": np.ascontiguousarray(xb[rows]),
            "wqt": Wq, "wkt": Wk, "wvt": Wv, "wot": Wo,
            "fker": fker, "fbias": fbias,
            "hp1": hp1, "hb1": hb1, "hp2": hp2, "hb2": hb2,
            "lat": lat, "last": last, "lbt": lbt, "lbst": lbst,
            "gateown": np.ascontiguousarray(
                gate[4 * b:4 * b + 4].reshape(1, 64)),
            "coskf": np.ascontiguousarray(cosT),
            "sinkf": np.ascontiguousarray(sinF),
            "cosqf": cosq, "sinqf": sinq,
            "identb": identb,
            "identf": np.eye(128, dtype=np.float32), "maskc": maskc,
            "cc1640": cc, "cr820": cr,
        })
    res = run_bass_kernel_spmd(nc, in_maps, core_ids=list(range(8)))
    outf = np.empty((B, S, E), np.float32)
    for c in range(8):
        outf[c // 4][rows_all[c]] = res.results[c]["out"]
    return outf


# version 15 (final)
# speedup vs baseline: 1.2062x; 1.0028x over previous
import numpy as np
from contextlib import ExitStack

import concourse.bass as bass
import concourse.mybir as mybir
from concourse.bass_utils import run_bass_kernel_spmd

F32 = mybir.dt.float32
BF16 = mybir.dt.bfloat16
F16 = mybir.dt.float16
AF = mybir.ActivationFunctionType
ALU = mybir.AluOpType
AXX = mybir.AxisListType.X

B, S, E, H, KVH, D = 2, 4096, 4096, 32, 8, 128
GRP = H // KVH
NSLOT = 8
EXT = [512 * (s + 1) for s in range(NSLOT)]
OFF = [0]
for _e in EXT:
    OFF.append(OFF[-1] + _e)
TOT = OFF[-1]
NUM_REMAIN = 820
SQD = float(np.sqrt(D))
STEPS = [128.0, 64.0, 32.0, 16.0, 8.0, 4.0, 2.0]
SPL = [0, 496, 736, 992, 1232, 1472, 1728, 1968]


class Res:
    __slots__ = ("writers", "readers")

    def __init__(self):
        self.writers = []
        self.readers = []


class Prog:
    ENGS = ("pe", "dve", "act", "pool", "syn")
    PARANOID = False

    def __init__(self, nc, ctx):
        self.nc = nc
        self.sem = {e: ctx.enter_context(nc.semaphore("sem_" + e)) for e in self.ENGS}
        self.cnt = {e: 0 for e in self.ENGS}
        self.sched = {e: [] for e in self.ENGS}
        self.waited = {e: {} for e in self.ENGS}
        self.barrier = []
        self.pad_sb = ctx.enter_context(nc.sbuf_tensor("pe_pad", [128, 1024], BF16))

    def op(self, eng, fn, reads=(), writes=(), writes_acc=(), ndma=0):
        deps = {}

        def add(tok):
            if tok is None:
                return
            e2, v2 = tok
            if e2 == eng and eng == "pe":
                return
            if deps.get(e2, 0) < v2:
                deps[e2] = v2

        for t in self.barrier:
            add(t)
        if Prog.PARANOID:
            for e2 in self.ENGS:
                if self.cnt[e2] > 0 and e2 != eng:
                    add((e2, self.cnt[e2]))
        for r in reads:
            for t in r.writers:
                add(t)
        for w in writes:
            for t in w.writers:
                add(t)
            for t in w.readers:
                add(t)
        for w in writes_acc:
            for t in w.readers:
                add(t)
        waits = []
        wt = self.waited[eng]
        for e2 in self.ENGS:
            v2 = deps.get(e2)
            if v2 is None:
                continue
            if wt.get(e2, -1) < v2:
                waits.append((e2, v2))
                wt[e2] = v2
        if eng in ("dve", "act") and any(e2 == "pe" for e2, _ in waits):
            pad = self.pad_sb
            inner = fn

            def fn(q, _inner=inner, _pad=pad, _eng=eng):
                if _eng == "dve":
                    p = q.tensor_copy(_pad[:, 0:256], _pad[:, 256:512])
                else:
                    p = q.copy(out=_pad[:, 512:768], in_=_pad[:, 768:1024])
                ins = _inner(q)
                if not isinstance(ins, (list, tuple)):
                    ins = [ins]
                return [p] + list(ins)
        inc = 16 * ndma if ndma else 1
        self.cnt[eng] += inc
        self.sched[eng].append((waits, fn, bool(ndma)))
        tok = (eng, self.cnt[eng])
        for r in reads:
            r.readers.append(tok)
        for w in writes:
            w.writers = [tok]
            w.readers = []
        for w in writes_acc:
            w.writers.append(tok)
        return tok

    def phase_barrier(self):
        self.barrier = [(e, self.cnt[e]) for e in self.ENGS if self.cnt[e] > 0]

    def emit(self, block):
        sections = {"pe": block.tensor, "dve": block.vector, "act": block.scalar,
                    "pool": block.gpsimd, "syn": block.sync}

        def mk(eng):
            sched = self.sched[eng]
            sems = self.sem

            def emit_stream(q):
                for waits, fn, is_dma in sched:
                    for e2, v2 in waits:
                        q.wait_ge(sems[e2], v2)
                    instrs = fn(q)
                    if not isinstance(instrs, (list, tuple)):
                        instrs = [instrs]
                    if is_dma:
                        for ins in instrs:
                            ins.then_inc(sems[eng], 16)
                    else:
                        instrs[-1].then_inc(sems[eng], 1)
            return emit_stream

        for eng, dec in sections.items():
            dec(mk(eng))


def build_program(n_heads=H, debug=False):
    nc = bass.Bass("TRN2", target_bir_lowering=False)

    def din(name, shape, dt=F32):
        return nc.dram_tensor(name, list(shape), dt, kind="ExternalInput").ap()

    def dscr(name, shape, dt):
        return nc.dram_tensor(name, list(shape), dt, kind="Internal").ap()

    xfull = din("xfull", (S, E))
    xown = din("xown", (1024, E))
    wqt = din("wqt", (E, E))
    wkt = din("wkt", (E, KVH * D))
    wvt = din("wvt", (E, KVH * D))
    wot = din("wot", (E, E))
    fker = din("fker", (E, E))
    fbias = din("fbias", (1, E))
    hp1 = din("hp1", (H, D, D))
    hb1 = din("hb1", (H, D, 1))
    hp2 = din("hp2", (H, D, D))
    hb2 = din("hb2", (H, D, 1))
    lat = din("lat", (16, 128, 1024))
    last = din("last", (128, 1024))
    lbt = din("lbt", (16, 32, E))
    lbst = din("lbst", (32, E))
    gateown = din("gateown", (1, 64))
    coskf = din("coskf", (D, S))
    sinkf = din("sinkf", (D, S))
    cosqf = din("cosqf", (D, 1024))
    sinqf = din("sinqf", (D, 1024))
    identb = din("identb", (128, 128), BF16)
    identf = din("identf", (128, 128))
    maskc = din("maskc", (128, 512), BF16)
    cc1640 = din("cc1640", (128, NSLOT))
    cr820 = din("cr820", (128, NSLOT))

    out = nc.dram_tensor("out", [1024, E], F32, kind="ExternalOutput").ap()
    if debug:
        dbg = {
            "d_xn": nc.dram_tensor("d_xn", [32, 128, S], BF16, kind="ExternalOutput").ap(),
            "d_kr": nc.dram_tensor("d_kr", [KVH, 128, S], BF16, kind="ExternalOutput").ap(),
            "d_v": nc.dram_tensor("d_v", [KVH, 128, 32, 128], BF16, kind="ExternalOutput").ap(),
            "d_qr": nc.dram_tensor("d_qr", [32, 128, 1024], BF16, kind="ExternalOutput").ap(),
            "d_qn2": nc.dram_tensor("d_qn2", [32, 1024], F32, kind="ExternalOutput").ap(),
            "d_ctx": nc.dram_tensor("d_ctx", [H, 128, 1024], BF16, kind="ExternalOutput").ap(),
            "d_negb": nc.dram_tensor("d_negb", [128, NSLOT * H], F32, kind="ExternalOutput").ap(),
            "d_sim": nc.dram_tensor("d_sim", [128, TOT], BF16, kind="ExternalOutput").ap(),
            "d_book": nc.dram_tensor("d_book", [128, 2 * NSLOT], F32, kind="ExternalOutput").ap(),
            "d_sel": nc.dram_tensor("d_sel", [128, 4096], BF16, kind="ExternalOutput").ap(),
            "d_p": nc.dram_tensor("d_p", [128, 4096], BF16, kind="ExternalOutput").ap(),
            "d_xo": nc.dram_tensor("d_xo", [32, 128, 1024], BF16, kind="ExternalOutput").ap(),
            "d_cosk": nc.dram_tensor("d_cosk", [128, S], F32, kind="ExternalOutput").ap(),
            "d_sink": nc.dram_tensor("d_sink", [128, S], F32, kind="ExternalOutput").ap(),
            "d_cosq": nc.dram_tensor("d_cosq", [128, 1024], F32, kind="ExternalOutput").ap(),
            "d_qn2r": nc.dram_tensor("d_qn2r", [1, 1024], F32, kind="ExternalOutput").ap(),
            "d_acc": nc.dram_tensor("d_acc", [128, 2 * NSLOT], F32, kind="ExternalOutput").ap(),
            "d_acomb": nc.dram_tensor("d_acomb", [4, 128, 1024], BF16, kind="ExternalOutput").ap(),
            "d_bcomb": nc.dram_tensor("d_bcomb", [4, 64, E], BF16, kind="ExternalOutput").ap(),
            "d_axt": nc.dram_tensor("d_axt", [64, 1024], BF16, kind="ExternalOutput").ap(),
            "d_lastb": nc.dram_tensor("d_lastb", [128, 1024], BF16, kind="ExternalOutput").ap(),
            "d_gbc": nc.dram_tensor("d_gbc", [128, 64], F32, kind="ExternalOutput").ap(),
        }

    xnsp = dscr("xnsp", (32, 128, S), BF16)
    xosp = dscr("xosp", (32, 128, 1024), BF16)
    krsp = dscr("krsp", (KVH, 128, S), BF16)
    vsp = dscr("vsp", (KVH, 128, 32, 128), BF16)
    qrsp = dscr("qrsp", (32, 128, 1024), BF16)
    ctxsp = dscr("ctxsp", (H, 128, 1024), BF16)
    qn2sp = dscr("qn2sp", (32, 1024), F32)

    ctx = ExitStack()
    with ctx:
        P = Prog(nc, ctx)

        def sb(name, shape, dt):
            t = ctx.enter_context(nc.sbuf_tensor(name, list(shape), dt))
            return t, Res()

        zero_c, r_zero = sb("zero_c", (128, 1), F32)
        onescol, r_onescol = sb("onescol", (128, 1), BF16)
        ones1b, r_ones1b = sb("ones1b", (1, 128), BF16)
        ones1F, r_ones1F = sb("ones1F", (1, 128), F32)
        idb_sb, r_idb = sb("idb_sb", (128, 128), BF16)
        idf_sb, r_idf = sb("idf_sb", (128, 128), F32)
        maskc_sb, r_maskc = sb("maskc_sb", (128, 512), BF16)
        cc_sb, r_cc = sb("cc_sb", (128, NSLOT), F32)
        cr_sb, r_cr = sb("cr_sb", (128, NSLOT), F32)
        kmax_acc, r_kmax = sb("kmax_acc", (1, 1), F32)
        negkb, r_negkb = sb("negkb", (128, 1), F32)
        negb_sb, r_negb = sb("negb_sb", (128, NSLOT * H), F32)

        P.op("pool", lambda q: [q.memset(zero_c[:], 0.0),
                                q.memset(onescol[:], 1.0),
                                q.memset(ones1b[:], 1.0),
                                q.memset(ones1F[:], 1.0),
                                q.memset(kmax_acc[:], 0.0)],
             writes=[r_zero, r_onescol, r_ones1b, r_ones1F, r_kmax])
        P.op("syn", lambda q: [q.dma_start(out=idb_sb[:], in_=identb[:, :]),
                               q.dma_start(out=idf_sb[:], in_=identf[:, :]),
                               q.dma_start(out=maskc_sb[:], in_=maskc[:, :]),
                               q.dma_start(out=cc_sb[:], in_=cc1640[:, :]),
                               q.dma_start(out=cr_sb[:], in_=cr820[:, :])],
             writes=[r_idb, r_idf, r_maskc, r_cc, r_cr], ndma=5)

        r_xnsp, r_xosp, r_krsp, r_vsp, r_qrsp, r_ctxsp, r_qn2sp = (
            Res(), Res(), Res(), Res(), Res(), Res(), Res())
        r_xnsp_tc = [Res() for _ in range(32)]
        r_out = Res()

        def emit_ln(bx, r_bx, bxn, r_bxn, st48, r_st48, mv, r_mv, rstd, r_rstd,
                    nmr, r_nmr):
            P.op("dve", lambda q: [
                q.bn_stats(st48[:, 6 * i:6 * i + 6], bx[:, 512 * i:512 * i + 512])
                for i in range(8)], reads=[r_bx], writes=[r_st48])
            P.op("dve", lambda q: q.bn_aggr(mv[:], st48[:]),
                 reads=[r_st48], writes=[r_mv])
            P.op("dve", lambda q: q.tensor_scalar(
                out=mv[:, 1:2], in0=mv[:, 1:2], scalar1=1e-5, scalar2=None,
                op0=ALU.add), writes=[r_mv])
            P.op("act", lambda q: q.activation(
                out=rstd[:], in_=mv[:, 1:2], func=AF.Sqrt, bias=zero_c[:],
                scale=1.0), reads=[r_mv, r_zero], writes=[r_rstd])
            P.op("dve", lambda q: q.reciprocal(rstd[:], rstd[:]), writes=[r_rstd])
            P.op("dve", lambda q: q.tensor_tensor(
                out=nmr[:], in0=mv[:, 0:1], in1=rstd[:], op=ALU.mult),
                reads=[r_mv, r_rstd], writes=[r_nmr])
            P.op("dve", lambda q: q.tensor_scalar(
                out=nmr[:], in0=nmr[:], scalar1=-1.0, scalar2=None, op0=ALU.mult),
                writes=[r_nmr])
            P.op("act", lambda q: q.activation(
                out=bxn[:], in_=bx[:], func=AF.Identity, bias=nmr[:], scale=rstd[:]),
                reads=[r_bx, r_nmr, r_rstd], writes=[r_bxn])

        def _phase1():
            with ExitStack() as ph:
                def sbl(name, shape, dt):
                    t = ph.enter_context(nc.sbuf_tensor(name, list(shape), dt))
                    return t, Res()

                def psl(name, shape, dt):
                    t = ph.enter_context(nc.psum_tensor(name, list(shape), dt))
                    return t, Res()

                xt = [sbl("a0x%d" % i, (128, E), F32) for i in range(2)]
                xnt = [sbl("a0n%d" % i, (128, E), BF16) for i in range(2)]
                xo = [sbl("a0o%d" % i, (128, 4096), BF16) for i in range(2)]
                st48, r_st48 = sbl("a0s", (128, 48), F32)
                mv, r_mv = sbl("a0m", (128, 2), F32)
                rstd, r_rstd = sbl("a0r", (128, 1), F32)
                nmr, r_nmr = sbl("a0nm", (128, 1), F32)
                pst, r_pst = psl("a0p", (128, 1024), BF16)

                for tc in range(32):
                    bx, r_bx = xt[tc % 2]
                    bxn, r_bxn = xnt[tc % 2]
                    bo, r_bo = xo[tc % 2]
                    P.op("syn", lambda q, tc=tc, bx=bx: q.dma_start(
                        out=bx[:], in_=xfull[128 * tc:128 * tc + 128, :]),
                        writes=[r_bx], ndma=1)
                    emit_ln(bx, r_bx, bxn, r_bxn, st48, r_st48, mv, r_mv,
                            rstd, r_rstd, nmr, r_nmr)
                    for g in range(4):
                        P.op("pe", lambda q, bxn=bxn, g=g: [
                            q.transpose(pst[:, 128 * j:128 * j + 128],
                                        bxn[:, 128 * (8 * g + j):128 * (8 * g + j) + 128],
                                        idb_sb[:, :]) for j in range(8)],
                            reads=[r_bxn, r_idb], writes=[r_pst])
                        P.op("dve", lambda q, bo=bo, g=g: q.tensor_copy(
                            bo[:, 1024 * g:1024 * g + 1024], pst[:]),
                            reads=[r_pst], writes_acc=[r_bo])
                    P.op("pool", lambda q, bo=bo, tc=tc: [
                        q.dma_start(out=xnsp[ec, :, 128 * tc:128 * tc + 128],
                                    in_=bo[:, 128 * ec:128 * ec + 128])
                        for ec in range(32)], reads=[r_bo],
                        writes_acc=[r_xnsp, r_xnsp_tc[tc]], ndma=32)

        _phase1()
        P.phase_barrier()

        def _phase2():
            with ExitStack() as ph:
                def sbl(name, shape, dt):
                    t = ph.enter_context(nc.sbuf_tensor(name, list(shape), dt))
                    return t, Res()

                def psl(name, shape, dt):
                    t = ph.enter_context(nc.psum_tensor(name, list(shape), dt))
                    return t, Res()

                wk_sb, r_wk = sbl("a1w", (128, 32 * 1024), BF16)
                xnu = [sbl("a1x%d" % i, (128, 32 * 512), BF16) for i in range(2)]
                cosk_sb, r_cosk = sbl("a1c", (128, S), F32)
                sink_sb, r_sink = sbl("a1s", (128, S), F32)
                tmpc, r_tmpc = sbl("a1tc", (128, 512), F32)
                tmps, r_tmps = sbl("a1ts", (128, 512), F32)
                krbf, r_krbf = sbl("a1kb", (128, 512), BF16)
                kr2, r_kr2 = sbl("a1k2", (128, 512), BF16)
                kn1, r_kn1 = sbl("a1kn", (1, 1), F32)
                psk = [psl("a1p%d" % i, (128, 512), F32) for i in range(2)]
                psm, r_psm = psl("a1m", (128, 512), F32)

                P.op("syn", lambda q: [q.dma_start(out=cosk_sb[:], in_=coskf[:, :]),
                                       q.dma_start(out=sink_sb[:], in_=sinkf[:, :])],
                     writes=[r_cosk, r_sink], ndma=2)
                for j in range(8):
                    P.op("pool", lambda q, j=j: q.dma_start(
                        out=wk_sb[:].rearrange("p (ec o) -> p ec o", ec=32)[
                            :, :, 128 * j:128 * j + 128],
                        in_=wkt[:, 128 * j:128 * j + 128].rearrange(
                            "(ec p) o -> p ec o", p=128)),
                        writes_acc=[r_wk], ndma=1)
                for u in range(8):
                    bx, r_bx = xnu[u % 2]
                    P.op("syn", lambda q, u=u, bx=bx: [
                        q.dma_start(out=bx[:, 512 * ec:512 * ec + 512],
                                    in_=xnsp[ec, :, 512 * u:512 * u + 512])
                        for ec in range(32)],
                        reads=[r_xnsp_tc[t] for t in range(4 * u, 4 * u + 4)],
                        writes=[r_bx], ndma=32)
                    for kvh in range(KVH):
                        pk, r_pk = psk[kvh % 2]
                        P.op("pe", lambda q, bx=bx, kvh=kvh, pk=pk: [
                            q.matmul(pk[:, 0:512],
                                     wk_sb[:, 1024 * ec + 128 * kvh:1024 * ec + 128 * kvh + 128],
                                     bx[:, 512 * ec:512 * ec + 512],
                                     start=(ec == 0), stop=(ec == 31))
                            for ec in range(32)], reads=[r_wk, r_bx], writes=[r_pk])
                        P.op("dve", lambda q, pk=pk, u=u: q.tensor_tensor(
                            out=tmpc[:], in0=pk[:, 0:512],
                            in1=cosk_sb[:, 512 * u:512 * u + 512], op=ALU.mult),
                            reads=[r_pk, r_cosk], writes=[r_tmpc])
                        P.op("dve", lambda q, pk=pk, u=u: [
                            q.tensor_tensor(out=tmps[0:64, :], in0=pk[64:128, 0:512],
                                            in1=sink_sb[0:64, 512 * u:512 * u + 512],
                                            op=ALU.mult),
                            q.tensor_tensor(out=tmps[64:128, :], in0=pk[0:64, 0:512],
                                            in1=sink_sb[64:128, 512 * u:512 * u + 512],
                                            op=ALU.mult)],
                            reads=[r_pk, r_sink], writes=[r_tmps])
                        P.op("dve", lambda q: q.tensor_tensor(
                            out=krbf[:], in0=tmpc[:], in1=tmps[:], op=ALU.add),
                            reads=[r_tmpc, r_tmps], writes=[r_krbf])
                        P.op("pool", lambda q, kvh=kvh, u=u: q.dma_start(
                            out=krsp[kvh, :, 512 * u:512 * u + 512], in_=krbf[:]),
                            reads=[r_krbf], writes_acc=[r_krsp], ndma=1)
                        P.op("act", lambda q: q.activation(
                            out=kr2[:], in_=krbf[:], func=AF.Square, bias=zero_c[:],
                            scale=1.0), reads=[r_krbf, r_zero], writes=[r_kr2])
                        P.op("pe", lambda q: q.matmul(
                            psm[0:1, 0:512], onescol[:, :], kr2[:], start=True,
                            stop=True), reads=[r_onescol, r_kr2], writes=[r_psm])
                        P.op("dve", lambda q: q.tensor_reduce(
                            out=kn1[:], in_=psm[0:1, 0:512], axis=AXX, op=ALU.max),
                            reads=[r_psm], writes=[r_kn1])
                        P.op("dve", lambda q: q.tensor_tensor(
                            out=kmax_acc[:], in0=kmax_acc[:], in1=kn1[:], op=ALU.max),
                            reads=[r_kn1], writes=[r_kmax])
                if debug:
                    P.op("syn", lambda q: [
                        q.dma_start(out=dbg["d_cosk"][:, :], in_=cosk_sb[:]),
                        q.dma_start(out=dbg["d_sink"][:, :], in_=sink_sb[:])],
                        reads=[r_cosk, r_sink], ndma=2)

        _phase2()
        P.phase_barrier()

        def _phase3():
            with ExitStack() as ph:
                def sbl(name, shape, dt):
                    t = ph.enter_context(nc.sbuf_tensor(name, list(shape), dt))
                    return t, Res()

                def psl(name, shape, dt):
                    t = ph.enter_context(nc.psum_tensor(name, list(shape), dt))
                    return t, Res()

                wv_sb, r_wv = sbl("a2w", (128, 32 * 1024), BF16)
                xnu = [sbl("a2x%d" % i, (128, 32 * 512), BF16) for i in range(2)]
                vbf = [sbl("a2v%d" % i, (128, 1024), BF16) for i in range(2)]
                psvs = [psl("a2p%d" % i, (128, 1024), F32) for i in range(2)]

                for j in range(8):
                    P.op("pool", lambda q, j=j: q.dma_start(
                        out=wv_sb[:].rearrange("p (ec o) -> p ec o", ec=32)[
                            :, :, 128 * j:128 * j + 128],
                        in_=wvt[:, 128 * j:128 * j + 128].rearrange(
                            "(ec p) o -> p ec o", p=128)),
                        writes_acc=[r_wv], ndma=1)
                for u in range(8):
                    bx, r_bx = xnu[u % 2]
                    P.op("syn", lambda q, u=u, bx=bx: [
                        q.dma_start(out=bx[:, 512 * ec:512 * ec + 512],
                                    in_=xnsp[ec, :, 512 * u:512 * u + 512])
                        for ec in range(32)],
                        reads=[r_xnsp_tc[t] for t in range(4 * u, 4 * u + 4)],
                        writes=[r_bx], ndma=32)
                    for tsub in range(4):
                        bv, r_bv = vbf[tsub % 2]
                        psv, r_psv = psvs[tsub % 2]
                        P.op("pe", lambda q, bx=bx, tsub=tsub, psv=psv: [
                            q.matmul(psv[:, 512 * oh:512 * oh + 512],
                                     bx[:, 512 * ec + 128 * tsub:512 * ec + 128 * tsub + 128],
                                     wv_sb[:, 1024 * ec + 512 * oh:1024 * ec + 512 * oh + 512],
                                     start=(ec == 0), stop=(ec == 31))
                            for oh in range(2) for ec in range(32)],
                            reads=[r_wv, r_bx], writes=[r_psv])
                        P.op("act", lambda q, bv=bv, psv=psv: q.copy(out=bv[:], in_=psv[:]),
                             reads=[r_psv], writes=[r_bv])
                        P.op("pool", lambda q, bv=bv, u=u, tsub=tsub: [
                            q.dma_start(out=vsp[kvh, :, 4 * u + tsub, :],
                                        in_=bv[:, 128 * kvh:128 * kvh + 128])
                            for kvh in range(KVH)],
                            reads=[r_bv], writes_acc=[r_vsp], ndma=KVH)

        _phase3()
        P.phase_barrier()

        def _phase4():
            with ExitStack() as ph:
                def sbl(name, shape, dt):
                    t = ph.enter_context(nc.sbuf_tensor(name, list(shape), dt))
                    return t, Res()

                def psl(name, shape, dt):
                    t = ph.enter_context(nc.psum_tensor(name, list(shape), dt))
                    return t, Res()

                xno_sb, r_xno = sbl("a3xo", (128, 32 * 1024), BF16)
                xt = [sbl("a3x%d" % i, (128, E), F32) for i in range(2)]
                xnt = [sbl("a3n%d" % i, (128, E), BF16) for i in range(2)]
                st48, r_st48 = sbl("a3s", (128, 48), F32)
                mv, r_mv = sbl("a3m", (128, 2), F32)
                rstd, r_rstd = sbl("a3r", (128, 1), F32)
                nmr, r_nmr = sbl("a3nm", (128, 1), F32)
                pst, r_pst = psl("a3p", (128, 1024), BF16)

                for sl in range(8):
                    bx, r_bx = xt[sl % 2]
                    bxn, r_bxn = xnt[sl % 2]
                    P.op("syn", lambda q, sl=sl, bx=bx: q.dma_start(
                        out=bx[:], in_=xown[128 * sl:128 * sl + 128, :]),
                        writes=[r_bx], ndma=1)
                    emit_ln(bx, r_bx, bxn, r_bxn, st48, r_st48, mv, r_mv,
                            rstd, r_rstd, nmr, r_nmr)
                    for g in range(4):
                        P.op("pe", lambda q, bxn=bxn, g=g: [
                            q.transpose(pst[:, 128 * j:128 * j + 128],
                                        bxn[:, 128 * (8 * g + j):128 * (8 * g + j) + 128],
                                        idb_sb[:, :]) for j in range(8)],
                            reads=[r_bxn, r_idb], writes=[r_pst])
                        P.op("dve", lambda q, g=g, sl=sl: [
                            q.tensor_copy(
                                xno_sb[:, 1024 * (8 * g + j) + 128 * sl:
                                       1024 * (8 * g + j) + 128 * sl + 128],
                                pst[:, 128 * j:128 * j + 128]) for j in range(8)],
                            reads=[r_pst], writes_acc=[r_xno])
                P.op("pool", lambda q: [
                    q.dma_start(out=xosp[ec, :, :],
                                in_=xno_sb[:, 1024 * ec:1024 * ec + 1024])
                    for ec in range(32)], reads=[r_xno], writes_acc=[r_xosp], ndma=32)

                wqb = [sbl("a3wb%d" % i, (128, 4096), BF16) for i in range(2)]
                cosq_sb, r_cosq = sbl("a3c", (128, 1024), F32)
                sinq_sb, r_sinq = sbl("a3si", (128, 1024), F32)
                tmpc, r_tmpc = sbl("a3tc", (128, 512), F32)
                tmps, r_tmps = sbl("a3ts", (128, 512), F32)
                qrbf, r_qrbf = sbl("a3qb", (128, 512), BF16)
                qr2, r_qr2 = sbl("a3q2", (128, 512), BF16)
                qn2row, r_qn2row = sbl("a3qn", (1, 1024), F32)
                psq = [psl("a3pq%d" % i, (128, 512), F32) for i in range(2)]
                psm, r_psm = psl("a3pm", (128, 512), F32)

                P.op("syn", lambda q: [q.dma_start(out=cosq_sb[:], in_=cosqf[:, :]),
                                       q.dma_start(out=sinq_sb[:], in_=sinqf[:, :])],
                     writes=[r_cosq, r_sinq], ndma=2)
                for oc in range(32):
                    bb, r_bb = wqb[oc % 2]
                    P.op("pool", lambda q, oc=oc, bb=bb: [
                        q.dma_start(out=bb[:, 1024 * jj:1024 * jj + 1024].rearrange(
                                        "p (ec o) -> p ec o", ec=8),
                                    in_=wqt[1024 * jj:1024 * jj + 1024,
                                            128 * oc:128 * oc + 128].rearrange(
                                        "(ec p) o -> p ec o", p=128))
                        for jj in range(4)], writes=[r_bb], ndma=4)
                    for th in range(2):
                        pq, r_pq = psq[th]
                        P.op("pe", lambda q, bb=bb, th=th, pq=pq: [
                            q.matmul(pq[:, 0:512],
                                     bb[:, 128 * ec:128 * ec + 128],
                                     xno_sb[:, 1024 * ec + 512 * th:1024 * ec + 512 * th + 512],
                                     start=(ec == 0), stop=(ec == 31))
                            for ec in range(32)], reads=[r_bb, r_xno], writes=[r_pq])
                        P.op("dve", lambda q, pq=pq, th=th: q.tensor_tensor(
                            out=tmpc[:], in0=pq[:, 0:512],
                            in1=cosq_sb[:, 512 * th:512 * th + 512], op=ALU.mult),
                            reads=[r_pq, r_cosq], writes=[r_tmpc])
                        P.op("dve", lambda q, pq=pq, th=th: [
                            q.tensor_tensor(out=tmps[0:64, :], in0=pq[64:128, 0:512],
                                            in1=sinq_sb[0:64, 512 * th:512 * th + 512],
                                            op=ALU.mult),
                            q.tensor_tensor(out=tmps[64:128, :], in0=pq[0:64, 0:512],
                                            in1=sinq_sb[64:128, 512 * th:512 * th + 512],
                                            op=ALU.mult)],
                            reads=[r_pq, r_sinq], writes=[r_tmps])
                        P.op("dve", lambda q: q.tensor_tensor(
                            out=qrbf[:], in0=tmpc[:], in1=tmps[:], op=ALU.add),
                            reads=[r_tmpc, r_tmps], writes=[r_qrbf])
                        P.op("pool", lambda q, oc=oc, th=th: q.dma_start(
                            out=qrsp[oc, :, 512 * th:512 * th + 512], in_=qrbf[:]),
                            reads=[r_qrbf], writes_acc=[r_qrsp], ndma=1)
                        P.op("act", lambda q: q.activation(
                            out=qr2[:], in_=qrbf[:], func=AF.Square, bias=zero_c[:],
                            scale=1.0), reads=[r_qrbf, r_zero], writes=[r_qr2])
                        P.op("pe", lambda q, th=th: q.matmul(
                            psm[0:1, 0:512], onescol[:, :], qr2[:], start=True,
                            stop=True), reads=[r_onescol, r_qr2], writes=[r_psm])
                        P.op("dve", lambda q, th=th: q.tensor_copy(
                            qn2row[0:1, 512 * th:512 * th + 512], psm[0:1, 0:512]),
                            reads=[r_psm], writes_acc=[r_qn2row])
                    P.op("pool", lambda q, oc=oc: q.dma_start(
                        out=qn2sp[oc:oc + 1, :], in_=qn2row[:]),
                        reads=[r_qn2row], writes_acc=[r_qn2sp], ndma=1)

                if debug:
                    P.op("syn", lambda q: [
                        q.dma_start(out=dbg["d_cosq"][:, :], in_=cosq_sb[:]),
                        q.dma_start(out=dbg["d_qn2r"][:, :], in_=qn2row[:])],
                        reads=[r_cosq, r_qn2row], ndma=2)

                psb, r_psb = psl("a3pb", (128, 512), F32)
                sqk, r_sqk = sbl("a3sq", (1, 1), F32)
                qnall, r_qnall = sbl("a3qa", (32, 1024), F32)
                sqn, r_sqn = sbl("a3sn", (128, 32), F32)

                P.op("act", lambda q: q.activation(
                    out=sqk[:], in_=kmax_acc[:], func=AF.Sqrt, bias=zero_c[0:1, :],
                    scale=1.0), reads=[r_kmax, r_zero], writes=[r_sqk])
                P.op("dve", lambda q: q.tensor_scalar(
                    out=sqk[:], in0=sqk[:], scalar1=-1.001 / SQD, scalar2=None,
                    op0=ALU.mult), writes=[r_sqk])
                P.op("pe", lambda q: q.matmul(psb[:, 64:65], ones1F[0:1, :],
                                              sqk[0:1, :], start=True, stop=True),
                     reads=[r_ones1F, r_sqk], writes=[r_psb])
                P.op("dve", lambda q: q.tensor_copy(negkb[:], psb[:, 64:65]),
                     reads=[r_psb], writes=[r_negkb])
                P.op("syn", lambda q: q.dma_start(out=qnall[:], in_=qn2sp[:, :]),
                     reads=[r_qn2sp], writes=[r_qnall], ndma=1)
                for s in range(NSLOT):
                    P.op("pe", lambda q, s=s: q.transpose(
                        psb[:, 96:128], qnall[0:32, 128 * s:128 * s + 128],
                        idf_sb[0:32, 0:32]),
                        reads=[r_qnall, r_idf], writes=[r_psb])
                    P.op("act", lambda q, s=s: q.activation(
                        out=sqn[:], in_=psb[:, 96:128], func=AF.Sqrt,
                        bias=zero_c[:], scale=1.0),
                        reads=[r_psb, r_zero], writes=[r_sqn])
                    P.op("dve", lambda q, s=s: q.tensor_scalar(
                        out=negb_sb[:, H * s:H * s + 32], in0=sqn[:],
                        scalar1=negkb[:], scalar2=None, op0=ALU.mult),
                        reads=[r_sqn, r_negkb], writes_acc=[r_negb])

        _phase4()
        P.phase_barrier()
        if debug:
            P.op("syn", lambda q: [
                q.dma_start(out=dbg["d_xn"][:, :, :], in_=xnsp[:, :, :]),
                q.dma_start(out=dbg["d_kr"][:, :, :], in_=krsp[:, :, :]),
                q.dma_start(out=dbg["d_v"][:, :, :, :], in_=vsp[:, :, :, :]),
                q.dma_start(out=dbg["d_qr"][:, :, :], in_=qrsp[:, :, :]),
                q.dma_start(out=dbg["d_qn2"][:, :], in_=qn2sp[:, :]),
                q.dma_start(out=dbg["d_xo"][:, :, :], in_=xosp[:, :, :]),
                q.dma_start(out=dbg["d_negb"][:, :], in_=negb_sb[:])],
                reads=[r_xnsp, r_krsp, r_vsp, r_qrsp, r_qn2sp, r_negb, r_xosp],
                ndma=7)

        def _phase5():
            with ExitStack() as ph:
                def sbl(name, shape, dt):
                    t = ph.enter_context(nc.sbuf_tensor(name, list(shape), dt))
                    return t, Res()

                def psl(name, shape, dt):
                    t = ph.enter_context(nc.psum_tensor(name, list(shape), dt))
                    return t, Res()

                sim_sb, r_sim = sbl("bsim", (128, TOT), BF16)
                kr_sb0, r_kr0 = sbl("bkr0", (128, S), BF16)
                kr_sb1, r_kr1 = sbl("bkr1", (128, S), BF16)
                v_sb0, r_v0 = sbl("bv0", (128, S), BF16)
                v_sb1, r_v1 = sbl("bv1", (128, S), BF16)
                KRV = [((kr_sb0, r_kr0), (v_sb0, r_v0)),
                       ((kr_sb1, r_kr1), (v_sb1, r_v1))]
                qr_sb, r_qr = sbl("bqr", (128, 1024), BF16)
                hw1b, r_hw1b = sbl("bh1b", (128, 128), BF16)
                hw2b, r_hw2b = sbl("bh2b", (128, 128), BF16)
                hb1_sb, r_hb1 = sbl("bhb1", (128, 1), F32)
                hb2_sb, r_hb2 = sbl("bhb2", (128, 1), F32)
                y1p_sb, r_y1p = sbl("by1p", (128, 1024), BF16)
                zp_sb, r_zp = sbl("bzp", (128, 1024), F32)
                y_sb, r_y = sbl("by", (128, 5120), BF16)
                hq_sb, r_hq = sbl("bhq", (128, 5120), BF16)
                p0a_sb, r_p0a = sbl("bp0a", (128, 4096), BF16)
                p0b_sb, r_p0b = sbl("bp0b", (128, 4096), BF16)
                P0 = [(p0a_sb, r_p0a), (p0b_sb, r_p0b)]
                ps_sb, r_ps = sbl("bps", (128, 4096), F16)
                sel_sb, r_sel = sbl("bsel", (128, 4096), BF16)
                p_sb0, r_p0r = sbl("bpA0", (128, 4096), BF16)
                p_sb1, r_p1r = sbl("bpA1", (128, 4096), BF16)
                PBUF = [(p_sb0, r_p0r), (p_sb1, r_p1r)]
                pt_sb, r_pt = sbl("bpt", (128, 4096), BF16)
                rr_sb, r_rr = sbl("brr", (128, 4096), F16)
                trash_a = sel_sb
                trash_d = ps_sb
                loT, r_lo = sbl("blo", (128, NSLOT), F32)
                midT, r_mid = sbl("bmid", (128, NSLOT), F32)
                ngm, r_ngm = sbl("bngm", (128, NSLOT), F32)
                accA, r_accA = sbl("baca", (128, NSLOT), F32)
                accD, r_accD = sbl("bacd", (128, NSLOT), F32)
                t2, r_t2 = sbl("bt2", (128, NSLOT), F32)
                geT, r_ge = sbl("bge", (128, NSLOT), F32)
                geb, r_geb = sbl("bgeb", (128, NSLOT), F32)
                cntT, r_cnt = sbl("bcnt", (128, NSLOT), F32)
                t3, r_t3 = sbl("bt3", (128, NSLOT), F32)
                chiT, r_chi = sbl("bchi", (128, NSLOT), F32)
                rmin, r_rmin = sbl("brm", (128, NSLOT), F32)
                zacc, r_zacc = sbl("bza", (128, NSLOT), F32)
                zrT, r_zr = sbl("bzr", (128, NSLOT), F32)
                cstg, r_cstg = sbl("bcs", (128, 1024), BF16)
                psA, r_psA = psl("bpA", (128, 2048), F32)
                psB, r_psB = psl("bpB", (128, 1024), F32)
                psC, r_psC = psl("bpC", (128, 512), F32)
                psT, r_psT = psl("bpT", (128, 1024), BF16)
                r_psTh = [Res(), Res()]

                sim2_sb, r_sim2 = sbl("bsim2", (128, TOT), BF16)
                qr2_sb, r_qr2 = sbl("bqr2", (128, 1024), BF16)
                SIM = [(sim_sb, r_sim), (sim2_sb, r_sim2)]
                QR = [(qr_sb, r_qr), (qr2_sb, r_qr2)]

                def front_load(h):
                    qrb, r_qrb = QR[h % 2]
                    kvh = h // GRP
                    (kr_sb, r_kr), (v_sb, r_v) = KRV[kvh % 2]
                    if h % GRP == 0:
                        P.op("syn", lambda q, kvh=kvh, kr_sb=kr_sb: q.dma_start(
                            out=kr_sb[:], in_=krsp[kvh, :, :]),
                            reads=[r_krsp], writes=[r_kr], ndma=1)
                        P.op("syn", lambda q, kvh=kvh, v_sb=v_sb: q.dma_start(
                            out=v_sb[:], in_=vsp[kvh, :, :, :].rearrange(
                                "p tc d -> p (tc d)")),
                            reads=[r_vsp], writes=[r_v], ndma=1)
                    P.op("syn", lambda q, h=h, qrb=qrb: [
                        q.dma_start(out=qrb[:], in_=qrsp[h, :, :]),
                        q.dma_start(out=hb1_sb[:], in_=hb1[h, :, :]),
                        q.dma_start(out=hb2_sb[:], in_=hb2[h, :, :])],
                        reads=[r_qrsp], writes=[r_qrb, r_hb1, r_hb2], ndma=3)
                    P.op("pool", lambda q, h=h: [
                        q.dma_start(out=hw1b[:], in_=hp1[h, :, :]),
                        q.dma_start(out=hw2b[:], in_=hp2[h, :, :])],
                        writes=[r_hw1b, r_hw2b], ndma=2)

                def front_hash(h):
                    qrb, r_qrb = QR[h % 2]
                    (kr_sb, r_kr), (v_sb, r_v) = KRV[(h // GRP) % 2]
                    P.op("dve", lambda q, qrb=qrb, kr_sb=kr_sb: [
                        q.tensor_copy(y_sb[:, 0:4096], kr_sb[:]),
                        q.tensor_copy(y_sb[:, 4096:5120], qrb[:])],
                        reads=[r_kr, r_qrb], writes=[r_y])
                    for pc in range(5):
                        base = 1024 * pc
                        bank = psA[:, 1024 * (pc % 2):1024 * (pc % 2) + 1024]
                        P.op("pe", lambda q, base=base, bank=bank: [
                            q.matmul(bank[:, 512 * j:512 * j + 512], hw1b[:, :],
                                     y_sb[:, base + 512 * j:base + 512 * j + 512],
                                     start=True, stop=True) for j in range(2)],
                            reads=[r_hw1b, r_y], writes=[r_psA])
                        P.op("act", lambda q, bank=bank: q.activation(
                            out=y1p_sb[:], in_=bank, func=AF.Silu, bias=hb1_sb[:],
                            scale=1.0), reads=[r_psA, r_hb1], writes=[r_y1p])
                        P.op("pool", lambda q, base=base: q.tensor_tensor(
                            out=y_sb[:, base:base + 1024], in0=y1p_sb[:],
                            in1=y_sb[:, base:base + 1024], op=ALU.add),
                            reads=[r_y1p], writes_acc=[r_y])
                        P.op("pe", lambda q, base=base, bank=bank: [
                            q.matmul(bank[:, 512 * j:512 * j + 512], hw2b[:, :],
                                     y_sb[:, base + 512 * j:base + 512 * j + 512],
                                     start=True, stop=True) for j in range(2)],
                            reads=[r_hw2b, r_y], writes=[r_psA])
                        P.op("act", lambda q, bank=bank: q.activation(
                            out=zp_sb[:], in_=bank, func=AF.Identity, bias=hb2_sb[:],
                            scale=1.0), reads=[r_psA, r_hb2], writes=[r_zp])
                        P.op("dve", lambda q, base=base: q.tensor_tensor(
                            out=zp_sb[:], in0=zp_sb[:],
                            in1=y_sb[:, base:base + 1024], op=ALU.add),
                            reads=[r_y], writes=[r_zp])
                        P.op("act", lambda q, base=base: q.activation(
                            out=hq_sb[:, base:base + 1024], in_=zp_sb[:],
                            func=AF.Sign, bias=zero_c[:], scale=1.0),
                            reads=[r_zp, r_zero], writes_acc=[r_hq])

                def front_sim(h, s):
                    simb, r_simb = SIM[h % 2]
                    if True:
                        Es = EXT[s]
                        qsl = hq_sb[:, 4096 + 128 * s:4096 + 128 * s + 128]
                        for piece in range(0, Es, 2048):
                            plen = min(2048, Es - piece)
                            nj = plen // 512

                            def mmfn(q, s=s, piece=piece, nj=nj, Es=Es, qsl=qsl):
                                ins = []
                                for j in range(nj):
                                    t0 = piece + 512 * j
                                    is_diag = (t0 == Es - 512)
                                    ins.append(q.matmul(
                                        psA[:, 512 * j:512 * j + 512], qsl,
                                        hq_sb[:, t0:t0 + 512],
                                        start=True, stop=not is_diag))
                                    if is_diag:
                                        ins.append(q.matmul(
                                            psA[:, 512 * j:512 * j + 512],
                                            idb_sb[:, :], maskc_sb[:, :],
                                            start=False, stop=True))
                                return ins
                            P.op("pe", mmfn, reads=[r_hq, r_idb, r_maskc],
                                 writes=[r_psA])
                            P.op("act", lambda q, s=s, piece=piece, plen=plen,
                                 simb=simb:
                                 q.activation(
                                     out=simb[:, OFF[s] + piece:OFF[s] + piece + plen],
                                     in_=psA[:, 0:plen], func=AF.Identity,
                                     bias=zero_c[:], scale=1.0),
                                 reads=[r_psA, r_zero], writes_acc=[r_simb])
                    if debug and h == 0 and s == NSLOT - 1:
                        P.op("syn", lambda q, simb=simb: q.dma_start(
                            out=dbg["d_sim"][:, :], in_=simb[:]),
                            reads=[r_simb], ndma=1)

                def front(h):
                    front_load(h)
                    front_hash(h)
                    for s in range(NSLOT):
                        front_sim(h, s)

                def mid_a(h):
                    simb, r_simb = SIM[h % 2]
                    P.op("dve", lambda q: [q.memset(loT[:], -130.0),
                                           q.memset(chiT[:], 0.0)],
                         writes=[r_lo, r_chi])
                    for rnd, step in enumerate(STEPS):
                        P.op("dve", lambda q, step=step: q.tensor_scalar(
                            out=midT[:], in0=loT[:], scalar1=step, scalar2=None,
                            op0=ALU.add), reads=[r_lo], writes=[r_mid])
                        P.op("dve", lambda q: q.tensor_scalar(
                            out=ngm[:], in0=midT[:], scalar1=-1.0, scalar2=0.5,
                            op0=ALU.mult, op1=ALU.add),
                            reads=[r_mid], writes=[r_ngm])
                        P.op("act", lambda q, simb=simb: [
                            q.activation(out=trash_a[:, 0:SPL[s]],
                                         in_=simb[:, OFF[s]:OFF[s] + SPL[s]],
                                         func=AF.Sign, bias=ngm[:, s:s + 1],
                                         scale=1.0,
                                         accum_out=accA[:, s:s + 1])
                            for s in range(1, NSLOT)] + [
                            q.activation(out=trash_a[:, 0:512],
                                         in_=simb[:, 0:512], func=AF.Sign,
                                         bias=zero_c[:], scale=1.0)],
                            reads=[r_simb, r_ngm], writes=[r_accA])
                        P.op("dve", lambda q, simb=simb: [
                            q.tensor_scalar(out=trash_d[:, 0:EXT[s] - SPL[s]],
                                            in0=simb[:, OFF[s] + SPL[s]:OFF[s] + EXT[s]],
                                            scalar1=midT[:, s:s + 1], scalar2=None,
                                            op0=ALU.is_ge, op1=ALU.add,
                                            accum_out=accD[:, s:s + 1])
                            for s in range(1, NSLOT)] + [
                            q.tensor_copy(trash_d[:, 0:512], simb[:, 0:512])],
                            reads=[r_simb, r_mid], writes=[r_accD])
                        P.op("dve", lambda q: q.scalar_tensor_tensor(
                            out=t2[:], in0=accD[:], scalar=2.0, in1=accA[:],
                            op0=ALU.mult, op1=ALU.add),
                            reads=[r_accD, r_accA], writes=[r_t2])
                        P.op("dve", lambda q: q.tensor_tensor(
                            out=geT[:], in0=t2[:], in1=cc_sb[:], op=ALU.is_ge),
                            reads=[r_t2, r_cc], writes=[r_ge])
                        P.op("dve", lambda q, step=step: q.scalar_tensor_tensor(
                            out=loT[:], in0=geT[:], scalar=step, in1=loT[:],
                            op0=ALU.mult, op1=ALU.add),
                            reads=[r_ge], writes=[r_lo])
                        P.op("dve", lambda q: q.scalar_tensor_tensor(
                            out=cntT[:], in0=t2[:], scalar=0.5, in1=cr_sb[:],
                            op0=ALU.mult, op1=ALU.add),
                            reads=[r_t2, r_cr], writes=[r_cnt])
                        P.op("dve", lambda q: q.tensor_scalar(
                            out=geb[:], in0=geT[:], scalar1=-1.0, scalar2=1.0,
                            op0=ALU.mult, op1=ALU.add),
                            reads=[r_ge], writes=[r_geb])
                        P.op("dve", lambda q: q.tensor_tensor(
                            out=t3[:], in0=cntT[:], in1=chiT[:], op=ALU.subtract),
                            reads=[r_cnt, r_chi], writes=[r_t3])
                        P.op("dve", lambda q: q.tensor_tensor(
                            out=t3[:], in0=t3[:], in1=geb[:], op=ALU.mult),
                            reads=[r_geb], writes=[r_t3])
                        P.op("dve", lambda q: q.tensor_tensor(
                            out=chiT[:], in0=chiT[:], in1=t3[:], op=ALU.add),
                            reads=[r_t3], writes=[r_chi])
                    P.op("dve", lambda q: q.scalar_tensor_tensor(
                        out=rmin[:], in0=loT[:], scalar=-4096.0, in1=chiT[:],
                        op0=ALU.mult, op1=ALU.subtract),
                        reads=[r_lo, r_chi], writes=[r_rmin])
                    P.op("dve", lambda q: q.tensor_scalar(
                        out=rmin[:], in0=rmin[:], scalar1=820.0, scalar2=None,
                        op0=ALU.add), writes=[r_rmin])
                    P.op("dve", lambda q: [q.memset(loT[:, 0:1], -130.0),
                                           q.memset(rmin[:, 0:1], 533300.0)],
                         writes_acc=[r_lo, r_rmin])
                    if debug and h == 0:
                        P.op("syn", lambda q: [
                            q.dma_start(out=dbg["d_acc"][:, 0:NSLOT], in_=accA[:]),
                            q.dma_start(out=dbg["d_acc"][:, NSLOT:2 * NSLOT],
                                        in_=accD[:])],
                            reads=[r_accA, r_accD], ndma=2)
                        P.op("syn", lambda q: [
                            q.dma_start(out=dbg["d_book"][:, 0:NSLOT], in_=loT[:]),
                            q.dma_start(out=dbg["d_book"][:, NSLOT:2 * NSLOT],
                                        in_=rmin[:])],
                            reads=[r_lo, r_rmin], ndma=2)

                def mid_b(h):
                    simb, r_simb = SIM[h % 2]
                    qrb, r_qrb = QR[h % 2]
                    (kr_sb, r_kr), (v_sb, r_v) = KRV[(h // GRP) % 2]

                    def scores_exp(s):
                        Es = EXT[s]
                        ncol = H * s + h
                        p0_sb, r_p0 = P0[s % 2]
                        qsl = qrb[:, 128 * s:128 * s + 128]
                        for pb in range(0, Es, 1024):
                            plen = min(1024, Es - pb)
                            P.op("pe", lambda q, qsl=qsl, pb=pb, plen=plen,
                                 kr_sb=kr_sb: [
                                q.matmul(psB[:, 512 * j:512 * j + 512], qsl,
                                         kr_sb[:, pb + 512 * j:pb + 512 * j + 512],
                                         start=True, stop=True)
                                for j in range(plen // 512)],
                                reads=[r_qrb, r_kr], writes=[r_psB])
                            P.op("act", lambda q, pb=pb, plen=plen, ncol=ncol,
                                 p0_sb=p0_sb:
                                 q.activation(
                                     out=p0_sb[:, pb:pb + plen], in_=psB[:, 0:plen],
                                     func=AF.Exp, scale=1.0 / SQD,
                                     bias=negb_sb[:, ncol:ncol + 1]),
                                 reads=[r_psB, r_negb], writes_acc=[r_p0])

                    def chain(s):
                        Es = EXT[s]
                        p_sb, r_p = PBUF[s % 2]
                        p0_sb, r_p0 = P0[s % 2]
                        eq_sb, r_eq = p_sb, r_p
                        if s == 0:
                            P.op("dve", lambda q, Es=Es, simb=simb: q.tensor_scalar(
                                out=sel_sb[:, 0:Es], in0=simb[:, OFF[0]:OFF[0] + Es],
                                scalar1=-129.0, scalar2=None, op0=ALU.is_ge),
                                reads=[r_simb], writes=[r_sel])
                        else:
                            P.op("dve", lambda q, s=s, Es=Es, simb=simb: q.tensor_scalar(
                                out=eq_sb[:, 0:Es], in0=simb[:, OFF[s]:OFF[s] + Es],
                                scalar1=loT[:, s:s + 1], scalar2=None, op0=ALU.is_equal),
                                reads=[r_simb, r_lo], writes=[r_eq])
                            P.op("dve", lambda q, Es=Es: q.tensor_tensor_scan(
                                out=ps_sb[:, 0:Es], data0=eq_sb[:, 0:Es],
                                data1=eq_sb[:, 0:Es], initial=0.0, op0=ALU.add,
                                op1=ALU.bypass), reads=[r_eq], writes=[r_ps])
                            P.op("dve", lambda q, s=s, Es=Es, simb=simb: q.tensor_scalar(
                                out=rr_sb[:, 0:Es], in0=simb[:, OFF[s]:OFF[s] + Es],
                                scalar1=4096.0, scalar2=rmin[:, s:s + 1], op0=ALU.mult,
                                op1=ALU.add), reads=[r_simb, r_rmin], writes=[r_rr])
                            P.op("dve", lambda q, Es=Es: q.tensor_tensor(
                                out=sel_sb[:, 0:Es], in0=ps_sb[:, 0:Es],
                                in1=rr_sb[:, 0:Es], op=ALU.is_le),
                                reads=[r_ps, r_rr], writes=[r_sel])
                        P.op("dve", lambda q, s=s, Es=Es: q.scalar_tensor_tensor(
                            out=p_sb[:, 0:Es], in0=p0_sb[:, 0:Es], scalar=1.0,
                            in1=sel_sb[:, 0:Es], op0=ALU.mult, op1=ALU.mult,
                            accum_out=zacc[:, s:s + 1]),
                            reads=[r_p0, r_sel], writes=[r_p], writes_acc=[r_zacc])
                        P.op("dve", lambda q, simb=simb: q.tensor_copy(
                            trash_d[:, 0:512], simb[:, 0:512]),
                            reads=[r_simb])
                        P.op("dve", lambda q, s=s: q.reciprocal(
                            zrT[:, s:s + 1], zacc[:, s:s + 1]),
                            reads=[r_zacc], writes_acc=[r_zr])
                        P.op("dve", lambda q, s=s, Es=Es: q.tensor_scalar(
                            out=p_sb[:, 0:Es], in0=p_sb[:, 0:Es],
                            scalar1=zrT[:, s:s + 1], scalar2=None, op0=ALU.mult),
                            reads=[r_zr], writes=[r_p])

                    def tail(s):
                        Es = EXT[s]
                        p_sb, r_p = PBUF[s % 2]
                        for pb in range(0, Es, 1024):
                            plen = min(1024, Es - pb)
                            P.op("pe", lambda q, pb=pb, plen=plen, p_sb=p_sb: [
                                q.transpose(psT[:, 128 * j:128 * j + 128],
                                            p_sb[:, pb + 128 * j:pb + 128 * j + 128],
                                            idb_sb[:, :])
                                for j in range(plen // 128)],
                                reads=[r_p, r_idb], writes=[r_psT])
                            if (pb // 1024) % 2 == 0:
                                P.op("act", lambda q, pb=pb, plen=plen: q.copy(
                                    out=pt_sb[:, pb:pb + plen], in_=psT[:, 0:plen]),
                                    reads=[r_psT], writes_acc=[r_pt])
                            else:
                                P.op("dve", lambda q, pb=pb, plen=plen: q.tensor_copy(
                                    pt_sb[:, pb:pb + plen], psT[:, 0:plen]),
                                    reads=[r_psT], writes_acc=[r_pt])
                        ntc = Es // 128

                        def avfn(q, ntc=ntc, v_sb=v_sb):
                            return [q.matmul(
                                psC[:, 0:128], v_sb[:, 128 * tc:128 * tc + 128],
                                pt_sb[:, 128 * tc:128 * tc + 128],
                                start=(tc == 0), stop=(tc == ntc - 1))
                                for tc in range(ntc)]
                        P.op("pe", avfn, reads=[r_v, r_pt], writes=[r_psC])
                        P.op("dve", lambda q, s=s: q.tensor_copy(
                            cstg[:, 128 * s:128 * s + 128], psC[:, 0:128]),
                            reads=[r_psC], writes_acc=[r_cstg])

                    nx = h + 1 < n_heads
                    sim_order = [7, 6, 5, 4, 3, 2, 1, 0]
                    for s in range(NSLOT):
                        scores_exp(s)
                        chain(s)
                        if nx and s == 0:
                            front_load(h + 1)
                        if nx and s == 1:
                            front_hash(h + 1)
                        if nx and 2 <= s:
                            front_sim(h + 1, sim_order[s - 2])
                        if s > 0:
                            tail(s - 1)
                    tail(NSLOT - 1)
                    if nx:
                        for s2 in sim_order[6:]:
                            front_sim(h + 1, s2)
                    if debug and h == 0:
                        P.op("syn", lambda q: [
                            q.dma_start(out=dbg["d_sel"][:, :], in_=sel_sb[:]),
                            q.dma_start(out=dbg["d_p"][:, :],
                                        in_=PBUF[(NSLOT - 1) % 2][0][:])],
                            reads=[r_sel, PBUF[(NSLOT - 1) % 2][1]], ndma=2)
                    P.op("pool", lambda q, h=h: q.dma_start(
                        out=ctxsp[h, :, :], in_=cstg[:]),
                        reads=[r_cstg], writes_acc=[r_ctxsp], ndma=1)

                front(0)
                for h in range(n_heads):
                    mid_a(h)
                    mid_b(h)
        _phase5()
        P.phase_barrier()

        if debug:
            P.op("syn", lambda q: q.dma_start(
                out=dbg["d_ctx"][:, :, :], in_=ctxsp[:, :, :]),
                reads=[r_ctxsp], ndma=1)
        def _phase6():
            with ExitStack() as ph:
                def sbl(name, shape, dt):
                    t = ph.enter_context(nc.sbuf_tensor(name, list(shape), dt))
                    return t, Res()

                def psl(name, shape, dt):
                    t = ph.enter_context(nc.psum_tensor(name, list(shape), dt))
                    return t, Res()

                acomb, bcomb = [], []
                for g in range(4):
                    acomb.append(sbl("acomb%d" % g, (128, 1024), BF16))
                    bcomb.append(sbl("bcomb%d" % g, (64, E), BF16))
                last_b, r_lastb = sbl("last_b", (128, 1024), BF16)
                fbias_sb, r_fbias = sbl("fbias_sb", (1, E), BF16)
                pcs = [psl("cp%d" % s, (128, 512), F32) for s in range(8)]
                cm1 = ExitStack()
                def sbl1(name, shape, dt):
                    t = cm1.enter_context(nc.sbuf_tensor(name, list(shape), dt))
                    return t, Res()
                gbc_sb, r_gbc = sbl1("cg", (128, 64), F32)
                g1, r_g1 = sbl1("cg1", (1, 64), F32)
                lal, r_lal = sbl1("cla", (128, 1024), F32)
                lbl, r_lbl = sbl1("clb", (32, E // 2), F32)

                P.op("syn", lambda q: q.dma_start(out=g1[:], in_=gateown[:, :]),
                     writes=[r_g1], ndma=1)
                P.op("pe", lambda q: q.matmul(pcs[0][0][:, 0:64], ones1F[0:1, :],
                                              g1[0:1, :], start=True, stop=True),
                     reads=[r_ones1F, r_g1], writes=[pcs[0][1]])
                P.op("dve", lambda q: q.tensor_copy(gbc_sb[:], pcs[0][0][:, 0:64]),
                     reads=[pcs[0][1]], writes=[r_gbc])
                for g in range(4):
                    P.op("dve", lambda q, g=g: q.memset(acomb[g][0][:], 0.0),
                         writes=[acomb[g][1]])
                    P.op("dve", lambda q, g=g: q.memset(bcomb[g][0][0:32, :], 0.0),
                         writes=[bcomb[g][1]])
                for n in range(16):
                    P.op("syn", lambda q, n=n: q.dma_start(out=lal[:], in_=lat[n, :, :]),
                         writes=[r_lal], ndma=1)
                    for g in range(4):
                        P.op("dve", lambda q, g=g, n=n: q.scalar_tensor_tensor(
                            out=acomb[g][0][:], in0=lal[:],
                            scalar=gbc_sb[:, 16 * g + n:16 * g + n + 1],
                            in1=acomb[g][0][:], op0=ALU.mult, op1=ALU.add),
                            reads=[r_lal, r_gbc], writes=[acomb[g][1]])
                    for hh in range(2):
                        P.op("syn", lambda q, n=n, hh=hh: q.dma_start(
                            out=lbl[:], in_=lbt[n, :, 2048 * hh:2048 * hh + 2048]),
                            writes=[r_lbl], ndma=1)
                        for g in range(4):
                            P.op("dve", lambda q, g=g, n=n, hh=hh: q.scalar_tensor_tensor(
                                out=bcomb[g][0][0:32, 2048 * hh:2048 * hh + 2048],
                                in0=lbl[:],
                                scalar=gbc_sb[0:32, 16 * g + n:16 * g + n + 1],
                                in1=bcomb[g][0][0:32, 2048 * hh:2048 * hh + 2048],
                                op0=ALU.mult, op1=ALU.add),
                                reads=[r_lbl, r_gbc], writes=[bcomb[g][1]])
                P.op("syn", lambda q: q.dma_start(out=lal[:], in_=last[:, :]),
                     writes=[r_lal], ndma=1)
                P.op("act", lambda q: q.copy(out=last_b[:], in_=lal[:]),
                     reads=[r_lal], writes=[r_lastb])
                for hh in range(2):
                    P.op("syn", lambda q, hh=hh: q.dma_start(
                        out=lbl[:], in_=lbst[:, 2048 * hh:2048 * hh + 2048]),
                        writes=[r_lbl], ndma=1)
                    for g in range(4):
                        P.op("act", lambda q, g=g, hh=hh: q.copy(
                            out=bcomb[g][0][32:64, 2048 * hh:2048 * hh + 2048],
                            in_=lbl[:]),
                            reads=[r_lbl], writes_acc=[bcomb[g][1]])
                for hh in range(2):
                    P.op("syn", lambda q, hh=hh: q.dma_start(
                        out=lbl[0:1, :], in_=fbias[:, 2048 * hh:2048 * hh + 2048]),
                        writes=[r_lbl], ndma=1)
                    P.op("act", lambda q, hh=hh: q.copy(
                        out=fbias_sb[0:1, 2048 * hh:2048 * hh + 2048],
                        in_=lbl[0:1, :]),
                        reads=[r_lbl], writes_acc=[r_fbias])
                cm1.close()

                ctx_sb, r_ctx = sbl("cctx", (128, 32 * 1024), BF16)
                xno_sb, r_xno2 = sbl("cxno", (128, 32 * 1024), BF16)
                axt_sb, r_axt = sbl("caxt", (64, 8 * 128), BF16)
                wob = [sbl("cwob%d" % i, (128, 512), BF16) for i in range(2)]
                knb = [sbl("cknb%d" % i, (128, 512), BF16) for i in range(2)]
                xog = [sbl("cxo%d" % i, (128, 512), F32) for i in range(2)]
                ost = [sbl("cos%d" % i, (128, 512), F32) for i in range(2)]

                P.op("syn", lambda q: [
                    q.dma_start(out=ctx_sb[:, 1024 * hh:1024 * hh + 1024],
                                in_=ctxsp[hh, :, :]) for hh in range(H)],
                    reads=[r_ctxsp], writes=[r_ctx], ndma=H)
                P.op("syn", lambda q: [
                    q.dma_start(out=xno_sb[:, 1024 * ec:1024 * ec + 1024],
                                in_=xosp[ec, :, :]) for ec in range(32)],
                    reads=[r_xosp], writes=[r_xno2], ndma=32)

                for s in range(NSLOT):
                    g = s // 2
                    pc, r_pc = pcs[0]
                    def axfn(q, s=s, g=g, pc=pc):
                        ins = []
                        for ic in range(32):
                            ins.append(q.matmul(
                                pc[0:32, 0:128], acomb[g][0][:, 32 * ic:32 * ic + 32],
                                xno_sb[:, 1024 * ic + 128 * s:1024 * ic + 128 * s + 128],
                                start=(ic == 0), stop=(ic == 31)))
                            ins.append(q.matmul(
                                pc[32:64, 0:128], last_b[:, 32 * ic:32 * ic + 32],
                                xno_sb[:, 1024 * ic + 128 * s:1024 * ic + 128 * s + 128],
                                start=(ic == 0), stop=(ic == 31)))
                        return ins
                    P.op("pe", axfn, reads=[acomb[g][1], r_lastb, r_xno2],
                         writes=[r_pc])
                    P.op("dve", lambda q, s=s, pc=pc: q.tensor_copy(
                        axt_sb[:, 128 * s:128 * s + 128], pc[0:64, 0:128]),
                        reads=[r_pc], writes_acc=[r_axt])
                if debug:
                    P.op("syn", lambda q: [
                        q.dma_start(out=dbg["d_acomb"][g, :, :], in_=acomb[g][0][:])
                        for g in range(4)] + [
                        q.dma_start(out=dbg["d_bcomb"][g, :, :], in_=bcomb[g][0][:])
                        for g in range(4)] + [
                        q.dma_start(out=dbg["d_axt"][:, :], in_=axt_sb[:]),
                        q.dma_start(out=dbg["d_lastb"][:, :], in_=last_b[:])],
                        reads=[acomb[g][1] for g in range(4)] +
                              [bcomb[g][1] for g in range(4)] +
                              [r_axt, r_lastb], ndma=10)

                for ec in range(8):
                    for hc in range(32):
                        bb, r_bb = wob[hc % 2]
                        P.op("pool", lambda q, ec=ec, hc=hc, bb=bb: q.dma_start(
                            out=bb[:], in_=wot[128 * hc:128 * hc + 128,
                                               512 * ec:512 * ec + 512]),
                            writes=[r_bb], ndma=1)
                        P.op("pe", lambda q, hc=hc, bb=bb: [
                            q.matmul(pcs[s][0][:, 0:512],
                                     ctx_sb[:, 1024 * hc + 128 * s:1024 * hc + 128 * s + 128],
                                     bb[:], start=(hc == 0), stop=False)
                            for s in range(8)],
                            reads=[r_bb, r_ctx], writes=[pcs[s][1] for s in range(8)])
                    for ic in range(32):
                        bb, r_bb = knb[ic % 2]
                        P.op("pool", lambda q, ec=ec, ic=ic, bb=bb: q.dma_start(
                            out=bb[:], in_=fker[128 * ic:128 * ic + 128,
                                                512 * ec:512 * ec + 512]),
                            writes=[r_bb], ndma=1)
                        P.op("pe", lambda q, ic=ic, bb=bb: [
                            q.matmul(pcs[s][0][:, 0:512],
                                     xno_sb[:, 1024 * ic + 128 * s:1024 * ic + 128 * s + 128],
                                     bb[:], start=False, stop=False)
                            for s in range(8)],
                            reads=[r_bb, r_xno2], writes=[pcs[s][1] for s in range(8)])
                    P.op("pe", lambda q, ec=ec: [
                        q.matmul(pcs[s][0][:, 0:512], axt_sb[:, 128 * s:128 * s + 128],
                                 bcomb[s // 2][0][:, 512 * ec:512 * ec + 512],
                                 start=False, stop=False) for s in range(8)],
                        reads=[r_axt] + [bcomb[g][1] for g in range(4)],
                        writes=[pcs[s][1] for s in range(8)])
                    P.op("pe", lambda q, ec=ec: [
                        q.matmul(pcs[s][0][:, 0:512], ones1b[0:1, :],
                                 fbias_sb[0:1, 512 * ec:512 * ec + 512],
                                 start=False, stop=True) for s in range(8)],
                        reads=[r_ones1b, r_fbias],
                        writes=[pcs[s][1] for s in range(8)])
                    for s in range(8):
                        bx, r_bx = xog[s % 2]
                        bo, r_bo = ost[s % 2]
                        P.op("syn", lambda q, s=s, ec=ec, bx=bx: q.dma_start(
                            out=bx[:], in_=xown[128 * s:128 * s + 128,
                                                512 * ec:512 * ec + 512]),
                            writes=[r_bx], ndma=1)
                        P.op("dve", lambda q, s=s, bx=bx, bo=bo: q.tensor_tensor(
                            out=bo[:], in0=pcs[s][0][:, 0:512], in1=bx[:],
                            op=ALU.add), reads=[pcs[s][1], r_bx], writes=[r_bo])
                        P.op("syn", lambda q, s=s, ec=ec, bo=bo: q.dma_start(
                            out=out[128 * s:128 * s + 128, 512 * ec:512 * ec + 512],
                            in_=bo[:]), reads=[r_bo], writes_acc=[r_out], ndma=1)

                P.op("syn", lambda q: q.nop(), reads=[r_out])

        _phase6()
        block = ctx.enter_context(nc.Block())
        P.emit(block)

    return nc


_NC_CACHE = None


def _get_program():
    global _NC_CACHE
    if _NC_CACHE is None:
        _NC_CACHE = build_program()
    return _NC_CACHE


def _host_tables(kcore):
    inv_freq = 1.0 / (10000.0 ** (np.arange(0, D, 2, dtype=np.float32) / D))
    t = np.arange(S, dtype=np.float32)
    freqs = np.outer(t, inv_freq)
    emb = np.concatenate([freqs, freqs], -1)
    cos = np.cos(emb).astype(np.float32)
    sin = np.sin(emb).astype(np.float32)
    cosT = np.ascontiguousarray(cos.T)
    sinT = sin.T
    sinF = sinT.copy()
    sinF[0:64] = -sinF[0:64]
    sinF = np.ascontiguousarray(sinF)
    rows = np.concatenate([np.arange(128 * (4 * s + kcore),
                                     128 * (4 * s + kcore) + 128)
                           for s in range(NSLOT)])
    cosq = np.ascontiguousarray(cosT[:, rows])
    sinq = np.ascontiguousarray(sinF[:, rows])
    pp, jj = np.meshgrid(np.arange(128), np.arange(512), indexing="ij")
    mk = np.where(jj <= 128 * kcore + pp, 0.0, -10000.0).astype(np.float32)
    import ml_dtypes
    return (cosT, sinF, cosq, sinq, rows,
            np.eye(128, dtype=np.float32).astype(ml_dtypes.bfloat16),
            mk.astype(ml_dtypes.bfloat16))


def kernel(**inputs):
    x = np.asarray(inputs["L0_attn_hidden_states"], np.float32)
    Wq = np.ascontiguousarray(np.asarray(inputs["L0_attn_Wq"], np.float32).T)
    Wk = np.ascontiguousarray(np.asarray(inputs["L0_attn_Wk"], np.float32).T)
    Wv = np.ascontiguousarray(np.asarray(inputs["L0_attn_Wv"], np.float32).T)
    Wo = np.ascontiguousarray(np.asarray(inputs["L0_attn_Wo"], np.float32).T)
    fker = np.ascontiguousarray(np.asarray(inputs["L0_ffn_kernel"], np.float32))
    fbias = np.asarray(inputs["L0_ffn_bias"], np.float32).reshape(1, E)
    hp1 = np.ascontiguousarray(np.asarray(inputs["L0_attn_hash_proj1"], np.float32))
    hp2 = np.ascontiguousarray(np.asarray(inputs["L0_attn_hash_proj2"], np.float32))
    hb1 = np.ascontiguousarray(
        np.asarray(inputs["L0_attn_hash_bias1"], np.float32).transpose(0, 2, 1))
    hb2 = np.ascontiguousarray(
        np.asarray(inputs["L0_attn_hash_bias2"], np.float32).transpose(0, 2, 1))
    gate = np.asarray(inputs["L0_ffn_gate"], np.float32)
    lA = np.asarray(inputs["L0_ffn_lora_A"], np.float32)
    lB = np.asarray(inputs["L0_ffn_lora_B"], np.float32)
    lAs = np.asarray(inputs["L0_ffn_lora_A_shared"], np.float32)
    lBs = np.asarray(inputs["L0_ffn_lora_B_shared"], np.float32)

    lat = np.ascontiguousarray(
        lA.transpose(0, 2, 1).reshape(16, 32, 128, 32).transpose(0, 2, 1, 3)
        .reshape(16, 128, 1024))
    last = np.ascontiguousarray(
        lAs.T.reshape(32, 128, 32).transpose(1, 0, 2).reshape(128, 1024))
    lbt = np.ascontiguousarray(lB.transpose(0, 2, 1))
    lbst = np.ascontiguousarray(lBs.T)

    cc = np.broadcast_to(
        np.array([1e9] + [1640.0 - SPL[s] for s in range(1, NSLOT)], np.float32),
        (128, NSLOT)).copy()
    cr = np.broadcast_to(
        np.array([0.0] + [SPL[s] / 2.0 for s in range(1, NSLOT)], np.float32),
        (128, NSLOT)).copy()

    nc = _get_program()
    in_maps = []
    rows_all = []
    for c in range(8):
        b, k = c // 4, c % 4
        cosT, sinF, cosq, sinq, rows, identb, maskc = _host_tables(k)
        rows_all.append(rows)
        xb = np.ascontiguousarray(x[b])
        in_maps.append({
            "xfull": xb,
            "xown": np.ascontiguousarray(xb[rows]),
            "wqt": Wq, "wkt": Wk, "wvt": Wv, "wot": Wo,
            "fker": fker, "fbias": fbias,
            "hp1": hp1, "hb1": hb1, "hp2": hp2, "hb2": hb2,
            "lat": lat, "last": last, "lbt": lbt, "lbst": lbst,
            "gateown": np.ascontiguousarray(
                gate[4 * b:4 * b + 4].reshape(1, 64)),
            "coskf": np.ascontiguousarray(cosT),
            "sinkf": np.ascontiguousarray(sinF),
            "cosqf": cosq, "sinqf": sinq,
            "identb": identb,
            "identf": np.eye(128, dtype=np.float32), "maskc": maskc,
            "cc1640": cc, "cr820": cr,
        })
    res = run_bass_kernel_spmd(nc, in_maps, core_ids=list(range(8)))
    outf = np.empty((B, S, E), np.float32)
    for c in range(8):
        outf[c // 4][rows_all[c]] = res.results[c]["out"]
    return outf
